# version 23
# baseline (speedup 1.0000x reference)
import numpy as np
from contextlib import ExitStack

import concourse.bass as bass
import concourse.mybir as mybir
from concourse.bass_utils import run_bass_kernel_spmd

F32 = mybir.dt.float32
BF16 = mybir.dt.bfloat16
I32 = mybir.dt.int32
AF = mybir.ActivationFunctionType
ALU = mybir.AluOpType
AX = mybir.AxisListType

D = 2048
H = 8192
RK = 16
TT = 128
TBT = 4
TB = TT * TBT
NBLK = 8
TC = NBLK * TB
ND = D // TT
NH = H // TT
NEC = D // 512
NHC = H // 512
EPS = 1e-5
WRING = 8
TKR = 4
NCORES = 8

ENG_SEM = {"t": "st", "v": "sv", "s": "ss"}


class Prog:

    ENGS = ("t", "v", "s", "g", "y")

    def __init__(self):
        self.q = {e: [] for e in self.ENGS}
        semnames = (["st", "sv", "ss", "sg"] + ["sy%d" % i for i in range(8)]
                    + ["sgd%d" % i for i in range(6)])
        self.semnames = semnames
        self.cnt = {s: 0 for s in semnames}
        self.floor = {e: {} for e in self.ENGS}
        self.lw = {}
        self.lr = {}
        self.rr = {"y": 0, "g": 0}

    def emit(self, eng, fn, r=(), w=(), dma=False):
        if dma:
            if eng == "y":
                self.rr["y"] += 1
                sem = "sy%d" % (self.rr["y"] % 8)
            elif eng == "g":
                self.rr["g"] += 1
                sem = "sgd%d" % (self.rr["g"] % 6)
            else:
                raise ValueError(eng)
            inc = 16
        elif eng == "g":
            sem, inc = "sg", 1
        else:
            sem, inc = ENG_SEM[eng], 1
        waits = {}

        def need(s, c):
            if c > waits.get(s, 0):
                waits[s] = c

        if dma and self.cnt[sem] > 0:
            need(sem, self.cnt[sem])
        for k in r:
            x = self.lw.get(k)
            if x:
                need(*x)
        for k in w:
            x = self.lw.get(k)
            if x:
                need(*x)
            for s2, c2 in self.lr.get(k, {}).items():
                need(s2, c2)
        fl = self.floor[eng]
        waits = {s2: c2 for s2, c2 in waits.items() if c2 > fl.get(s2, 0)}
        for s2, c2 in waits.items():
            fl[s2] = c2
        self.cnt[sem] += inc
        cafter = self.cnt[sem]
        self.q[eng].append((waits, fn, sem, inc))
        for k in w:
            self.lw[k] = (sem, cafter)
            self.lr[k] = {}
        for k in r:
            self.lr.setdefault(k, {})[sem] = cafter
        return sem, cafter

    def wait_all(self, eng, sems):
        self.q[eng].append((dict(sems), None, None, 0))


def build_program(nblocks=NBLK, debug=False):
    nc = bass.Bass("TRN2", target_bir_lowering=False)
    p = Prog()

    xin = nc.dram_tensor("x", [TC, D], F32, kind="ExternalInput").ap()
    xprev = nc.dram_tensor("xprev", [1, D], F32, kind="ExternalInput").ap()
    wk_in = nc.dram_tensor("Wk", [D, D], F32, kind="ExternalInput").ap()
    wv_in = nc.dram_tensor("Wv", [D, D], F32, kind="ExternalInput").ap()
    wr_in = nc.dram_tensor("Wr", [D, D], F32, kind="ExternalInput").ap()
    wo_in = nc.dram_tensor("Wo", [D, D], F32, kind="ExternalInput").ap()
    td_in = nc.dram_tensor("td", [D], F32, kind="ExternalInput").ap()
    tf_in = nc.dram_tensor("tf", [D], F32, kind="ExternalInput").ap()
    tmk_in = nc.dram_tensor("tmk", [D], F32, kind="ExternalInput").ap()
    tmv_in = nc.dram_tensor("tmv", [D], F32, kind="ExternalInput").ap()
    tmr_in = nc.dram_tensor("tmr", [D], F32, kind="ExternalInput").ap()
    wupq_in = nc.dram_tensor("wupq", [H, D], I32, kind="ExternalInput").ap()
    wups_in = nc.dram_tensor("wups", [H, D // 64], F32, kind="ExternalInput").ap()
    bup_in = nc.dram_tensor("bup", [H], F32, kind="ExternalInput").ap()
    upla_in = nc.dram_tensor("upla", [D, RK], F32, kind="ExternalInput").ap()
    uplb_in = nc.dram_tensor("uplb", [RK, H], F32, kind="ExternalInput").ap()
    wdnq_in = nc.dram_tensor("wdnq", [D, H], I32, kind="ExternalInput").ap()
    wdns_in = nc.dram_tensor("wdns", [D, H // 64], F32, kind="ExternalInput").ap()
    bdn_in = nc.dram_tensor("bdn", [D], F32, kind="ExternalInput").ap()
    dnla_in = nc.dram_tensor("dnla", [H, RK], F32, kind="ExternalInput").ap()
    dnlb_in = nc.dram_tensor("dnlb", [RK, D], F32, kind="ExternalInput").ap()
    cid_in = nc.dram_tensor("c_id", [TT, TT], F32, kind="ExternalInput").ap()
    cones_in = nc.dram_tensor("c_ones", [1, 512], F32, kind="ExternalInput").ap()
    xwarm_in = nc.dram_tensor("xwarm", [TT, D], F32, kind="ExternalInput").ap()
    wvalid_in = nc.dram_tensor("wvalid", [TT, 1], F32, kind="ExternalInput").ap()
    out = nc.dram_tensor("out", [TC, D], F32, kind="ExternalOutput").ap()

    wkt = nc.dram_tensor("wkt", [ND, D, TT], BF16, kind="Internal").ap()
    wvt = nc.dram_tensor("wvt", [ND, D, TT], BF16, kind="Internal").ap()
    wrt = nc.dram_tensor("wrt", [ND, D, TT], BF16, kind="Internal").ap()
    wot = nc.dram_tensor("wot", [D, D], BF16, kind="Internal").ap()
    wut = nc.dram_tensor("wut", [NH, D, TT], BF16, kind="Internal").ap()
    wdt = nc.dram_tensor("wdt", [H, D], BF16, kind="Internal").ap()
    bupxd = nc.dram_tensor("bupxd", [RK + 1, H], BF16, kind="Internal").ap()
    bdnxd = nc.dram_tensor("bdnxd", [RK + 1, D], BF16, kind="Internal").ap()
    onesd = nc.dram_tensor("onesd", [1, 512], BF16, kind="Internal").ap()

    alloc = []
    cur = [0]

    def sb(name, shape, dtype):
        nbytes = int(np.prod(shape[1:])) * (2 if dtype in (BF16,) else 4)
        cur[0] += (nbytes + 31) // 32 * 32
        t = nc.alloc_sbuf_tensor(name, shape, dtype)
        alloc.append((name, nbytes))
        return t.ap(), 0

    X, x_off = sb("Xar", [TT, TBT * D], F32)
    OA, o_off = sb("Oar", [TT, 16 * 512], F32)
    VR, vr_off = sb("VRar", [TT, 32 * 512], BF16)
    XN, _ = sb("XNar", [TT, ND * 513], BF16)
    MX, _ = sb("MXar", [TT, ND * 512], BF16)
    XT, _ = sb("xn_tok", [TT, D], F32)
    SCAN_BUFS = ["sc_p", "sc_q", "sc_pv", "sc_sa", "sc_sb", "sc_wkv", "sc_num", "sc_pad"]
    SCANT, _ = sb("scant", [TT, 8 * 512], F32)
    SC = {nm: SCANT[:, i * 512:(i + 1) * 512] for i, nm in enumerate(SCAN_BUFS)}
    SCBF = SCANT.bitcast(BF16)
    DIFF, _ = sb("diff", [TT, 512], BF16)
    TK = [sb("tok%d" % i, [TT, 512], F32)[0] for i in range(TKR)]
    WH = [sb("wh%d" % i, [TT, 4 * 512], BF16)[0] for i in range(8)]
    BUPS = [sb("bupx%d" % i, [RK + 1, 512], BF16)[0] for i in range(2)]
    BDNX, _ = sb("bdnx", [RK + 1, D], BF16)
    BUPC, _ = sb("bupc", [TT, NH], F32)
    AUPB, _ = sb("aupb", [TT, ND * RK], BF16)
    ADNB, _ = sb("adnb", [TT, NH * RK], BF16)
    L1E, _ = sb("l1e", [RK + 1, 512], BF16)
    L2E, _ = sb("l2e", [RK + 1, 512], BF16)
    IDF, _ = sb("identf", [TT, TT], F32)
    LAM, _ = sb("lamcm", [TT, ND], F32)
    UCM, _ = sb("ucm", [TT, ND], F32)
    TMK, _ = sb("tmkcm", [TT, ND], F32)
    TMV, _ = sb("tmvcm", [TT, ND], F32)
    TMR, _ = sb("tmrcm", [TT, ND], F32)
    BCOL, _ = sb("bcol", [TT, ND], BF16)
    CA, _ = sb("carrya", [TT, ND], F32)
    CB, _ = sb("carryb", [TT, ND], F32)
    NEG1, _ = sb("neg1", [TT, 1], F32)
    NEGH, _ = sb("negh", [TT, 1], F32)
    VALID, _ = sb("valid", [TT, 1], F32)
    SSUM, _ = sb("ssum0", [TT, 8], F32)
    SSQ, _ = sb("ssq0", [TT, 8], F32)
    SMEAN, _ = sb("smean0", [TT, 8], F32)
    SMSQ, _ = sb("smsq0", [TT, 8], F32)
    SVAR, _ = sb("svar0", [TT, 8], F32)
    SSTD, _ = sb("sstd0", [TT, 8], F32)
    SRSTD, _ = sb("srstd0", [TT, 8], F32)
    PSC = [sb("pscale%d" % i, [TT, 32], F32)[0] for i in range(2)]
    ONEB, _ = sb("onebf", [1, 512], BF16)
    PIN_I = [X[:, 0:D].bitcast(I32), X[:, D:2 * D].bitcast(I32)]
    PIN_F = [X[:, 0:D], X[:, D:2 * D]]
    PKEY_IN = [["X0", "X1"], ["X2", "X3"]]
    SEXP = [X[:, 2 * D:3 * D], X[:, 3 * D:4 * D]]
    PKEY_SEXP = [["X4", "X5"], ["X6", "X7"]]
    DQF = [OA[:, 0:D], OA[:, D:2 * D]]
    PKEY_DQF = [["O0", "O1", "O2", "O3"], ["O4", "O5", "O6", "O7"]]
    TBUF = [OA[:, 2 * D:2 * D + 1024].bitcast(BF16), OA[:, 2 * D + 1024:3 * D].bitcast(BF16)]
    PKEY_TB = [["O8", "O9"], ["O10", "O11"]]

    def xkeys(t):
        return ["X%d" % (2 * t), "X%d" % (2 * t + 1)]

    def okeys(slot):
        return "O%d" % slot

    def xnt(s):
        return XN[:, s * 513 + 1:s * 513 + 513]

    def xnt_ext(s):
        return XN[:, s * 513:s * 513 + 512]

    def x2tile(j):
        if j < 32:
            return VR[:, j * 512:(j + 1) * 512], ["VR%d" % j]
        if j < 48:
            m = j - 32
            return MX[:, m * 512:(m + 1) * 512], ["MX%d" % m]
        m = j - 48
        smap = ["sc_p", "sc_p", "sc_q", "sc_q", "sc_pv", "sc_pv", "sc_sa",
                "sc_sa", "sc_sb", "sc_sb", "sc_wkv", "sc_wkv", "sc_num",
                "sc_num", "sc_pad", "sc_pad"]
        return SCBF[:, m * 512:(m + 1) * 512], [smap[m]]

    psum_ctx = ExitStack()
    PB2 = [psum_ctx.enter_context(nc.psum_tensor("pbw%d" % i, [TT, 1024], F32)).ap()
           for i in range(4)]
    PB = [PB2[i // 2][:, (i % 2) * 512:(i % 2 + 1) * 512] for i in range(8)]
    LP = PB[7][0:RK, :]

    tbank = [0]

    def nexttr():
        b = tbank[0] % 8
        tbank[0] += 1
        return PB[b], "PB%d" % b

    def ln_stats(src_ap, src_keys, junk_ap, junk_keys, c=0):
        ks = "%d" % c
        p.emit("s", lambda e, a=src_ap, j=junk_ap, cc=c: e.activation(
            out=j, in_=a, func=AF.Identity, accum_out=SSUM[:, cc:cc + 1]),
            r=src_keys, w=list(junk_keys) + ["ssum" + ks])
        p.emit("s", lambda e, a=src_ap, j=junk_ap, cc=c: e.activation(
            out=j, in_=a, func=AF.Square, accum_out=SSQ[:, cc:cc + 1]),
            r=src_keys, w=list(junk_keys) + ["ssq" + ks])
        p.emit("v", lambda e, cc=c: e.tensor_scalar_mul(
            SMEAN[:, cc:cc + 1], SSUM[:, cc:cc + 1], 1.0 / D),
            r=["ssum" + ks], w=["smean" + ks])
        p.emit("v", lambda e, cc=c: e.scalar_tensor_tensor(
            SMSQ[:, cc:cc + 1], SSUM[:, cc:cc + 1], 1.0 / (D * D), SSUM[:, cc:cc + 1],
            ALU.mult, ALU.mult), r=["ssum" + ks], w=["smsq" + ks])
        p.emit("v", lambda e, cc=c: e.tensor_scalar(
            out=SVAR[:, cc:cc + 1], in0=SSQ[:, cc:cc + 1], scalar1=1.0 / D, scalar2=EPS,
            op0=ALU.mult, op1=ALU.add), r=["ssq" + ks], w=["svar" + ks])
        p.emit("v", lambda e, cc=c: e.tensor_sub(
            SVAR[:, cc:cc + 1], SVAR[:, cc:cc + 1], SMSQ[:, cc:cc + 1]),
            r=["smsq" + ks], w=["svar" + ks])
        p.emit("s", lambda e, cc=c: e.activation(
            out=SSTD[:, cc:cc + 1], in_=SVAR[:, cc:cc + 1], func=AF.Ln),
            r=["svar" + ks], w=["sstd" + ks])
        p.emit("s", lambda e, cc=c: e.activation(
            out=SRSTD[:, cc:cc + 1], in_=SSTD[:, cc:cc + 1],
            func=AF.Exp, scale=NEGH[:, 0:1]),
            r=["sstd" + ks, "negh"], w=["srstd" + ks])

    def transpose_group(src_ap, src_keys, n, dst_ap, dst_keys, dst_eng="v"):
        trap, trkey = nexttr()
        def tfn(e, a=src_ap, nn=n, bk=trap):
            last = None
            for j in range(nn):
                last = e.transpose(bk[:, j * TT:(j + 1) * TT],
                                   a[:, j * TT:(j + 1) * TT], IDF)
            return last
        p.emit("t", tfn, r=list(src_keys) + ["identf"], w=[trkey])
        p.emit(dst_eng, lambda e, bk=trap, d=dst_ap, nn=n: e.tensor_copy(
            d, bk[:, 0:nn * TT].rearrange("p (j c) -> p j c", j=nn)),
            r=[trkey], w=dst_keys)

    p.emit("y", lambda e: e.dma_start(out=IDF, in_=cid_in), w=["identf"], dma=True)
    p.emit("v", lambda e: e.memset(NEG1[:, 0:1], -1.0), w=["neg1"])
    p.emit("v", lambda e: e.memset(NEGH[:, 0:1], -0.5), w=["negh"])
    p.emit("v", lambda e: e.memset(CA, 0.0), w=["carrya"])
    p.emit("v", lambda e: e.memset(CB, 0.0), w=["carryb"])

    p.emit("y", lambda e: e.dma_start(out=TK[0][:1, 0:512], in_=cones_in),
           w=["TK0"], dma=True)
    p.emit("s", lambda e: e.copy(out=ONEB, in_=TK[0][:1, 0:512]),
           r=["TK0"], w=["onebf"])
    p.emit("g", lambda e: e.dma_start(out=onesd, in_=ONEB), r=["onebf"],
           w=["onesd"], dma=True)
    p.emit("y", lambda e: e.dma_start(out=L1E[RK:RK + 1, :], in_=onesd),
           r=["onesd"], w=["l1ones"], dma=True)
    p.emit("y", lambda e: e.dma_start(out=L2E[RK:RK + 1, :], in_=onesd),
           r=["onesd"], w=["l2ones"], dma=True)

    def cm_load(src, dst_ap, dst_key, post=None):
        tmp = TK[1][0:ND, 0:TT]
        p.emit("y", lambda e, s=src: e.dma_start(
            out=tmp, in_=s.rearrange("(s q) -> s q", q=TT)), w=["TK1"], dma=True)
        trap, trkey = nexttr()
        p.emit("t", lambda e, bk=trap: e.transpose(
            bk[:, 0:ND], tmp, IDF[0:ND, 0:ND]),
            r=["TK1", "identf"], w=[trkey])
        p.emit("v", lambda e, bk=trap, d=dst_ap: e.tensor_copy(d, bk[:, 0:ND]),
               r=[trkey], w=[dst_key])
        if post:
            post(dst_ap, dst_key)

    cm_load(td_in, LAM, "lamcm")
    p.emit("s", lambda e: e.activation(out=LAM, in_=LAM, func=AF.Exp),
           r=["lamcm"], w=["lamcm"])
    p.emit("s", lambda e: e.activation(out=LAM, in_=LAM, func=AF.Exp,
                                       scale=NEG1[:, 0:1]),
           r=["lamcm", "neg1"], w=["lamcm"])
    cm_load(tf_in, UCM, "ucm")
    p.emit("s", lambda e: e.activation(out=UCM, in_=UCM, func=AF.Exp),
           r=["ucm"], w=["ucm"])
    cm_load(tmk_in, TMK, "tmkcm")
    cm_load(tmv_in, TMV, "tmvcm")
    cm_load(tmr_in, TMR, "tmrcm")

    def lora_a(src, n, dst_ap, dst_key):
        step = 512 // RK
        for c0 in range(0, n, step):
            nn = min(step, n - c0)
            tmp = TK[2][:, 0:nn * RK]
            p.emit("y", lambda e, s=src, c=c0, nn=nn, t=tmp: e.dma_start(
                out=t.rearrange("p (s r) -> p s r", s=nn),
                in_=s.rearrange("(s p) r -> p s r", p=TT)[:, c:c + nn, :]),
                w=["TK2"], dma=True)
            p.emit("s", lambda e, d=dst_ap, c=c0, nn=nn, t=tmp: e.copy(
                out=d[:, c * RK:(c + nn) * RK], in_=t),
                r=["TK2"], w=[dst_key])

    lora_a(upla_in, ND, AUPB, "aupb")
    lora_a(dnla_in, NH, ADNB, "adnb")

    def lora_b(srcB, srcb, n, dram):
        for c in range(n // 512):
            tmpf = TK[2][0:RK, 0:512]
            tmpb = TK[3][0:RK, 0:256].bitcast(BF16)
            tmpb1 = TK[3][0:1, 256:512].bitcast(BF16)
            p.emit("y", lambda e, s=srcB, cc=c: e.dma_start(
                out=tmpf, in_=s[:, cc * 512:(cc + 1) * 512]), w=["TK2"], dma=True)
            p.emit("s", lambda e: e.copy(out=tmpb, in_=tmpf),
                   r=["TK2"], w=["TK3"])
            p.emit("y", lambda e, s=srcb, cc=c: e.dma_start(
                out=tmpf[0:1, :], in_=s[cc * 512:(cc + 1) * 512].rearrange("(a q) -> a q", a=1)),
                r=["TK3"], w=["TK2"], dma=True)
            p.emit("s", lambda e: e.copy(out=tmpb1, in_=tmpf[0:1, :]),
                   r=["TK2"], w=["TK3b"])
            p.emit("g", lambda e, dd=dram, cc=c: e.dma_start(
                out=dd[0:RK, cc * 512:(cc + 1) * 512], in_=tmpb),
                r=["TK3"], w=["lorabd"], dma=True)
            p.emit("g", lambda e, dd=dram, cc=c: e.dma_start(
                out=dd[RK:RK + 1, cc * 512:(cc + 1) * 512], in_=tmpb1),
                r=["TK3b"], w=["lorabd"], dma=True)

    lora_b(uplb_in, bup_in, H, bupxd)
    lora_b(dnlb_in, bdn_in, D, bdnxd)
    p.emit("y", lambda e: e.dma_start(out=BDNX, in_=bdnxd), r=["lorabd"],
           w=["bdnx"], dma=True)

    p.emit("v", lambda e: e.memset(X[:, 0:D], 0.0), w=xkeys(0))
    p.emit("y", lambda e: e.dma_start(out=X[0:1, 0:D], in_=xprev),
           r=xkeys(0), w=xkeys(0), dma=True)
    xp = X[0:1, 0:D]
    p.emit("v", lambda e: e.reduce_sum(SSUM[0:1, 0:1], xp, axis=AX.X),
           r=xkeys(0), w=["ssum0"])
    p.emit("s", lambda e: e.activation(out=XT[0:1, :], in_=xp, func=AF.Square,
                                       accum_out=SSQ[0:1, 0:1]),
           r=xkeys(0), w=["xn_tok", "ssq0"])
    p.emit("v", lambda e: e.tensor_scalar_mul(SMEAN[0:1, 0:1], SSUM[0:1, 0:1], 1.0 / D),
           r=["ssum0"], w=["smean0"])
    p.emit("v", lambda e: e.scalar_tensor_tensor(
        SMSQ[0:1, 0:1], SSUM[0:1, 0:1], 1.0 / (D * D), SSUM[0:1, 0:1],
        ALU.mult, ALU.mult), r=["ssum0"], w=["smsq0"])
    p.emit("v", lambda e: e.tensor_scalar(
        out=SVAR[0:1, 0:1], in0=SSQ[0:1, 0:1], scalar1=1.0 / D, scalar2=EPS,
        op0=ALU.mult, op1=ALU.add), r=["ssq0"], w=["svar0"])
    p.emit("v", lambda e: e.tensor_sub(SVAR[0:1, 0:1], SVAR[0:1, 0:1], SMSQ[0:1, 0:1]),
           r=["smsq0"], w=["svar0"])
    p.emit("s", lambda e: e.activation(out=SSTD[0:1, 0:1], in_=SVAR[0:1, 0:1],
                                       func=AF.Ln), r=["svar0"], w=["sstd0"])
    p.emit("s", lambda e: e.activation(out=SRSTD[0:1, 0:1], in_=SSTD[0:1, 0:1],
                                       func=AF.Exp, scale=NEGH[0:1, 0:1]),
           r=["sstd0", "negh"], w=["srstd0"])
    p.emit("v", lambda e: e.tensor_scalar(
        out=xp, in0=xp, scalar1=SMEAN[0:1, 0:1], scalar2=SRSTD[0:1, 0:1],
        op0=ALU.subtract, op1=ALU.mult),
        r=xkeys(0) + ["smean0", "srstd0"], w=xkeys(0))
    for g in range(4):
        trap, trkey = nexttr()

        def tfn(e, gg=g, bk=trap):
            last = None
            for j in range(4):
                s = gg * 4 + j
                last = e.transpose(bk[:, j * TT:(j + 1) * TT],
                                   X[:, s * TT:(s + 1) * TT], IDF)
            return last
        p.emit("t", tfn, r=xkeys(0) + ["identf"], w=[trkey])
        p.emit("v", lambda e, gg=g, bk=trap: e.tensor_copy(
            BCOL[:, gg * 4:gg * 4 + 4],
            bk[:, 0:4 * TT].rearrange("p (j c) -> p j c", j=4)[:, :, 0]),
            r=[trkey], w=["bcol"])


    prep_units = []
    prep_it = [0]

    def prep_one(qsrc, ssrc, rr, cc, dst_rearr, wkey):
        i = prep_it[0]
        prep_it[0] += 1
        slot = i % 2
        ink = PKEY_IN[slot]
        if ssrc is None:
            dq = PIN_F[slot]
            dqk = ink
            p.emit("y", lambda e, s=qsrc, r_=rr, c_=cc, b=dq: e.dma_start(
                out=b, in_=s[r_ * TT:(r_ + 1) * TT, c_ * D:(c_ + 1) * D]),
                w=ink, dma=True)
        else:
            inb = PIN_I[slot]
            sx = SEXP[slot]
            sxk = PKEY_SEXP[slot]
            dq = DQF[slot]
            dqk = PKEY_DQF[slot]
            psc = PSC[slot]
            psck = "pscale%d" % slot
            p.emit("y", lambda e, s=qsrc, r_=rr, c_=cc, b=inb: e.dma_start(
                out=b, in_=s[r_ * TT:(r_ + 1) * TT, c_ * D:(c_ + 1) * D]),
                w=ink, dma=True)
            p.emit("y", lambda e, s=ssrc, r_=rr, c_=cc, b=psc: e.dma_start(
                out=b, in_=s[r_ * TT:(r_ + 1) * TT, c_ * 32:(c_ + 1) * 32]),
                w=[psck], dma=True)
            p.emit("s", lambda e, b=psc, d=sx: e.copy(
                out=d.rearrange("p (a b) -> p a b", a=32),
                in_=b.unsqueeze(2).broadcast_to([TT, 32, 64])),
                r=[psck], w=sxk)
            p.emit("v", lambda e, b=inb, s2=sx, d=dq: e.scalar_tensor_tensor(
                d, b, -7.5, s2, ALU.add, ALU.mult),
                r=ink + sxk, w=dqk)
        tslot = i % 2
        tb = TBUF[tslot]
        tbk = PKEY_TB[tslot]
        for g in range(2):
            wb = PB2[(tbank[0] + g) % 4]
            wkeys = ["PB%d" % (((tbank[0] + g) % 4) * 2),
                     "PB%d" % (((tbank[0] + g) % 4) * 2 + 1)]

            def tfn(e, gg=g, bk=wb, d=dq):
                last = None
                for j in range(8):
                    s = gg * 8 + j
                    last = e.transpose(bk[:, j * TT:(j + 1) * TT],
                                       d[:, s * TT:(s + 1) * TT], IDF)
                return last
            p.emit("t", tfn, r=dqk + ["identf"], w=wkeys)
            if g % 2 == 0:
                p.emit("v", lambda e, gg=g, bk=wb, t=tb: e.tensor_copy(
                    t.rearrange("p (s c) -> p s c", s=ND)[:, gg * 8:gg * 8 + 8, :],
                    bk[:, 0:8 * TT].rearrange("p (j c) -> p j c", j=8)),
                    r=wkeys, w=tbk)
            else:
                p.emit("s", lambda e, gg=g, bk=wb, t=tb: e.copy(
                    out=t.rearrange("p (s c) -> p s c", s=ND)[:, gg * 8:gg * 8 + 8, :],
                    in_=bk[:, 0:8 * TT].rearrange("p (j c) -> p j c", j=8)),
                    r=wkeys, w=tbk)
        tbank[0] += 2
        p.emit("g", lambda e, r_=rr, c_=cc, t=tb: e.dma_start(
            out=dst_rearr(r_, c_), in_=t.rearrange("p (s c) -> p s c", s=ND)),
            r=tbk, w=[wkey], dma=True)

    def prep_quant(qsrc, ssrc, dram, n_row_tiles, col_tiles_per_row, dst_rearr, wkey):
        for rr in range(n_row_tiles):
            for cc in range(col_tiles_per_row):
                prep_units.append((qsrc, ssrc, rr, cc, dst_rearr, wkey))

    for w_in_, w_t_, wkey_ in [(wk_in, wkt, "wbf_k"), (wv_in, wvt, "wbf_v"),
                               (wr_in, wrt, "wbf_r")]:
        def att_dst(rr, cc, wt2=w_t_):
            return wt2[rr].rearrange("(s q) ee -> q s ee", q=TT)

        prep_quant(w_in_, None, w_t_, ND, 1, att_dst, wkey_)
    wro_ = wot.rearrange("(s q) ee -> q s ee", q=TT)

    def att_dst_o(rr, cc):
        return wro_[:, :, rr * TT:(rr + 1) * TT]

    prep_quant(wo_in, None, wot, ND, 1, att_dst_o, "wbf_o")

    def wut_dst(rr, cc):
        return wut[rr].rearrange("(s q) hh -> q s hh", q=TT)

    prep_quant(wupq_in, wups_in, wut, NH, 1, wut_dst, "wutd")
    p.emit("y", lambda e: e.dma_start(
        out=TK[1][0:NH, 0:TT], in_=bup_in.rearrange("(s q) -> s q", q=TT)),
        w=["TK1"], dma=True)
    trapb, trkeyb = nexttr()
    p.emit("t", lambda e, bk=trapb: e.transpose(
        bk[:, 0:NH], TK[1][0:NH, 0:TT], IDF[0:NH, 0:NH]),
        r=["TK1", "identf"], w=[trkeyb])
    p.emit("v", lambda e, bk=trapb: e.tensor_copy(BUPC, bk[:, 0:NH]),
           r=[trkeyb], w=["bupc"])
    wdt_r = wdt.rearrange("(s q) ee -> q s ee", q=TT)

    def wdt_dst(rr, cc):
        return wdt_r[:, cc * ND:(cc + 1) * ND, rr * TT:(rr + 1) * TT]

    prep_quant(wdnq_in, wdns_in, wdt, ND, NEC, wdt_dst, "wdtd")

    attn_u = [u for u in prep_units if u[1] is None]
    quant_u = [u for u in prep_units if u[1] is not None]
    ai, qi2 = 0, 0
    while ai < len(attn_u) or qi2 < len(quant_u):
        if ai < len(attn_u):
            prep_one(*attn_u[ai]); ai += 1
        for _ in range(2):
            if qi2 < len(quant_u):
                prep_one(*quant_u[qi2]); qi2 += 1
    prep_units.clear()




    wring = [0]

    def whalf():
        s = wring[0] % 8
        wring[0] += 1
        return WH[s], "WH%d" % s

    tkr = [0]

    def tkslot():
        s = tkr[0] % TKR
        tkr[0] += 1
        return s

    def proj_tokmajor(lhs_fn, lhs_keys_fn, wsrc3d_fn, n_s, banks,
                      extra_units=(), wkey_src=()):
        nx = len(extra_units)
        nq = n_s // 4
        bufs = {}
        qeng = [0]

        def emit_load(hh):
            buf, bkey = whalf()
            bufs[hh * 4] = (buf, bkey)
            p.emit("y" if hh % 2 == 0 else "g",
                   lambda e, a=buf, s_=hh * 4: e.dma_start(
                out=a.rearrange("p (s c) -> p s c", s=4), in_=wsrc3d_fn(s_)),
                r=list(wkey_src), w=[bkey], dma=True)

        for hh in range(min(3, nq)):
            emit_load(hh)
        for s in range(n_s):
            buf, bkey = bufs[(s // 4) * 4]
            wap = buf[:, (s % 4) * 512:(s % 4 + 1) * 512]
            first, last = (s == 0), (s == n_s - 1 and nx == 0)
            rl = [bkey] + lhs_keys_fn(s)
            wl = []
            if first or last:
                wl = ["PB%d" % b for b in banks]

            def mfn(e, ss=s, a=wap, ff=first, ll=last):
                li = None
                for t in range(TBT):
                    li = e.matmul(PB[banks[t]][:], lhs_fn(ss, t), a,
                                  start=ff, stop=ll)
                return li
            p.emit("t", mfn, r=rl, w=wl)
            if s % 4 == 3 and s // 4 + 3 < nq:
                emit_load(s // 4 + 3)
        for i, (xfn, xreads) in enumerate(extra_units):
            ll = (i == nx - 1)
            wl = ["PB%d" % b for b in banks] if ll else []

            def mfn(e, f=xfn, ll2=ll):
                li = None
                for t in range(TBT):
                    li = f(e, t, False, ll2)
                return li
            p.emit("t", mfn, r=list(xreads), w=wl)

    p.emit("y", lambda e: e.dma_start(out=VALID, in_=wvalid_in), w=["valid"], dma=True)
    p.emit("y", lambda e: e.dma_start(out=X[:, 0:D], in_=xwarm_in),
           r=xkeys(0), w=xkeys(0), dma=True)
    ln_stats(X[:, 0:D], xkeys(0), XT, ["xn_tok"], c=0)
    p.emit("v", lambda e: e.tensor_scalar(
        out=XT, in0=X[:, 0:D], scalar1=SMEAN[:, 0:1], scalar2=SRSTD[:, 0:1],
        op0=ALU.subtract, op1=ALU.mult),
        r=xkeys(0) + ["smean0", "srstd0"], w=["xn_tok"])
    for g in range(4):
        dst = XN.rearrange("p (s c) -> p s c", s=ND, c=513)[:, g * 4:g * 4 + 4, 1:1 + TT]
        transpose_group(XT[:, g * 512:(g + 1) * 512], ["xn_tok"], 4,
                        dst, ["XN%d" % (g * 4 + j) for j in range(4)])

    def warm_proj(wsrc, tmap, tkey, dst_off_fn, wbkey):
        for s in range(ND):
            xs = XN[:, s * 513 + 1:s * 513 + 1 + TT]
            p.emit("v", lambda e, a=xs: e.tensor_sub(
                DIFF[:, 1:TT], a[:, 1:TT], a[:, 0:TT - 1]),
                r=["XN%d" % s], w=["diff"])
            p.emit("v", lambda e, a=xs, ss=s, tm=tmap: e.scalar_tensor_tensor(
                MX[:, ss * 512 + 1:ss * 512 + TT], DIFF[:, 1:TT],
                tm[:, ss:ss + 1], a[:, 0:TT - 1], ALU.mult, ALU.add),
                r=["diff", tkey, "XN%d" % s], w=["MX%d" % s])
            p.emit("v", lambda e, a=xs, ss=s, tm=tmap: e.tensor_scalar_mul(
                MX[:, ss * 512:ss * 512 + 1], a[:, 0:1], tm[:, ss:ss + 1]),
                r=[tkey, "XN%d" % s], w=["MX%d" % s])
        for oc in range(ND):
            buf, bkey2 = whalf()
            p.emit("y", lambda e, a=buf, o2=oc, wb=wsrc: e.dma_start(
                out=a[:, 0:ND * TT].rearrange("p (s c) -> p s c", s=ND),
                in_=wb[o2].rearrange("(s q) ee -> q s ee", q=TT)),
                r=[wbkey], w=[bkey2], dma=True)
            bank = oc % 8

            def mfn(e, a=buf, bk=bank):
                li = None
                for s in range(ND):
                    li = e.matmul(PB[bk][:, 0:TT], a[:, s * TT:(s + 1) * TT],
                                  MX[:, s * 512:s * 512 + TT],
                                  start=(s == 0), stop=(s == ND - 1))
                return li
            p.emit("t", mfn,
                   r=[bkey2] + ["MX%d" % s for s in range(ND)],
                   w=["PB%d" % bank])
            dap, dkeys = dst_off_fn(oc)
            p.emit("v", lambda e, bk=bank, d=dap: e.tensor_copy(d, PB[bk][:, 0:TT]),
                   r=["PB%d" % bank], w=dkeys)

    warm_proj(wkt, TMK, "tmkcm",
              lambda oc: (OA[:, oc * 512:oc * 512 + TT], [okeys(oc)]), "wbf_k")
    warm_proj(wvt, TMV, "tmvcm",
              lambda oc: (VR[:, oc * 512:oc * 512 + TT], ["VR%d" % oc]), "wbf_v")
    for s in range(ND):
        kap = OA[:, s * 512:s * 512 + TT]
        vap = VR[:, s * 512:s * 512 + TT]
        p.emit("s", lambda e, a=kap: e.activation(out=SC["sc_p"][:, 0:TT], in_=a,
                                                  func=AF.Exp),
               r=[okeys(s)], w=["sc_p"])
        p.emit("v", lambda e: e.tensor_scalar_mul(
            SC["sc_p"][:, 0:TT], SC["sc_p"][:, 0:TT], VALID[:, 0:1]),
            r=["sc_p", "valid"], w=["sc_p"])
        p.emit("v", lambda e, a=vap: e.tensor_mul(
            SC["sc_pv"][:, 0:TT], SC["sc_p"][:, 0:TT], a),
            r=["sc_p", "VR%d" % s], w=["sc_pv"])
        p.emit("v", lambda e, ss=s: e.tensor_tensor_scan(
            SC["sc_sa"][:, 0:TT], LAM[:, ss:ss + 1].broadcast_to([TT, TT]),
            SC["sc_pv"][:, 0:TT], 0.0, ALU.mult, ALU.add),
            r=["sc_pv", "lamcm"], w=["sc_sa"])
        p.emit("v", lambda e, ss=s: e.tensor_tensor_scan(
            SC["sc_sb"][:, 0:TT], LAM[:, ss:ss + 1].broadcast_to([TT, TT]),
            SC["sc_p"][:, 0:TT], 0.0, ALU.mult, ALU.add),
            r=["sc_p", "lamcm"], w=["sc_sb"])
        p.emit("v", lambda e, ss=s: e.tensor_copy(
            CA[:, ss:ss + 1], SC["sc_sa"][:, TT - 1:TT]),
            r=["sc_sa"], w=["carrya"])
        p.emit("v", lambda e, ss=s: e.tensor_copy(
            CB[:, ss:ss + 1], SC["sc_sb"][:, TT - 1:TT]),
            r=["sc_sb"], w=["carryb"])

    def xload(b, t):
        p.emit("y", lambda e, bb=b, tt_=t: e.dma_start(
            out=X[:, tt_ * D:(tt_ + 1) * D],
            in_=xin[(bb * TBT + tt_) * TT:(bb * TBT + tt_ + 1) * TT, :]),
            w=xkeys(t), dma=True)

    def ln1(t):
        xap = X[:, t * D:(t + 1) * D]
        ln_stats(xap, xkeys(t), XT, ["xn_tok"], c=t)
        p.emit("v", lambda e, a=xap, cc=t: e.tensor_scalar(
            out=XT, in0=a, scalar1=SMEAN[:, cc:cc + 1], scalar2=SRSTD[:, cc:cc + 1],
            op0=ALU.subtract, op1=ALU.mult),
            r=xkeys(t) + ["smean%d" % t, "srstd%d" % t], w=["xn_tok"])
        for g in range(4):
            dst = XN.rearrange("p (s c) -> p s c", s=ND, c=513)[:, g * 4:g * 4 + 4,
                                                               1 + t * TT:1 + (t + 1) * TT]
            transpose_group(XT[:, g * 512:(g + 1) * 512], ["xn_tok"], 4,
                            dst, ["XN%d" % (g * 4 + j) for j in range(4)])

    for t in range(TBT):
        xload(0, t)

    for b in range(nblocks):
        for t in range(TBT):
            ln1(t)

        def emit_mix_one(s, tmap, tkey, reg, kfn):
                dst = reg[:, s * 512:(s + 1) * 512]
                p.emit("v", lambda e, ss=s: e.tensor_sub(
                    DIFF, xnt(ss), xnt_ext(ss)),
                    r=["XN%d" % s], w=["diff"])
                p.emit("v", lambda e, ss=s, tm=tmap, d=dst: e.scalar_tensor_tensor(
                    d, DIFF, tm[:, ss:ss + 1], xnt_ext(ss), ALU.mult, ALU.add),
                    r=["diff", tkey, "XN%d" % s], w=[kfn(s)])

        def emit_mix(tmap, tkey, reg, kfn):
            for s in range(ND):
                emit_mix_one(s, tmap, tkey, reg, kfn)

        def proj_chanmajor(wsrcb, n_oc, rhs_fn, rhs_keys_sub, dst_fn, wkey_src,
                           evac=None, post_oc=None):
            for oc in range(n_oc):
                buf, bkey2 = whalf()
                p.emit("y" if oc % 2 == 0 else "g",
                       lambda e, a=buf, o2=oc: e.dma_start(
                    out=a[:, 0:ND * TT].rearrange("p (s c) -> p s c", s=ND),
                    in_=wsrcb[o2].rearrange("(s q) cc -> q s cc", q=TT)),
                    r=list(wkey_src), w=[bkey2], dma=True)
                bank = oc % 8
                for s0 in range(0, ND, 4):
                    last_sub = (s0 == ND - 4)

                    def mfn(e, a=buf, bk=bank, s_=s0):
                        li = None
                        for s in range(s_, s_ + 4):
                            li = e.matmul(PB[bk][:], a[:, s * TT:(s + 1) * TT],
                                          rhs_fn(s), start=(s == 0),
                                          stop=(s == ND - 1))
                        return li
                    wl = ["PB%d" % bank] if (s0 == 0 or last_sub) else []
                    p.emit("t", mfn, r=[bkey2] + rhs_keys_sub(s0), w=wl)
                evac(oc, bank)
                if post_oc is not None:
                    post_oc(oc)

        def kvr_proj(wsrcb, rhs_reg, rhs_kfn, dst_fn, wbkey, post_oc=None):

            def evac(oc, bank):
                dap, dkeys = dst_fn(oc)
                if oc % 2 == 0:
                    p.emit("v", lambda e, bk=bank, d=dap: e.tensor_copy(d, PB[bk][:]),
                           r=["PB%d" % bank], w=dkeys)
                else:
                    p.emit("s", lambda e, bk=bank, d=dap: e.copy(out=d, in_=PB[bk][:]),
                           r=["PB%d" % bank], w=dkeys)

            proj_chanmajor(wsrcb, ND,
                           lambda s: rhs_reg[:, s * 512:(s + 1) * 512],
                           lambda s0: [rhs_kfn(s) for s in range(s0, s0 + 4)],
                           dst_fn, [wbkey], evac=evac, post_oc=post_oc)

        def scan_part1(s):
            kap = OA[:, s * 512:(s + 1) * 512]
            vap = VR[:, s * 512:(s + 1) * 512]
            kk = [okeys(s)]
            p.emit("s", lambda e, a=kap: e.activation(out=SC["sc_p"], in_=a,
                                                      func=AF.Exp),
                   r=kk, w=["sc_p"])
            p.emit("v", lambda e, a=vap: e.tensor_mul(SC["sc_pv"], SC["sc_p"], a),
                   r=["sc_p", "VR%d" % s], w=["sc_pv"])
            p.emit("v", lambda e, ss=s: e.tensor_tensor_scan(
                SC["sc_sa"], LAM[:, ss:ss + 1].broadcast_to([TT, 512]),
                SC["sc_pv"], CA[:, ss:ss + 1], ALU.mult, ALU.add),
                r=["sc_pv", "lamcm", "carrya"], w=["sc_sa"])
            p.emit("v", lambda e, ss=s: e.tensor_tensor_scan(
                SC["sc_sb"], LAM[:, ss:ss + 1].broadcast_to([TT, 512]),
                SC["sc_p"], CB[:, ss:ss + 1], ALU.mult, ALU.add),
                r=["sc_p", "lamcm", "carryb"], w=["sc_sb"])
            p.emit("v", lambda e, a=vap, ss=s: e.scalar_tensor_tensor(
                SC["sc_wkv"], SC["sc_p"], UCM[:, ss:ss + 1], a,
                ALU.mult, ALU.mult),
                r=["sc_p", "ucm", "VR%d" % s], w=["sc_wkv"])
            p.emit("v", lambda e, ss=s: e.tensor_add(
                SC["sc_num"][:, 0:1], CA[:, ss:ss + 1], SC["sc_wkv"][:, 0:1]),
                r=["carrya", "sc_wkv"], w=["sc_num"])
            p.emit("v", lambda e: e.tensor_add(
                SC["sc_num"][:, 1:512], SC["sc_sa"][:, 0:511],
                SC["sc_wkv"][:, 1:512]), r=["sc_sa", "sc_wkv"], w=["sc_num"])
            p.emit("v", lambda e, ss=s: e.scalar_tensor_tensor(
                SC["sc_q"][:, 0:1], SC["sc_p"][:, 0:1], UCM[:, ss:ss + 1],
                CB[:, ss:ss + 1], ALU.mult, ALU.add),
                r=["carryb", "sc_p", "ucm"], w=["sc_q"])
            p.emit("v", lambda e, ss=s: e.scalar_tensor_tensor(
                SC["sc_q"][:, 1:512], SC["sc_p"][:, 1:512], UCM[:, ss:ss + 1],
                SC["sc_sb"][:, 0:511], ALU.mult, ALU.add),
                r=["sc_sb", "sc_p", "ucm"], w=["sc_q"])
            p.emit("v", lambda e, ss=s: e.tensor_copy(CA[:, ss:ss + 1],
                                                      SC["sc_sa"][:, 511:512]),
                   r=["sc_sa"], w=["carrya"])
            p.emit("v", lambda e, ss=s: e.tensor_copy(CB[:, ss:ss + 1],
                                                      SC["sc_sb"][:, 511:512]),
                   r=["sc_sb"], w=["carryb"])
            p.emit("s", lambda e: e.activation(out=SC["sc_pv"], in_=SC["sc_q"],
                                               func=AF.Ln),
                   r=["sc_q"], w=["sc_pv"])
            p.emit("s", lambda e: e.activation(out=SC["sc_pv"], in_=SC["sc_pv"],
                                               func=AF.Exp, scale=NEG1[:, 0:1]),
                   r=["sc_pv", "neg1"], w=["sc_pv"])
            p.emit("v", lambda e, ss=s: e.tensor_mul(
                OA[:, ss * 512:(ss + 1) * 512], SC["sc_num"], SC["sc_pv"]),
                r=["sc_num", "sc_pv"], w=[okeys(s)])

        def rproj_hook(oc):
            if oc % 4 != 3:
                return
            ec2 = oc // 4
            p.emit("s", lambda e, e2=ec2: e.activation(
                out=VR[:, (16 + e2 * 4) * 512:(16 + e2 * 4 + 4) * 512],
                in_=VR[:, (16 + e2 * 4) * 512:(16 + e2 * 4 + 4) * 512],
                func=AF.Sigmoid),
                r=["VR%d" % (16 + ec2 * 4 + j) for j in range(4)],
                w=["VR%d" % (16 + ec2 * 4 + j) for j in range(4)])
            for j in range(4):
                s = ec2 * 4 + j
                p.emit("v", lambda e, ss=s: e.tensor_mul(
                    VR[:, ss * 512:(ss + 1) * 512],
                    OA[:, ss * 512:(ss + 1) * 512],
                    VR[:, (16 + ss) * 512:(16 + ss + 1) * 512]),
                    r=[okeys(s), "VR%d" % (16 + s)], w=["VR%d" % s])

        p.emit("v", lambda e: e.tensor_copy(
            XN.rearrange("p (s c) -> p s c", s=ND, c=513)[:, :, 0:1], BCOL.unsqueeze(2)),
            r=["bcol"], w=["XN%d" % s for s in range(ND)])
        VR16 = VR[:, 16 * 512:32 * 512]
        kmx = lambda s: "MX%d" % s
        kvr16 = lambda s: "VR%d" % (16 + s)
        emit_mix(TMK, "tmkcm", MX, kmx)
        kvr_proj(wkt, MX, kmx,
                 lambda oc: (OA[:, oc * 512:(oc + 1) * 512], [okeys(oc)]), "wbf_k",
                 post_oc=lambda oc: emit_mix_one(oc, TMV, "tmvcm", VR16, kvr16))
        def vproj_post(oc):
            emit_mix_one(oc, TMR, "tmrcm", MX, kmx)
            scan_part1(oc)

        kvr_proj(wvt, VR16, kvr16,
                 lambda oc: (VR[:, oc * 512:(oc + 1) * 512], ["VR%d" % oc]), "wbf_v",
                 post_oc=vproj_post)
        kvr_proj(wrt, MX, kmx,
                 lambda oc: (VR[:, (16 + oc) * 512:(16 + oc + 1) * 512],
                             ["VR%d" % (16 + oc)]), "wbf_r",
                 post_oc=rproj_hook)

        p.emit("v", lambda e: e.tensor_copy(
            BCOL.unsqueeze(2),
            XN.rearrange("p (s c) -> p s c", s=ND, c=513)[:, :, 512:513]),
            r=["XN%d" % s for s in range(ND)], w=["bcol"])

        wot_r = wot.rearrange("(s q) ee -> q s ee", q=TT)
        for ec in range(NEC):
            banks = [0, 1, 2, 3] if ec % 2 == 0 else [4, 5, 6, 7]

            def wsrc3d(s0, e2=ec):
                return wot_r[:, s0:s0 + 4, e2 * 512:(e2 + 1) * 512]

            proj_tokmajor(
                lambda s, t: VR[:, s * 512 + t * TT:s * 512 + (t + 1) * TT],
                lambda s: ["VR%d" % s], wsrc3d, ND, banks,
                wkey_src=["wbf_o"])
            for t in range(TBT):
                p.emit("s", lambda e, bk=banks[t], tt_=t, e2=ec: e.copy(
                    out=OA[:, (tt_ * 4 + e2) * 512:(tt_ * 4 + e2 + 1) * 512],
                    in_=PB[bk][:]),
                    r=["PB%d" % banks[t]], w=[okeys(t * 4 + ec)])

        for t in range(TBT):
            oap = OA[:, t * 4 * 512:(t + 1) * 4 * 512]
            okl = [okeys(t * 4 + j) for j in range(4)]
            ln_stats(oap, okl, XT, ["xn_tok"], c=t + 4)
            p.emit("v", lambda e, a=oap, cc=t + 4: e.tensor_scalar(
                out=a, in0=a, scalar1=SMEAN[:, cc:cc + 1], scalar2=SRSTD[:, cc:cc + 1],
                op0=ALU.subtract, op1=ALU.mult),
                r=okl + ["smean%d" % (t + 4), "srstd%d" % (t + 4)], w=okl)
            p.emit("v", lambda e, a=oap, tt_=t: e.tensor_add(
                X[:, tt_ * D:(tt_ + 1) * D], X[:, tt_ * D:(tt_ + 1) * D], a),
                r=okl + xkeys(t), w=xkeys(t))

        for t in range(TBT):
            xap = X[:, t * D:(t + 1) * D]
            ln_stats(xap, xkeys(t), XT, ["xn_tok"], c=t)
            p.emit("v", lambda e, a=xap, cc=t: e.tensor_scalar(
                out=XT, in0=a, scalar1=SMEAN[:, cc:cc + 1], scalar2=SRSTD[:, cc:cc + 1],
                op0=ALU.subtract, op1=ALU.mult),
                r=xkeys(t) + ["smean%d" % t, "srstd%d" % t], w=["xn_tok"])
            for g in range(4):
                dst = XN.rearrange("p (s c) -> p s c", s=ND, c=513)[:, g * 4:g * 4 + 4,
                                                                   1 + t * TT:1 + (t + 1) * TT]
                transpose_group(XT[:, g * 512:(g + 1) * 512], ["xn_tok"], 4,
                                dst, ["XN%d" % (g * 4 + j) for j in range(4)])

        def l1fn(e):
            li = None
            for s in range(ND):
                li = e.matmul(LP[:],
                              AUPB[:, s * RK:(s + 1) * RK],
                              xnt(s), start=(s == 0), stop=(s == ND - 1))
            return li
        p.emit("t", l1fn, r=["aupb"] + ["XN%d" % s for s in range(ND)], w=["PB7"])
        p.emit("v", lambda e: e.tensor_copy(L1E[0:RK, :], LP[:]),
               r=["PB7"], w=["l1e"])

        for hc4 in range(NHC):
            bslot = hc4 % 2
            bkey = "BUP%d" % bslot
            p.emit("y", lambda e, h=hc4, sl=bslot: e.dma_start(
                out=BUPS[sl], in_=bupxd[:, h * 512:(h + 1) * 512]),
                r=["lorabd"], w=[bkey], dma=True)
            for j4 in range(4):
                oc = hc4 * 4 + j4
                buf, bkey2 = whalf()
                p.emit("y" if oc % 2 == 0 else "g",
                       lambda e, a=buf, o2=oc: e.dma_start(
                    out=a[:, 0:ND * TT].rearrange("p (s c) -> p s c", s=ND),
                    in_=wut[o2].rearrange("(s q) hh -> q s hh", q=TT)),
                    r=["wutd"], w=[bkey2], dma=True)
                bank = oc % 8
                for s0 in range(0, ND, 4):
                    last_sub = (s0 == ND - 4)

                    def mfn(e, a=buf, bk=bank, s_=s0, ls=last_sub, o2=oc, sl=bslot):
                        li = None
                        for s in range(s_, s_ + 4):
                            li = e.matmul(PB[bk][:], a[:, s * TT:(s + 1) * TT],
                                          xnt(s), start=(s == 0), stop=False)
                        if ls:
                            li = e.matmul(
                                PB[bk][:], BUPS[sl][0:RK, (o2 % 4) * TT:(o2 % 4 + 1) * TT],
                                L1E[0:RK, :], start=False, stop=True)
                        return li
                    rl = [bkey2] + ["XN%d" % s for s in range(s0, s0 + 4)]
                    if last_sub:
                        rl += [bkey, "l1e"]
                    wl = ["PB%d" % bank] if (s0 == 0 or last_sub) else []
                    p.emit("t", mfn, r=rl, w=wl)
                dap, dkeys = x2tile(oc)
                p.emit("s", lambda e, bk=bank, d=dap, o2=oc: e.activation(
                    out=d, in_=PB[bk][:], func=AF.Relu,
                    bias=BUPC[:, o2:o2 + 1]),
                    r=["PB%d" % bank, "bupc"], w=dkeys)

        x2keys_all = []
        for j in range(NH):
            x2keys_all += x2tile(j)[1]
        x2keys_all = list(set(x2keys_all))

        def l2fn(e):
            li = None
            for s in range(NH):
                li = e.matmul(LP[:], ADNB[:, s * RK:(s + 1) * RK],
                              x2tile(s)[0], start=(s == 0), stop=(s == NH - 1))
            return li
        p.emit("t", l2fn, r=["adnb"] + x2keys_all, w=["PB7"])
        p.emit("v", lambda e: e.tensor_copy(L2E[0:RK, :], LP[:]),
               r=["PB7"], w=["l2e"])

        wdt_r3 = wdt.rearrange("(s q) ee -> q s ee", q=TT)
        for ec in range(NEC):
            banks = [0, 1, 2, 3] if ec % 2 == 0 else [4, 5, 6, 7]

            def wsrc3d(s0, e2=ec):
                return wdt_r3[:, s0:s0 + 4, e2 * 512:(e2 + 1) * 512]

            def lorafn(e, t, start, stop, bb=tuple(banks), e2=ec):
                return e.matmul(PB[bb[t]][:], L2E[:, t * TT:(t + 1) * TT],
                                BDNX[:, e2 * 512:(e2 + 1) * 512],
                                start=start, stop=stop)

            proj_tokmajor(
                lambda s, t: x2tile(s)[0][:, t * TT:(t + 1) * TT],
                lambda s: x2tile(s)[1], wsrc3d, NH, banks,
                extra_units=[(lorafn, ["l2e", "l2ones", "bdnx"])],
                wkey_src=["wdtd"])
            for t in range(TBT):
                p.emit("v", lambda e, bk=banks[t], tt_=t, e2=ec: e.tensor_copy(
                    OA[:, (tt_ * 4 + e2) * 512:(tt_ * 4 + e2 + 1) * 512],
                    PB[bk][:]),
                    r=["PB%d" % banks[t]], w=[okeys(t * 4 + ec)])

        for t in range(TBT):
            yap = OA[:, t * 4 * 512:(t + 1) * 4 * 512]
            okl = [okeys(t * 4 + j) for j in range(4)]
            ln_stats(yap, okl, XT, ["xn_tok"], c=t + 4)
            p.emit("v", lambda e, a=yap, cc=t + 4: e.tensor_scalar(
                out=a, in0=a, scalar1=SMEAN[:, cc:cc + 1], scalar2=SRSTD[:, cc:cc + 1],
                op0=ALU.subtract, op1=ALU.mult),
                r=okl + ["smean%d" % (t + 4), "srstd%d" % (t + 4)], w=okl)
            p.emit("v", lambda e, a=yap, tt_=t: e.tensor_add(
                a, X[:, tt_ * D:(tt_ + 1) * D], a),
                r=okl + xkeys(t), w=okl)
            p.emit("g", lambda e, bb=b, tt_=t, a=yap: e.dma_start(
                out=out[(bb * TBT + tt_) * TT:(bb * TBT + tt_ + 1) * TT, :],
                in_=a),
                r=okl, w=["outw"], dma=True)
            if b + 1 < nblocks:
                xload(b + 1, t)

    p.wait_all("g", {s: p.cnt[s] for s in ["sgd%d" % i for i in range(6)] if p.cnt[s] > 0})

    with ExitStack() as ctx:
        sems = {}
        for s in p.semnames:
            sems[s] = ctx.enter_context(nc.semaphore(s))
        block = ctx.enter_context(nc.Block())
        eng_holder = {}

        @block.sync
        def _(sync):
            eng_holder["y"] = sync
            for waits, fn, sem, inc in p.q["y"]:
                for s2, c2 in waits.items():
                    sync.wait_ge(sems[s2], c2)
                if fn is not None:
                    inst = fn(sync)
                    if inc:
                        inst.then_inc(sems[sem], inc)

        @block.tensor
        def _(tensor):
            for waits, fn, sem, inc in p.q["t"]:
                for s2, c2 in waits.items():
                    tensor.wait_ge(sems[s2], c2)
                if fn is not None:
                    inst = fn(tensor)
                    if inc:
                        inst.then_inc(sems[sem], inc)

        @block.vector
        def _(vector):
            for waits, fn, sem, inc in p.q["v"]:
                for s2, c2 in waits.items():
                    vector.wait_ge(sems[s2], c2)
                if fn is not None:
                    inst = fn(vector)
                    if inc:
                        inst.then_inc(sems[sem], inc)

        @block.scalar
        def _(scalar):
            for waits, fn, sem, inc in p.q["s"]:
                for s2, c2 in waits.items():
                    scalar.wait_ge(sems[s2], c2)
                if fn is not None:
                    inst = fn(scalar)
                    if inc:
                        inst.then_inc(sems[sem], inc)

        @block.gpsimd
        def _(gpsimd):
            for waits, fn, sem, inc in p.q["g"]:
                for s2, c2 in waits.items():
                    gpsimd.wait_ge(sems[s2], c2)
                if fn is not None:
                    inst = fn(gpsimd)
                    if inc:
                        inst.then_inc(sems[sem], inc)

    psum_ctx.close()
    return nc


def make_in_maps(inputs, nblocks=NBLK):
    x = np.asarray(inputs["L0_attn_x"], dtype=np.float32)
    base = {
        "Wk": np.ascontiguousarray(inputs["L0_attn_Wk"], dtype=np.float32),
        "Wv": np.ascontiguousarray(inputs["L0_attn_Wv"], dtype=np.float32),
        "Wr": np.ascontiguousarray(inputs["L0_attn_Wr"], dtype=np.float32),
        "Wo": np.ascontiguousarray(inputs["L0_attn_Wo"], dtype=np.float32),
        "td": np.ascontiguousarray(inputs["L0_attn_time_decay"], dtype=np.float32),
        "tf": np.ascontiguousarray(inputs["L0_attn_time_first"], dtype=np.float32),
        "tmk": np.ascontiguousarray(np.reshape(inputs["L0_attn_time_mix_k"], [D]), dtype=np.float32),
        "tmv": np.ascontiguousarray(np.reshape(inputs["L0_attn_time_mix_v"], [D]), dtype=np.float32),
        "tmr": np.ascontiguousarray(np.reshape(inputs["L0_attn_time_mix_r"], [D]), dtype=np.float32),
        "wupq": np.ascontiguousarray(inputs["L0_ffn_w_up_q"], dtype=np.int32),
        "wups": np.ascontiguousarray(inputs["L0_ffn_w_up_scale"], dtype=np.float32),
        "bup": np.ascontiguousarray(inputs["L0_ffn_b_up"], dtype=np.float32),
        "upla": np.ascontiguousarray(inputs["L0_ffn_w_up_lora_a"], dtype=np.float32),
        "uplb": np.ascontiguousarray(inputs["L0_ffn_w_up_lora_b"], dtype=np.float32),
        "wdnq": np.ascontiguousarray(inputs["L0_ffn_w_down_q"], dtype=np.int32),
        "wdns": np.ascontiguousarray(inputs["L0_ffn_w_down_scale"], dtype=np.float32),
        "bdn": np.ascontiguousarray(inputs["L0_ffn_b_down"], dtype=np.float32),
        "dnla": np.ascontiguousarray(inputs["L0_ffn_w_down_lora_a"], dtype=np.float32),
        "dnlb": np.ascontiguousarray(inputs["L0_ffn_w_down_lora_b"], dtype=np.float32),
        "c_id": np.eye(TT, dtype=np.float32),
        "c_ones": np.ones((1, 512), dtype=np.float32),
    }
    in_maps = []
    for c in range(NCORES):
        bi, qi = divmod(c, 4)
        t0 = qi * TC
        m = dict(base)
        m["x"] = np.ascontiguousarray(x[bi, t0:t0 + TC, :], dtype=np.float32)
        if qi == 0:
            m["xprev"] = np.zeros((1, D), dtype=np.float32)
            m["xwarm"] = np.zeros((TT, D), dtype=np.float32)
            m["wvalid"] = np.zeros((TT, 1), dtype=np.float32)
        else:
            m["xprev"] = np.ascontiguousarray(x[bi, t0 - 1:t0, :], dtype=np.float32)
            m["xwarm"] = np.ascontiguousarray(x[bi, t0 - TT:t0, :], dtype=np.float32)
            m["wvalid"] = np.ones((TT, 1), dtype=np.float32)
        in_maps.append(m)
    return in_maps


def kernel(**inputs):
    nc = build_program()
    in_maps = make_in_maps(inputs)
    res = run_bass_kernel_spmd(nc, in_maps, core_ids=list(range(NCORES)))
    out = np.empty((2, 16384, D), dtype=np.float32)
    for c in range(NCORES):
        bi, qi = divmod(c, 4)
        t0 = qi * TC
        out[bi, t0:t0 + TC, :] = res.results[c]["out"]
    return out


# version 24
# speedup vs baseline: 1.0128x; 1.0128x over previous
import numpy as np
from contextlib import ExitStack

import concourse.bass as bass
import concourse.mybir as mybir
from concourse.bass_utils import run_bass_kernel_spmd

F32 = mybir.dt.float32
BF16 = mybir.dt.bfloat16
I32 = mybir.dt.int32
AF = mybir.ActivationFunctionType
ALU = mybir.AluOpType
AX = mybir.AxisListType

D = 2048
H = 8192
RK = 16
TT = 128
TBT = 4
TB = TT * TBT
NBLK = 8
TC = NBLK * TB
ND = D // TT
NH = H // TT
NEC = D // 512
NHC = H // 512
EPS = 1e-5
WRING = 8
TKR = 4
NCORES = 8

ENG_SEM = {"t": "st", "v": "sv", "s": "ss"}


class Prog:

    ENGS = ("t", "v", "s", "g", "y")

    def __init__(self):
        self.q = {e: [] for e in self.ENGS}
        semnames = (["st", "sv", "ss", "sg"] + ["sy%d" % i for i in range(8)]
                    + ["sgd%d" % i for i in range(6)])
        self.semnames = semnames
        self.cnt = {s: 0 for s in semnames}
        self.floor = {e: {} for e in self.ENGS}
        self.lw = {}
        self.lr = {}
        self.rr = {"y": 0, "g": 0}

    def emit(self, eng, fn, r=(), w=(), dma=False):
        if dma:
            if eng == "y":
                self.rr["y"] += 1
                sem = "sy%d" % (self.rr["y"] % 8)
            elif eng == "g":
                self.rr["g"] += 1
                sem = "sgd%d" % (self.rr["g"] % 6)
            else:
                raise ValueError(eng)
            inc = 16
        elif eng == "g":
            sem, inc = "sg", 1
        else:
            sem, inc = ENG_SEM[eng], 1
        waits = {}

        def need(s, c):
            if c > waits.get(s, 0):
                waits[s] = c

        if dma and self.cnt[sem] > 0:
            need(sem, self.cnt[sem])
        for k in r:
            x = self.lw.get(k)
            if x:
                need(*x)
        for k in w:
            x = self.lw.get(k)
            if x:
                need(*x)
            for s2, c2 in self.lr.get(k, {}).items():
                need(s2, c2)
        fl = self.floor[eng]
        waits = {s2: c2 for s2, c2 in waits.items() if c2 > fl.get(s2, 0)}
        for s2, c2 in waits.items():
            fl[s2] = c2
        self.cnt[sem] += inc
        cafter = self.cnt[sem]
        self.q[eng].append((waits, fn, sem, inc))
        for k in w:
            self.lw[k] = (sem, cafter)
            self.lr[k] = {}
        for k in r:
            self.lr.setdefault(k, {})[sem] = cafter
        return sem, cafter

    def wait_all(self, eng, sems):
        self.q[eng].append((dict(sems), None, None, 0))


def build_program(nblocks=NBLK, debug=False):
    nc = bass.Bass("TRN2", target_bir_lowering=False)
    p = Prog()

    xin = nc.dram_tensor("x", [TC, D], F32, kind="ExternalInput").ap()
    xprev = nc.dram_tensor("xprev", [1, D], F32, kind="ExternalInput").ap()
    wk_in = nc.dram_tensor("Wk", [D, D], F32, kind="ExternalInput").ap()
    wv_in = nc.dram_tensor("Wv", [D, D], F32, kind="ExternalInput").ap()
    wr_in = nc.dram_tensor("Wr", [D, D], F32, kind="ExternalInput").ap()
    wo_in = nc.dram_tensor("Wo", [D, D], F32, kind="ExternalInput").ap()
    td_in = nc.dram_tensor("td", [D], F32, kind="ExternalInput").ap()
    tf_in = nc.dram_tensor("tf", [D], F32, kind="ExternalInput").ap()
    tmk_in = nc.dram_tensor("tmk", [D], F32, kind="ExternalInput").ap()
    tmv_in = nc.dram_tensor("tmv", [D], F32, kind="ExternalInput").ap()
    tmr_in = nc.dram_tensor("tmr", [D], F32, kind="ExternalInput").ap()
    wupq_in = nc.dram_tensor("wupq", [H, D], I32, kind="ExternalInput").ap()
    wups_in = nc.dram_tensor("wups", [H, D // 64], F32, kind="ExternalInput").ap()
    bup_in = nc.dram_tensor("bup", [H], F32, kind="ExternalInput").ap()
    upla_in = nc.dram_tensor("upla", [D, RK], F32, kind="ExternalInput").ap()
    uplb_in = nc.dram_tensor("uplb", [RK, H], F32, kind="ExternalInput").ap()
    wdnq_in = nc.dram_tensor("wdnq", [D, H], I32, kind="ExternalInput").ap()
    wdns_in = nc.dram_tensor("wdns", [D, H // 64], F32, kind="ExternalInput").ap()
    bdn_in = nc.dram_tensor("bdn", [D], F32, kind="ExternalInput").ap()
    dnla_in = nc.dram_tensor("dnla", [H, RK], F32, kind="ExternalInput").ap()
    dnlb_in = nc.dram_tensor("dnlb", [RK, D], F32, kind="ExternalInput").ap()
    cid_in = nc.dram_tensor("c_id", [TT, TT], F32, kind="ExternalInput").ap()
    cones_in = nc.dram_tensor("c_ones", [1, 512], F32, kind="ExternalInput").ap()
    xwarm_in = nc.dram_tensor("xwarm", [TT, D], F32, kind="ExternalInput").ap()
    wvalid_in = nc.dram_tensor("wvalid", [TT, 1], F32, kind="ExternalInput").ap()
    out = nc.dram_tensor("out", [TC, D], F32, kind="ExternalOutput").ap()

    wkt = nc.dram_tensor("wkt", [ND, D, TT], BF16, kind="Internal").ap()
    wvt = nc.dram_tensor("wvt", [ND, D, TT], BF16, kind="Internal").ap()
    wrt = nc.dram_tensor("wrt", [ND, D, TT], BF16, kind="Internal").ap()
    wot = nc.dram_tensor("wot", [D, D], BF16, kind="Internal").ap()
    wut = nc.dram_tensor("wut", [NH, D, TT], BF16, kind="Internal").ap()
    wdt = nc.dram_tensor("wdt", [H, D], BF16, kind="Internal").ap()
    bupxd = nc.dram_tensor("bupxd", [RK + 1, H], BF16, kind="Internal").ap()
    bdnxd = nc.dram_tensor("bdnxd", [RK + 1, D], BF16, kind="Internal").ap()
    onesd = nc.dram_tensor("onesd", [1, 512], BF16, kind="Internal").ap()

    alloc = []
    cur = [0]

    def sb(name, shape, dtype):
        nbytes = int(np.prod(shape[1:])) * (2 if dtype in (BF16,) else 4)
        cur[0] += (nbytes + 31) // 32 * 32
        t = nc.alloc_sbuf_tensor(name, shape, dtype)
        alloc.append((name, nbytes))
        return t.ap(), 0

    X, x_off = sb("Xar", [TT, TBT * D], F32)
    OA, o_off = sb("Oar", [TT, 16 * 512], F32)
    VR, vr_off = sb("VRar", [TT, 32 * 512], BF16)
    XN, _ = sb("XNar", [TT, ND * 513], BF16)
    MX, _ = sb("MXar", [TT, ND * 512], BF16)
    XT, _ = sb("xn_tok", [TT, D], F32)
    SCAN_BUFS = ["sc_p", "sc_q", "sc_pv", "sc_sa", "sc_sb", "sc_wkv", "sc_num", "sc_pad"]
    SCANT, _ = sb("scant", [TT, 8 * 512], F32)
    SC = {nm: SCANT[:, i * 512:(i + 1) * 512] for i, nm in enumerate(SCAN_BUFS)}
    SCBF = SCANT.bitcast(BF16)
    DIFF, _ = sb("diff", [TT, 512], BF16)
    TK = [sb("tok%d" % i, [TT, 512], F32)[0] for i in range(TKR)]
    WH = [sb("wh%d" % i, [TT, 4 * 512], BF16)[0] for i in range(8)]
    BUPS = [sb("bupx%d" % i, [RK + 1, 512], BF16)[0] for i in range(2)]
    BDNX, _ = sb("bdnx", [RK + 1, D], BF16)
    BUPC, _ = sb("bupc", [TT, NH], F32)
    AUPB, _ = sb("aupb", [TT, ND * RK], BF16)
    ADNB, _ = sb("adnb", [TT, NH * RK], BF16)
    L1E, _ = sb("l1e", [RK + 1, 512], BF16)
    L2E, _ = sb("l2e", [RK + 1, 512], BF16)
    IDF, _ = sb("identf", [TT, TT], F32)
    LAM, _ = sb("lamcm", [TT, ND], F32)
    UCM, _ = sb("ucm", [TT, ND], F32)
    TMK, _ = sb("tmkcm", [TT, ND], F32)
    TMV, _ = sb("tmvcm", [TT, ND], F32)
    TMR, _ = sb("tmrcm", [TT, ND], F32)
    BCOL, _ = sb("bcol", [TT, ND], BF16)
    CA, _ = sb("carrya", [TT, ND], F32)
    CB, _ = sb("carryb", [TT, ND], F32)
    NEG1, _ = sb("neg1", [TT, 1], F32)
    NEGH, _ = sb("negh", [TT, 1], F32)
    VALID, _ = sb("valid", [TT, 1], F32)
    SSUM, _ = sb("ssum0", [TT, 8], F32)
    SSQ, _ = sb("ssq0", [TT, 8], F32)
    SMEAN, _ = sb("smean0", [TT, 8], F32)
    SMSQ, _ = sb("smsq0", [TT, 8], F32)
    SVAR, _ = sb("svar0", [TT, 8], F32)
    SSTD, _ = sb("sstd0", [TT, 8], F32)
    SRSTD, _ = sb("srstd0", [TT, 8], F32)
    PSC = [sb("pscale%d" % i, [TT, 32], F32)[0] for i in range(2)]
    ONEB, _ = sb("onebf", [1, 512], BF16)
    PIN_I = [X[:, 0:D].bitcast(I32), X[:, D:2 * D].bitcast(I32)]
    PIN_F = [X[:, 0:D], X[:, D:2 * D]]
    PKEY_IN = [["X0", "X1"], ["X2", "X3"]]
    SEXP = [X[:, 2 * D:3 * D], X[:, 3 * D:4 * D]]
    PKEY_SEXP = [["X4", "X5"], ["X6", "X7"]]
    DQF = [OA[:, 0:D], OA[:, D:2 * D]]
    PKEY_DQF = [["O0", "O1", "O2", "O3"], ["O4", "O5", "O6", "O7"]]
    TBUF = [OA[:, 2 * D:2 * D + 1024].bitcast(BF16), OA[:, 2 * D + 1024:3 * D].bitcast(BF16)]
    PKEY_TB = [["O8", "O9"], ["O10", "O11"]]

    def xkeys(t):
        return ["X%d" % (2 * t), "X%d" % (2 * t + 1)]

    def okeys(slot):
        return "O%d" % slot

    def xnt(s):
        return XN[:, s * 513 + 1:s * 513 + 513]

    def xnt_ext(s):
        return XN[:, s * 513:s * 513 + 512]

    def x2tile(j):
        if j < 32:
            return VR[:, j * 512:(j + 1) * 512], ["VR%d" % j]
        if j < 48:
            m = j - 32
            return MX[:, m * 512:(m + 1) * 512], ["MX%d" % m]
        m = j - 48
        smap = ["sc_p", "sc_p", "sc_q", "sc_q", "sc_pv", "sc_pv", "sc_sa",
                "sc_sa", "sc_sb", "sc_sb", "sc_wkv", "sc_wkv", "sc_num",
                "sc_num", "sc_pad", "sc_pad"]
        return SCBF[:, m * 512:(m + 1) * 512], [smap[m]]

    psum_ctx = ExitStack()
    PB2 = [psum_ctx.enter_context(nc.psum_tensor("pbw%d" % i, [TT, 1024], F32)).ap()
           for i in range(4)]
    PB = [PB2[i // 2][:, (i % 2) * 512:(i % 2 + 1) * 512] for i in range(8)]
    LP = PB[7][0:RK, :]

    tbank = [0]

    def nexttr():
        b = tbank[0] % 8
        tbank[0] += 1
        return PB[b], "PB%d" % b

    def ln_stats3(src_ap, src_keys, junk_ap, junk_keys, c=0):
        ks = "%d" % c
        p.emit("s", lambda e, a=src_ap, j=junk_ap, cc=c: e.activation(
            out=j, in_=a, func=AF.Square, accum_out=SSQ[:, cc:cc + 1]),
            r=src_keys, w=list(junk_keys) + ["ssq" + ks])
        p.emit("v", lambda e, cc=c: e.tensor_scalar(
            out=SVAR[:, cc:cc + 1], in0=SSQ[:, cc:cc + 1], scalar1=1.0 / D, scalar2=EPS,
            op0=ALU.mult, op1=ALU.add), r=["ssq" + ks], w=["svar" + ks])
        p.emit("v", lambda e, cc=c: e.scalar_tensor_tensor(
            SMSQ[:, cc:cc + 1], SMEAN[:, cc:cc + 1], 1.0, SMEAN[:, cc:cc + 1],
            ALU.mult, ALU.mult), r=["smean" + ks], w=["smsq" + ks])
        p.emit("v", lambda e, cc=c: e.tensor_sub(
            SVAR[:, cc:cc + 1], SVAR[:, cc:cc + 1], SMSQ[:, cc:cc + 1]),
            r=["smsq" + ks], w=["svar" + ks])
        p.emit("s", lambda e, cc=c: e.activation(
            out=SSTD[:, cc:cc + 1], in_=SVAR[:, cc:cc + 1], func=AF.Ln),
            r=["svar" + ks], w=["sstd" + ks])
        p.emit("s", lambda e, cc=c: e.activation(
            out=SRSTD[:, cc:cc + 1], in_=SSTD[:, cc:cc + 1],
            func=AF.Exp, scale=NEGH[:, 0:1]),
            r=["sstd" + ks, "negh"], w=["srstd" + ks])

    def ln_stats(src_ap, src_keys, junk_ap, junk_keys, c=0):
        ks = "%d" % c
        p.emit("s", lambda e, a=src_ap, j=junk_ap, cc=c: e.activation(
            out=j, in_=a, func=AF.Identity, accum_out=SSUM[:, cc:cc + 1]),
            r=src_keys, w=list(junk_keys) + ["ssum" + ks])
        p.emit("s", lambda e, a=src_ap, j=junk_ap, cc=c: e.activation(
            out=j, in_=a, func=AF.Square, accum_out=SSQ[:, cc:cc + 1]),
            r=src_keys, w=list(junk_keys) + ["ssq" + ks])
        p.emit("v", lambda e, cc=c: e.tensor_scalar_mul(
            SMEAN[:, cc:cc + 1], SSUM[:, cc:cc + 1], 1.0 / D),
            r=["ssum" + ks], w=["smean" + ks])
        p.emit("v", lambda e, cc=c: e.scalar_tensor_tensor(
            SMSQ[:, cc:cc + 1], SSUM[:, cc:cc + 1], 1.0 / (D * D), SSUM[:, cc:cc + 1],
            ALU.mult, ALU.mult), r=["ssum" + ks], w=["smsq" + ks])
        p.emit("v", lambda e, cc=c: e.tensor_scalar(
            out=SVAR[:, cc:cc + 1], in0=SSQ[:, cc:cc + 1], scalar1=1.0 / D, scalar2=EPS,
            op0=ALU.mult, op1=ALU.add), r=["ssq" + ks], w=["svar" + ks])
        p.emit("v", lambda e, cc=c: e.tensor_sub(
            SVAR[:, cc:cc + 1], SVAR[:, cc:cc + 1], SMSQ[:, cc:cc + 1]),
            r=["smsq" + ks], w=["svar" + ks])
        p.emit("s", lambda e, cc=c: e.activation(
            out=SSTD[:, cc:cc + 1], in_=SVAR[:, cc:cc + 1], func=AF.Ln),
            r=["svar" + ks], w=["sstd" + ks])
        p.emit("s", lambda e, cc=c: e.activation(
            out=SRSTD[:, cc:cc + 1], in_=SSTD[:, cc:cc + 1],
            func=AF.Exp, scale=NEGH[:, 0:1]),
            r=["sstd" + ks, "negh"], w=["srstd" + ks])

    def transpose_group(src_ap, src_keys, n, dst_ap, dst_keys, dst_eng="v"):
        trap, trkey = nexttr()
        def tfn(e, a=src_ap, nn=n, bk=trap):
            last = None
            for j in range(nn):
                last = e.transpose(bk[:, j * TT:(j + 1) * TT],
                                   a[:, j * TT:(j + 1) * TT], IDF)
            return last
        p.emit("t", tfn, r=list(src_keys) + ["identf"], w=[trkey])
        p.emit(dst_eng, lambda e, bk=trap, d=dst_ap, nn=n: e.tensor_copy(
            d, bk[:, 0:nn * TT].rearrange("p (j c) -> p j c", j=nn)),
            r=[trkey], w=dst_keys)

    p.emit("y", lambda e: e.dma_start(out=IDF, in_=cid_in), w=["identf"], dma=True)
    p.emit("v", lambda e: e.memset(NEG1[:, 0:1], -1.0), w=["neg1"])
    p.emit("v", lambda e: e.memset(NEGH[:, 0:1], -0.5), w=["negh"])
    p.emit("v", lambda e: e.memset(CA, 0.0), w=["carrya"])
    p.emit("v", lambda e: e.memset(CB, 0.0), w=["carryb"])

    p.emit("y", lambda e: e.dma_start(out=TK[0][:1, 0:512], in_=cones_in),
           w=["TK0"], dma=True)
    p.emit("s", lambda e: e.copy(out=ONEB, in_=TK[0][:1, 0:512]),
           r=["TK0"], w=["onebf"])
    p.emit("g", lambda e: e.dma_start(out=onesd, in_=ONEB), r=["onebf"],
           w=["onesd"], dma=True)
    p.emit("y", lambda e: e.dma_start(out=L1E[RK:RK + 1, :], in_=onesd),
           r=["onesd"], w=["l1ones"], dma=True)
    p.emit("y", lambda e: e.dma_start(out=L2E[RK:RK + 1, :], in_=onesd),
           r=["onesd"], w=["l2ones"], dma=True)

    def cm_load(src, dst_ap, dst_key, post=None):
        tmp = TK[1][0:ND, 0:TT]
        p.emit("y", lambda e, s=src: e.dma_start(
            out=tmp, in_=s.rearrange("(s q) -> s q", q=TT)), w=["TK1"], dma=True)
        trap, trkey = nexttr()
        p.emit("t", lambda e, bk=trap: e.transpose(
            bk[:, 0:ND], tmp, IDF[0:ND, 0:ND]),
            r=["TK1", "identf"], w=[trkey])
        p.emit("v", lambda e, bk=trap, d=dst_ap: e.tensor_copy(d, bk[:, 0:ND]),
               r=[trkey], w=[dst_key])
        if post:
            post(dst_ap, dst_key)

    cm_load(td_in, LAM, "lamcm")
    p.emit("s", lambda e: e.activation(out=LAM, in_=LAM, func=AF.Exp),
           r=["lamcm"], w=["lamcm"])
    p.emit("s", lambda e: e.activation(out=LAM, in_=LAM, func=AF.Exp,
                                       scale=NEG1[:, 0:1]),
           r=["lamcm", "neg1"], w=["lamcm"])
    cm_load(tf_in, UCM, "ucm")
    p.emit("s", lambda e: e.activation(out=UCM, in_=UCM, func=AF.Exp),
           r=["ucm"], w=["ucm"])
    cm_load(tmk_in, TMK, "tmkcm")
    cm_load(tmv_in, TMV, "tmvcm")
    cm_load(tmr_in, TMR, "tmrcm")

    def lora_a(src, n, dst_ap, dst_key):
        step = 512 // RK
        for c0 in range(0, n, step):
            nn = min(step, n - c0)
            tmp = TK[2][:, 0:nn * RK]
            p.emit("y", lambda e, s=src, c=c0, nn=nn, t=tmp: e.dma_start(
                out=t.rearrange("p (s r) -> p s r", s=nn),
                in_=s.rearrange("(s p) r -> p s r", p=TT)[:, c:c + nn, :]),
                w=["TK2"], dma=True)
            p.emit("s", lambda e, d=dst_ap, c=c0, nn=nn, t=tmp: e.copy(
                out=d[:, c * RK:(c + nn) * RK], in_=t),
                r=["TK2"], w=[dst_key])

    lora_a(upla_in, ND, AUPB, "aupb")
    lora_a(dnla_in, NH, ADNB, "adnb")

    def lora_b(srcB, srcb, n, dram):
        for c in range(n // 512):
            tmpf = TK[2][0:RK, 0:512]
            tmpb = TK[3][0:RK, 0:256].bitcast(BF16)
            tmpb1 = TK[3][0:1, 256:512].bitcast(BF16)
            p.emit("y", lambda e, s=srcB, cc=c: e.dma_start(
                out=tmpf, in_=s[:, cc * 512:(cc + 1) * 512]), w=["TK2"], dma=True)
            p.emit("s", lambda e: e.copy(out=tmpb, in_=tmpf),
                   r=["TK2"], w=["TK3"])
            p.emit("y", lambda e, s=srcb, cc=c: e.dma_start(
                out=tmpf[0:1, :], in_=s[cc * 512:(cc + 1) * 512].rearrange("(a q) -> a q", a=1)),
                r=["TK3"], w=["TK2"], dma=True)
            p.emit("s", lambda e: e.copy(out=tmpb1, in_=tmpf[0:1, :]),
                   r=["TK2"], w=["TK3b"])
            p.emit("g", lambda e, dd=dram, cc=c: e.dma_start(
                out=dd[0:RK, cc * 512:(cc + 1) * 512], in_=tmpb),
                r=["TK3"], w=["lorabd"], dma=True)
            p.emit("g", lambda e, dd=dram, cc=c: e.dma_start(
                out=dd[RK:RK + 1, cc * 512:(cc + 1) * 512], in_=tmpb1),
                r=["TK3b"], w=["lorabd"], dma=True)

    lora_b(uplb_in, bup_in, H, bupxd)
    lora_b(dnlb_in, bdn_in, D, bdnxd)
    p.emit("y", lambda e: e.dma_start(out=BDNX, in_=bdnxd), r=["lorabd"],
           w=["bdnx"], dma=True)

    p.emit("v", lambda e: e.memset(X[:, 0:D], 0.0), w=xkeys(0))
    p.emit("y", lambda e: e.dma_start(out=X[0:1, 0:D], in_=xprev),
           r=xkeys(0), w=xkeys(0), dma=True)
    xp = X[0:1, 0:D]
    p.emit("v", lambda e: e.reduce_sum(SSUM[0:1, 0:1], xp, axis=AX.X),
           r=xkeys(0), w=["ssum0"])
    p.emit("s", lambda e: e.activation(out=XT[0:1, :], in_=xp, func=AF.Square,
                                       accum_out=SSQ[0:1, 0:1]),
           r=xkeys(0), w=["xn_tok", "ssq0"])
    p.emit("v", lambda e: e.tensor_scalar_mul(SMEAN[0:1, 0:1], SSUM[0:1, 0:1], 1.0 / D),
           r=["ssum0"], w=["smean0"])
    p.emit("v", lambda e: e.scalar_tensor_tensor(
        SMSQ[0:1, 0:1], SSUM[0:1, 0:1], 1.0 / (D * D), SSUM[0:1, 0:1],
        ALU.mult, ALU.mult), r=["ssum0"], w=["smsq0"])
    p.emit("v", lambda e: e.tensor_scalar(
        out=SVAR[0:1, 0:1], in0=SSQ[0:1, 0:1], scalar1=1.0 / D, scalar2=EPS,
        op0=ALU.mult, op1=ALU.add), r=["ssq0"], w=["svar0"])
    p.emit("v", lambda e: e.tensor_sub(SVAR[0:1, 0:1], SVAR[0:1, 0:1], SMSQ[0:1, 0:1]),
           r=["smsq0"], w=["svar0"])
    p.emit("s", lambda e: e.activation(out=SSTD[0:1, 0:1], in_=SVAR[0:1, 0:1],
                                       func=AF.Ln), r=["svar0"], w=["sstd0"])
    p.emit("s", lambda e: e.activation(out=SRSTD[0:1, 0:1], in_=SSTD[0:1, 0:1],
                                       func=AF.Exp, scale=NEGH[0:1, 0:1]),
           r=["sstd0", "negh"], w=["srstd0"])
    p.emit("v", lambda e: e.tensor_scalar(
        out=xp, in0=xp, scalar1=SMEAN[0:1, 0:1], scalar2=SRSTD[0:1, 0:1],
        op0=ALU.subtract, op1=ALU.mult),
        r=xkeys(0) + ["smean0", "srstd0"], w=xkeys(0))
    for g in range(4):
        trap, trkey = nexttr()

        def tfn(e, gg=g, bk=trap):
            last = None
            for j in range(4):
                s = gg * 4 + j
                last = e.transpose(bk[:, j * TT:(j + 1) * TT],
                                   X[:, s * TT:(s + 1) * TT], IDF)
            return last
        p.emit("t", tfn, r=xkeys(0) + ["identf"], w=[trkey])
        p.emit("v", lambda e, gg=g, bk=trap: e.tensor_copy(
            BCOL[:, gg * 4:gg * 4 + 4],
            bk[:, 0:4 * TT].rearrange("p (j c) -> p j c", j=4)[:, :, 0]),
            r=[trkey], w=["bcol"])


    prep_units = []
    prep_it = [0]

    def prep_one(qsrc, ssrc, rr, cc, dst_rearr, wkey):
        i = prep_it[0]
        prep_it[0] += 1
        slot = i % 2
        ink = PKEY_IN[slot]
        if ssrc is None:
            dq = PIN_F[slot]
            dqk = ink
            p.emit("y", lambda e, s=qsrc, r_=rr, c_=cc, b=dq: e.dma_start(
                out=b, in_=s[r_ * TT:(r_ + 1) * TT, c_ * D:(c_ + 1) * D]),
                w=ink, dma=True)
        else:
            inb = PIN_I[slot]
            sx = SEXP[slot]
            sxk = PKEY_SEXP[slot]
            dq = DQF[slot]
            dqk = PKEY_DQF[slot]
            psc = PSC[slot]
            psck = "pscale%d" % slot
            p.emit("y", lambda e, s=qsrc, r_=rr, c_=cc, b=inb: e.dma_start(
                out=b, in_=s[r_ * TT:(r_ + 1) * TT, c_ * D:(c_ + 1) * D]),
                w=ink, dma=True)
            p.emit("y", lambda e, s=ssrc, r_=rr, c_=cc, b=psc: e.dma_start(
                out=b, in_=s[r_ * TT:(r_ + 1) * TT, c_ * 32:(c_ + 1) * 32]),
                w=[psck], dma=True)
            p.emit("s", lambda e, b=psc, d=sx: e.copy(
                out=d.rearrange("p (a b) -> p a b", a=32),
                in_=b.unsqueeze(2).broadcast_to([TT, 32, 64])),
                r=[psck], w=sxk)
            p.emit("v", lambda e, b=inb, s2=sx, d=dq: e.scalar_tensor_tensor(
                d, b, -7.5, s2, ALU.add, ALU.mult),
                r=ink + sxk, w=dqk)
        tslot = i % 2
        tb = TBUF[tslot]
        tbk = PKEY_TB[tslot]
        for g in range(2):
            wb = PB2[(tbank[0] + g) % 4]
            wkeys = ["PB%d" % (((tbank[0] + g) % 4) * 2),
                     "PB%d" % (((tbank[0] + g) % 4) * 2 + 1)]

            def tfn(e, gg=g, bk=wb, d=dq):
                last = None
                for j in range(8):
                    s = gg * 8 + j
                    last = e.transpose(bk[:, j * TT:(j + 1) * TT],
                                       d[:, s * TT:(s + 1) * TT], IDF)
                return last
            p.emit("t", tfn, r=dqk + ["identf"], w=wkeys)
            if g % 2 == 0:
                p.emit("v", lambda e, gg=g, bk=wb, t=tb: e.tensor_copy(
                    t.rearrange("p (s c) -> p s c", s=ND)[:, gg * 8:gg * 8 + 8, :],
                    bk[:, 0:8 * TT].rearrange("p (j c) -> p j c", j=8)),
                    r=wkeys, w=tbk)
            else:
                p.emit("s", lambda e, gg=g, bk=wb, t=tb: e.copy(
                    out=t.rearrange("p (s c) -> p s c", s=ND)[:, gg * 8:gg * 8 + 8, :],
                    in_=bk[:, 0:8 * TT].rearrange("p (j c) -> p j c", j=8)),
                    r=wkeys, w=tbk)
        tbank[0] += 2
        p.emit("g", lambda e, r_=rr, c_=cc, t=tb: e.dma_start(
            out=dst_rearr(r_, c_), in_=t.rearrange("p (s c) -> p s c", s=ND)),
            r=tbk, w=[wkey], dma=True)

    def prep_quant(qsrc, ssrc, dram, n_row_tiles, col_tiles_per_row, dst_rearr, wkey):
        for rr in range(n_row_tiles):
            for cc in range(col_tiles_per_row):
                prep_units.append((qsrc, ssrc, rr, cc, dst_rearr, wkey))

    for w_in_, w_t_, wkey_ in [(wk_in, wkt, "wbf_k"), (wv_in, wvt, "wbf_v"),
                               (wr_in, wrt, "wbf_r")]:
        def att_dst(rr, cc, wt2=w_t_):
            return wt2[rr].rearrange("(s q) ee -> q s ee", q=TT)

        prep_quant(w_in_, None, w_t_, ND, 1, att_dst, wkey_)
    wro_ = wot.rearrange("(s q) ee -> q s ee", q=TT)

    def att_dst_o(rr, cc):
        return wro_[:, :, rr * TT:(rr + 1) * TT]

    prep_quant(wo_in, None, wot, ND, 1, att_dst_o, "wbf_o")

    def wut_dst(rr, cc):
        return wut[rr].rearrange("(s q) hh -> q s hh", q=TT)

    prep_quant(wupq_in, wups_in, wut, NH, 1, wut_dst, "wutd")
    p.emit("y", lambda e: e.dma_start(
        out=TK[1][0:NH, 0:TT], in_=bup_in.rearrange("(s q) -> s q", q=TT)),
        w=["TK1"], dma=True)
    trapb, trkeyb = nexttr()
    p.emit("t", lambda e, bk=trapb: e.transpose(
        bk[:, 0:NH], TK[1][0:NH, 0:TT], IDF[0:NH, 0:NH]),
        r=["TK1", "identf"], w=[trkeyb])
    p.emit("v", lambda e, bk=trapb: e.tensor_copy(BUPC, bk[:, 0:NH]),
           r=[trkeyb], w=["bupc"])
    wdt_r = wdt.rearrange("(s q) ee -> q s ee", q=TT)

    def wdt_dst(rr, cc):
        return wdt_r[:, cc * ND:(cc + 1) * ND, rr * TT:(rr + 1) * TT]

    prep_quant(wdnq_in, wdns_in, wdt, ND, NEC, wdt_dst, "wdtd")

    for u in [u for u in prep_units if u[1] is None]:
        prep_one(*u)




    wring = [0]

    def whalf():
        s = wring[0] % 8
        wring[0] += 1
        return WH[s], "WH%d" % s

    tkr = [0]

    def tkslot():
        s = tkr[0] % TKR
        tkr[0] += 1
        return s

    def proj_tokmajor(lhs_fn, lhs_keys_fn, wsrc3d_fn, n_s, banks,
                      extra_units=(), wkey_src=()):
        nx = len(extra_units)
        nq = n_s // 4
        bufs = {}
        qeng = [0]

        def emit_load(hh):
            buf, bkey = whalf()
            bufs[hh * 4] = (buf, bkey)
            p.emit("y" if hh % 2 == 0 else "g",
                   lambda e, a=buf, s_=hh * 4: e.dma_start(
                out=a.rearrange("p (s c) -> p s c", s=4), in_=wsrc3d_fn(s_)),
                r=list(wkey_src), w=[bkey], dma=True)

        for hh in range(min(3, nq)):
            emit_load(hh)
        for s in range(n_s):
            buf, bkey = bufs[(s // 4) * 4]
            wap = buf[:, (s % 4) * 512:(s % 4 + 1) * 512]
            first, last = (s == 0), (s == n_s - 1 and nx == 0)
            rl = [bkey] + lhs_keys_fn(s)
            wl = []
            if first or last:
                wl = ["PB%d" % b for b in banks]

            def mfn(e, ss=s, a=wap, ff=first, ll=last):
                li = None
                for t in range(TBT):
                    li = e.matmul(PB[banks[t]][:], lhs_fn(ss, t), a,
                                  start=ff, stop=ll)
                return li
            p.emit("t", mfn, r=rl, w=wl)
            if s % 4 == 3 and s // 4 + 3 < nq:
                emit_load(s // 4 + 3)
        for i, (xfn, xreads) in enumerate(extra_units):
            ll = (i == nx - 1)
            wl = ["PB%d" % b for b in banks] if ll else []

            def mfn(e, f=xfn, ll2=ll):
                li = None
                for t in range(TBT):
                    li = f(e, t, False, ll2)
                return li
            p.emit("t", mfn, r=list(xreads), w=wl)

    p.emit("y", lambda e: e.dma_start(out=VALID, in_=wvalid_in), w=["valid"], dma=True)
    p.emit("y", lambda e: e.dma_start(out=X[:, 0:D], in_=xwarm_in),
           r=xkeys(0), w=xkeys(0), dma=True)
    ln_stats(X[:, 0:D], xkeys(0), XT, ["xn_tok"], c=0)
    p.emit("v", lambda e: e.tensor_scalar(
        out=XT, in0=X[:, 0:D], scalar1=SMEAN[:, 0:1], scalar2=SRSTD[:, 0:1],
        op0=ALU.subtract, op1=ALU.mult),
        r=xkeys(0) + ["smean0", "srstd0"], w=["xn_tok"])
    for g in range(4):
        dst = XN.rearrange("p (s c) -> p s c", s=ND, c=513)[:, g * 4:g * 4 + 4, 1:1 + TT]
        transpose_group(XT[:, g * 512:(g + 1) * 512], ["xn_tok"], 4,
                        dst, ["XN%d" % (g * 4 + j) for j in range(4)])

    def warm_proj(wsrc, tmap, tkey, dst_off_fn, wbkey):
        for s in range(ND):
            xs = XN[:, s * 513 + 1:s * 513 + 1 + TT]
            p.emit("v", lambda e, a=xs: e.tensor_sub(
                DIFF[:, 1:TT], a[:, 1:TT], a[:, 0:TT - 1]),
                r=["XN%d" % s], w=["diff"])
            p.emit("v", lambda e, a=xs, ss=s, tm=tmap: e.scalar_tensor_tensor(
                MX[:, ss * 512 + 1:ss * 512 + TT], DIFF[:, 1:TT],
                tm[:, ss:ss + 1], a[:, 0:TT - 1], ALU.mult, ALU.add),
                r=["diff", tkey, "XN%d" % s], w=["MX%d" % s])
            p.emit("v", lambda e, a=xs, ss=s, tm=tmap: e.tensor_scalar_mul(
                MX[:, ss * 512:ss * 512 + 1], a[:, 0:1], tm[:, ss:ss + 1]),
                r=[tkey, "XN%d" % s], w=["MX%d" % s])
        for oc in range(ND):
            buf, bkey2 = whalf()
            p.emit("y", lambda e, a=buf, o2=oc, wb=wsrc: e.dma_start(
                out=a[:, 0:ND * TT].rearrange("p (s c) -> p s c", s=ND),
                in_=wb[o2].rearrange("(s q) ee -> q s ee", q=TT)),
                r=[wbkey], w=[bkey2], dma=True)
            bank = oc % 8

            def mfn(e, a=buf, bk=bank):
                li = None
                for s in range(ND):
                    li = e.matmul(PB[bk][:, 0:TT], a[:, s * TT:(s + 1) * TT],
                                  MX[:, s * 512:s * 512 + TT],
                                  start=(s == 0), stop=(s == ND - 1))
                return li
            p.emit("t", mfn,
                   r=[bkey2] + ["MX%d" % s for s in range(ND)],
                   w=["PB%d" % bank])
            dap, dkeys = dst_off_fn(oc)
            p.emit("v", lambda e, bk=bank, d=dap: e.tensor_copy(d, PB[bk][:, 0:TT]),
                   r=["PB%d" % bank], w=dkeys)

    warm_proj(wkt, TMK, "tmkcm",
              lambda oc: (VR[:, (16 + oc) * 512:(16 + oc) * 512 + TT],
                          ["VR%d" % (16 + oc)]), "wbf_k")
    warm_proj(wvt, TMV, "tmvcm",
              lambda oc: (VR[:, oc * 512:oc * 512 + TT], ["VR%d" % oc]), "wbf_v")
    for s in range(ND):
        kap = VR[:, (16 + s) * 512:(16 + s) * 512 + TT]
        vap = VR[:, s * 512:s * 512 + TT]
        p.emit("s", lambda e, a=kap: e.activation(out=SC["sc_p"][:, 0:TT], in_=a,
                                                  func=AF.Exp),
               r=["VR%d" % (16 + s)], w=["sc_p"])
        p.emit("v", lambda e: e.tensor_scalar_mul(
            SC["sc_p"][:, 0:TT], SC["sc_p"][:, 0:TT], VALID[:, 0:1]),
            r=["sc_p", "valid"], w=["sc_p"])
        p.emit("v", lambda e, a=vap: e.tensor_mul(
            SC["sc_pv"][:, 0:TT], SC["sc_p"][:, 0:TT], a),
            r=["sc_p", "VR%d" % s], w=["sc_pv"])
        p.emit("v", lambda e, ss=s: e.tensor_tensor_scan(
            SC["sc_sa"][:, 0:TT], LAM[:, ss:ss + 1].broadcast_to([TT, TT]),
            SC["sc_pv"][:, 0:TT], 0.0, ALU.mult, ALU.add),
            r=["sc_pv", "lamcm"], w=["sc_sa"])
        p.emit("v", lambda e, ss=s: e.tensor_tensor_scan(
            SC["sc_sb"][:, 0:TT], LAM[:, ss:ss + 1].broadcast_to([TT, TT]),
            SC["sc_p"][:, 0:TT], 0.0, ALU.mult, ALU.add),
            r=["sc_p", "lamcm"], w=["sc_sb"])
        p.emit("v", lambda e, ss=s: e.tensor_copy(
            CA[:, ss:ss + 1], SC["sc_sa"][:, TT - 1:TT]),
            r=["sc_sa"], w=["carrya"])
        p.emit("v", lambda e, ss=s: e.tensor_copy(
            CB[:, ss:ss + 1], SC["sc_sb"][:, TT - 1:TT]),
            r=["sc_sb"], w=["carryb"])

    for u in [u for u in prep_units if u[1] is not None]:
        prep_one(*u)
    prep_units.clear()

    def xload(b, t):
        p.emit("y", lambda e, bb=b, tt_=t: e.dma_start(
            out=X[:, tt_ * D:(tt_ + 1) * D],
            in_=xin[(bb * TBT + tt_) * TT:(bb * TBT + tt_ + 1) * TT, :]),
            w=xkeys(t), dma=True)

    def ln1(t):
        xap = X[:, t * D:(t + 1) * D]
        ln_stats(xap, xkeys(t), XT, ["xn_tok"], c=t)
        p.emit("v", lambda e, a=xap, cc=t: e.tensor_scalar(
            out=XT, in0=a, scalar1=SMEAN[:, cc:cc + 1], scalar2=SRSTD[:, cc:cc + 1],
            op0=ALU.subtract, op1=ALU.mult),
            r=xkeys(t) + ["smean%d" % t, "srstd%d" % t], w=["xn_tok"])
        for g in range(4):
            dst = XN.rearrange("p (s c) -> p s c", s=ND, c=513)[:, g * 4:g * 4 + 4,
                                                               1 + t * TT:1 + (t + 1) * TT]
            transpose_group(XT[:, g * 512:(g + 1) * 512], ["xn_tok"], 4,
                            dst, ["XN%d" % (g * 4 + j) for j in range(4)])

    for t in range(TBT):
        xload(0, t)

    for b in range(nblocks):
        for t in range(TBT):
            ln1(t)

        def emit_mix_one(s, tmap, tkey, reg, kfn):
                dst = reg[:, s * 512:(s + 1) * 512]
                p.emit("v", lambda e, ss=s: e.tensor_sub(
                    DIFF, xnt(ss), xnt_ext(ss)),
                    r=["XN%d" % s], w=["diff"])
                p.emit("v", lambda e, ss=s, tm=tmap, d=dst: e.scalar_tensor_tensor(
                    d, DIFF, tm[:, ss:ss + 1], xnt_ext(ss), ALU.mult, ALU.add),
                    r=["diff", tkey, "XN%d" % s], w=[kfn(s)])

        def emit_mix(tmap, tkey, reg, kfn):
            for s in range(ND):
                emit_mix_one(s, tmap, tkey, reg, kfn)

        def proj_chanmajor(wsrcb, n_oc, rhs_fn, rhs_keys_sub, dst_fn, wkey_src,
                           evac=None, post_oc=None):
            for oc in range(n_oc):
                buf, bkey2 = whalf()
                p.emit("y" if oc % 2 == 0 else "g",
                       lambda e, a=buf, o2=oc: e.dma_start(
                    out=a[:, 0:ND * TT].rearrange("p (s c) -> p s c", s=ND),
                    in_=wsrcb[o2].rearrange("(s q) cc -> q s cc", q=TT)),
                    r=list(wkey_src), w=[bkey2], dma=True)
                bank = oc % 8
                for s0 in range(0, ND, 4):
                    last_sub = (s0 == ND - 4)

                    def mfn(e, a=buf, bk=bank, s_=s0):
                        li = None
                        for s in range(s_, s_ + 4):
                            li = e.matmul(PB[bk][:], a[:, s * TT:(s + 1) * TT],
                                          rhs_fn(s), start=(s == 0),
                                          stop=(s == ND - 1))
                        return li
                    wl = ["PB%d" % bank] if (s0 == 0 or last_sub) else []
                    p.emit("t", mfn, r=[bkey2] + rhs_keys_sub(s0), w=wl)
                evac(oc, bank)
                if post_oc is not None:
                    post_oc(oc)

        def kvr_proj(wsrcb, rhs_reg, rhs_kfn, dst_fn, wbkey, post_oc=None):

            def evac(oc, bank):
                dap, dkeys = dst_fn(oc)
                if oc % 2 == 0:
                    p.emit("v", lambda e, bk=bank, d=dap: e.tensor_copy(d, PB[bk][:]),
                           r=["PB%d" % bank], w=dkeys)
                else:
                    p.emit("s", lambda e, bk=bank, d=dap: e.copy(out=d, in_=PB[bk][:]),
                           r=["PB%d" % bank], w=dkeys)

            proj_chanmajor(wsrcb, ND,
                           lambda s: rhs_reg[:, s * 512:(s + 1) * 512],
                           lambda s0: [rhs_kfn(s) for s in range(s0, s0 + 4)],
                           dst_fn, [wbkey], evac=evac, post_oc=post_oc)

        def scan_part1(s):
            kap = OA[:, s * 512:(s + 1) * 512]
            vap = VR[:, s * 512:(s + 1) * 512]
            kk = [okeys(s)]
            p.emit("s", lambda e, a=kap: e.activation(out=SC["sc_p"], in_=a,
                                                      func=AF.Exp),
                   r=kk, w=["sc_p"])
            p.emit("v", lambda e, a=vap: e.tensor_mul(SC["sc_pv"], SC["sc_p"], a),
                   r=["sc_p", "VR%d" % s], w=["sc_pv"])
            p.emit("v", lambda e, ss=s: e.tensor_tensor_scan(
                SC["sc_sa"], LAM[:, ss:ss + 1].broadcast_to([TT, 512]),
                SC["sc_pv"], CA[:, ss:ss + 1], ALU.mult, ALU.add),
                r=["sc_pv", "lamcm", "carrya"], w=["sc_sa"])
            p.emit("v", lambda e, ss=s: e.tensor_tensor_scan(
                SC["sc_sb"], LAM[:, ss:ss + 1].broadcast_to([TT, 512]),
                SC["sc_p"], CB[:, ss:ss + 1], ALU.mult, ALU.add),
                r=["sc_p", "lamcm", "carryb"], w=["sc_sb"])
            p.emit("v", lambda e, a=vap, ss=s: e.scalar_tensor_tensor(
                SC["sc_wkv"], SC["sc_p"], UCM[:, ss:ss + 1], a,
                ALU.mult, ALU.mult),
                r=["sc_p", "ucm", "VR%d" % s], w=["sc_wkv"])
            p.emit("v", lambda e, ss=s: e.tensor_add(
                SC["sc_num"][:, 0:1], CA[:, ss:ss + 1], SC["sc_wkv"][:, 0:1]),
                r=["carrya", "sc_wkv"], w=["sc_num"])
            p.emit("v", lambda e: e.tensor_add(
                SC["sc_num"][:, 1:512], SC["sc_sa"][:, 0:511],
                SC["sc_wkv"][:, 1:512]), r=["sc_sa", "sc_wkv"], w=["sc_num"])
            p.emit("v", lambda e, ss=s: e.scalar_tensor_tensor(
                SC["sc_q"][:, 0:1], SC["sc_p"][:, 0:1], UCM[:, ss:ss + 1],
                CB[:, ss:ss + 1], ALU.mult, ALU.add),
                r=["carryb", "sc_p", "ucm"], w=["sc_q"])
            p.emit("v", lambda e, ss=s: e.scalar_tensor_tensor(
                SC["sc_q"][:, 1:512], SC["sc_p"][:, 1:512], UCM[:, ss:ss + 1],
                SC["sc_sb"][:, 0:511], ALU.mult, ALU.add),
                r=["sc_sb", "sc_p", "ucm"], w=["sc_q"])
            p.emit("v", lambda e, ss=s: e.tensor_copy(CA[:, ss:ss + 1],
                                                      SC["sc_sa"][:, 511:512]),
                   r=["sc_sa"], w=["carrya"])
            p.emit("v", lambda e, ss=s: e.tensor_copy(CB[:, ss:ss + 1],
                                                      SC["sc_sb"][:, 511:512]),
                   r=["sc_sb"], w=["carryb"])
            p.emit("s", lambda e: e.activation(out=SC["sc_pv"], in_=SC["sc_q"],
                                               func=AF.Ln),
                   r=["sc_q"], w=["sc_pv"])
            p.emit("s", lambda e: e.activation(out=SC["sc_pv"], in_=SC["sc_pv"],
                                               func=AF.Exp, scale=NEG1[:, 0:1]),
                   r=["sc_pv", "neg1"], w=["sc_pv"])
            p.emit("v", lambda e, ss=s: e.tensor_mul(
                OA[:, ss * 512:(ss + 1) * 512], SC["sc_num"], SC["sc_pv"]),
                r=["sc_num", "sc_pv"], w=[okeys(s)])

        def rproj_hook(oc):
            if oc % 4 != 3:
                return
            ec2 = oc // 4
            p.emit("s", lambda e, e2=ec2: e.activation(
                out=VR[:, (16 + e2 * 4) * 512:(16 + e2 * 4 + 4) * 512],
                in_=VR[:, (16 + e2 * 4) * 512:(16 + e2 * 4 + 4) * 512],
                func=AF.Sigmoid),
                r=["VR%d" % (16 + ec2 * 4 + j) for j in range(4)],
                w=["VR%d" % (16 + ec2 * 4 + j) for j in range(4)])
            for j in range(4):
                s = ec2 * 4 + j
                p.emit("v", lambda e, ss=s: e.tensor_mul(
                    VR[:, ss * 512:(ss + 1) * 512],
                    OA[:, ss * 512:(ss + 1) * 512],
                    VR[:, (16 + ss) * 512:(16 + ss + 1) * 512]),
                    r=[okeys(s), "VR%d" % (16 + s)], w=["VR%d" % s])

        p.emit("v", lambda e: e.tensor_copy(
            XN.rearrange("p (s c) -> p s c", s=ND, c=513)[:, :, 0:1], BCOL.unsqueeze(2)),
            r=["bcol"], w=["XN%d" % s for s in range(ND)])
        VR16 = VR[:, 16 * 512:32 * 512]
        kmx = lambda s: "MX%d" % s
        kvr16 = lambda s: "VR%d" % (16 + s)
        emit_mix(TMK, "tmkcm", MX, kmx)
        kvr_proj(wkt, MX, kmx,
                 lambda oc: (OA[:, oc * 512:(oc + 1) * 512], [okeys(oc)]), "wbf_k",
                 post_oc=lambda oc: emit_mix_one(oc, TMV, "tmvcm", VR16, kvr16))
        def vproj_post(oc):
            emit_mix_one(oc, TMR, "tmrcm", MX, kmx)
            scan_part1(oc)

        kvr_proj(wvt, VR16, kvr16,
                 lambda oc: (VR[:, oc * 512:(oc + 1) * 512], ["VR%d" % oc]), "wbf_v",
                 post_oc=vproj_post)
        kvr_proj(wrt, MX, kmx,
                 lambda oc: (VR[:, (16 + oc) * 512:(16 + oc + 1) * 512],
                             ["VR%d" % (16 + oc)]), "wbf_r",
                 post_oc=rproj_hook)

        p.emit("v", lambda e: e.tensor_copy(
            BCOL.unsqueeze(2),
            XN.rearrange("p (s c) -> p s c", s=ND, c=513)[:, :, 512:513]),
            r=["XN%d" % s for s in range(ND)], w=["bcol"])

        wot_r = wot.rearrange("(s q) ee -> q s ee", q=TT)
        for ec in range(NEC):
            banks = [0, 1, 2, 3] if ec % 2 == 0 else [4, 5, 6, 7]

            def wsrc3d(s0, e2=ec):
                return wot_r[:, s0:s0 + 4, e2 * 512:(e2 + 1) * 512]

            proj_tokmajor(
                lambda s, t: VR[:, s * 512 + t * TT:s * 512 + (t + 1) * TT],
                lambda s: ["VR%d" % s], wsrc3d, ND, banks,
                wkey_src=["wbf_o"])
            for t in range(TBT):
                p.emit("s", lambda e, bk=banks[t], tt_=t, e2=ec: e.copy(
                    out=OA[:, (tt_ * 4 + e2) * 512:(tt_ * 4 + e2 + 1) * 512],
                    in_=PB[bk][:]),
                    r=["PB%d" % banks[t]], w=[okeys(t * 4 + ec)])

        for t in range(TBT):
            oap = OA[:, t * 4 * 512:(t + 1) * 4 * 512]
            okl = [okeys(t * 4 + j) for j in range(4)]
            ln_stats(oap, okl, XT, ["xn_tok"], c=t + 4)
            p.emit("v", lambda e, a=oap, cc=t + 4: e.tensor_scalar(
                out=a, in0=a, scalar1=SMEAN[:, cc:cc + 1], scalar2=SRSTD[:, cc:cc + 1],
                op0=ALU.subtract, op1=ALU.mult),
                r=okl + ["smean%d" % (t + 4), "srstd%d" % (t + 4)], w=okl)
            p.emit("v", lambda e, a=oap, tt_=t: e.tensor_add(
                X[:, tt_ * D:(tt_ + 1) * D], X[:, tt_ * D:(tt_ + 1) * D], a),
                r=okl + xkeys(t), w=xkeys(t))

        for t in range(TBT):
            xap = X[:, t * D:(t + 1) * D]
            ln_stats3(xap, xkeys(t), XT, ["xn_tok"], c=t)
            p.emit("v", lambda e, a=xap, cc=t: e.tensor_scalar(
                out=XT, in0=a, scalar1=SMEAN[:, cc:cc + 1], scalar2=SRSTD[:, cc:cc + 1],
                op0=ALU.subtract, op1=ALU.mult),
                r=xkeys(t) + ["smean%d" % t, "srstd%d" % t], w=["xn_tok"])
            for g in range(4):
                dst = XN.rearrange("p (s c) -> p s c", s=ND, c=513)[:, g * 4:g * 4 + 4,
                                                                   1 + t * TT:1 + (t + 1) * TT]
                transpose_group(XT[:, g * 512:(g + 1) * 512], ["xn_tok"], 4,
                                dst, ["XN%d" % (g * 4 + j) for j in range(4)])

        def l1fn(e):
            li = None
            for s in range(ND):
                li = e.matmul(LP[:],
                              AUPB[:, s * RK:(s + 1) * RK],
                              xnt(s), start=(s == 0), stop=(s == ND - 1))
            return li
        p.emit("t", l1fn, r=["aupb"] + ["XN%d" % s for s in range(ND)], w=["PB7"])
        p.emit("v", lambda e: e.tensor_copy(L1E[0:RK, :], LP[:]),
               r=["PB7"], w=["l1e"])

        for hc4 in range(NHC):
            bslot = hc4 % 2
            bkey = "BUP%d" % bslot
            p.emit("y", lambda e, h=hc4, sl=bslot: e.dma_start(
                out=BUPS[sl], in_=bupxd[:, h * 512:(h + 1) * 512]),
                r=["lorabd"], w=[bkey], dma=True)
            for j4 in range(4):
                oc = hc4 * 4 + j4
                buf, bkey2 = whalf()
                p.emit("y" if oc % 2 == 0 else "g",
                       lambda e, a=buf, o2=oc: e.dma_start(
                    out=a[:, 0:ND * TT].rearrange("p (s c) -> p s c", s=ND),
                    in_=wut[o2].rearrange("(s q) hh -> q s hh", q=TT)),
                    r=["wutd"], w=[bkey2], dma=True)
                bank = oc % 8
                for s0 in range(0, ND, 4):
                    last_sub = (s0 == ND - 4)

                    def mfn(e, a=buf, bk=bank, s_=s0, ls=last_sub, o2=oc, sl=bslot):
                        li = None
                        for s in range(s_, s_ + 4):
                            li = e.matmul(PB[bk][:], a[:, s * TT:(s + 1) * TT],
                                          xnt(s), start=(s == 0), stop=False)
                        if ls:
                            li = e.matmul(
                                PB[bk][:], BUPS[sl][0:RK, (o2 % 4) * TT:(o2 % 4 + 1) * TT],
                                L1E[0:RK, :], start=False, stop=True)
                        return li
                    rl = [bkey2] + ["XN%d" % s for s in range(s0, s0 + 4)]
                    if last_sub:
                        rl += [bkey, "l1e"]
                    wl = ["PB%d" % bank] if (s0 == 0 or last_sub) else []
                    p.emit("t", mfn, r=rl, w=wl)
                dap, dkeys = x2tile(oc)
                p.emit("s", lambda e, bk=bank, d=dap, o2=oc: e.activation(
                    out=d, in_=PB[bk][:], func=AF.Relu,
                    bias=BUPC[:, o2:o2 + 1]),
                    r=["PB%d" % bank, "bupc"], w=dkeys)

        x2keys_all = []
        for j in range(NH):
            x2keys_all += x2tile(j)[1]
        x2keys_all = list(set(x2keys_all))

        def l2fn(e):
            li = None
            for s in range(NH):
                li = e.matmul(LP[:], ADNB[:, s * RK:(s + 1) * RK],
                              x2tile(s)[0], start=(s == 0), stop=(s == NH - 1))
            return li
        p.emit("t", l2fn, r=["adnb"] + x2keys_all, w=["PB7"])
        p.emit("v", lambda e: e.tensor_copy(L2E[0:RK, :], LP[:]),
               r=["PB7"], w=["l2e"])

        wdt_r3 = wdt.rearrange("(s q) ee -> q s ee", q=TT)
        for ec in range(NEC):
            banks = [0, 1, 2, 3] if ec % 2 == 0 else [4, 5, 6, 7]

            def wsrc3d(s0, e2=ec):
                return wdt_r3[:, s0:s0 + 4, e2 * 512:(e2 + 1) * 512]

            def lorafn(e, t, start, stop, bb=tuple(banks), e2=ec):
                return e.matmul(PB[bb[t]][:], L2E[:, t * TT:(t + 1) * TT],
                                BDNX[:, e2 * 512:(e2 + 1) * 512],
                                start=start, stop=stop)

            proj_tokmajor(
                lambda s, t: x2tile(s)[0][:, t * TT:(t + 1) * TT],
                lambda s: x2tile(s)[1], wsrc3d, NH, banks,
                extra_units=[(lorafn, ["l2e", "l2ones", "bdnx"])],
                wkey_src=["wdtd"])
            for t in range(TBT):
                p.emit("v", lambda e, bk=banks[t], tt_=t, e2=ec: e.tensor_copy(
                    OA[:, (tt_ * 4 + e2) * 512:(tt_ * 4 + e2 + 1) * 512],
                    PB[bk][:]),
                    r=["PB%d" % banks[t]], w=[okeys(t * 4 + ec)])

        for t in range(TBT):
            yap = OA[:, t * 4 * 512:(t + 1) * 4 * 512]
            okl = [okeys(t * 4 + j) for j in range(4)]
            ln_stats(yap, okl, XT, ["xn_tok"], c=t + 4)
            p.emit("v", lambda e, a=yap, cc=t + 4: e.tensor_scalar(
                out=a, in0=a, scalar1=SMEAN[:, cc:cc + 1], scalar2=SRSTD[:, cc:cc + 1],
                op0=ALU.subtract, op1=ALU.mult),
                r=okl + ["smean%d" % (t + 4), "srstd%d" % (t + 4)], w=okl)
            p.emit("v", lambda e, a=yap, tt_=t: e.tensor_add(
                a, X[:, tt_ * D:(tt_ + 1) * D], a),
                r=okl + xkeys(t), w=okl)
            p.emit("g", lambda e, bb=b, tt_=t, a=yap: e.dma_start(
                out=out[(bb * TBT + tt_) * TT:(bb * TBT + tt_ + 1) * TT, :],
                in_=a),
                r=okl, w=["outw"], dma=True)
            if b + 1 < nblocks:
                xload(b + 1, t)

    p.wait_all("g", {s: p.cnt[s] for s in ["sgd%d" % i for i in range(6)] if p.cnt[s] > 0})

    with ExitStack() as ctx:
        sems = {}
        for s in p.semnames:
            sems[s] = ctx.enter_context(nc.semaphore(s))
        block = ctx.enter_context(nc.Block())
        eng_holder = {}

        @block.sync
        def _(sync):
            eng_holder["y"] = sync
            for waits, fn, sem, inc in p.q["y"]:
                for s2, c2 in waits.items():
                    sync.wait_ge(sems[s2], c2)
                if fn is not None:
                    inst = fn(sync)
                    if inc:
                        inst.then_inc(sems[sem], inc)

        @block.tensor
        def _(tensor):
            for waits, fn, sem, inc in p.q["t"]:
                for s2, c2 in waits.items():
                    tensor.wait_ge(sems[s2], c2)
                if fn is not None:
                    inst = fn(tensor)
                    if inc:
                        inst.then_inc(sems[sem], inc)

        @block.vector
        def _(vector):
            for waits, fn, sem, inc in p.q["v"]:
                for s2, c2 in waits.items():
                    vector.wait_ge(sems[s2], c2)
                if fn is not None:
                    inst = fn(vector)
                    if inc:
                        inst.then_inc(sems[sem], inc)

        @block.scalar
        def _(scalar):
            for waits, fn, sem, inc in p.q["s"]:
                for s2, c2 in waits.items():
                    scalar.wait_ge(sems[s2], c2)
                if fn is not None:
                    inst = fn(scalar)
                    if inc:
                        inst.then_inc(sems[sem], inc)

        @block.gpsimd
        def _(gpsimd):
            for waits, fn, sem, inc in p.q["g"]:
                for s2, c2 in waits.items():
                    gpsimd.wait_ge(sems[s2], c2)
                if fn is not None:
                    inst = fn(gpsimd)
                    if inc:
                        inst.then_inc(sems[sem], inc)

    psum_ctx.close()
    return nc


def make_in_maps(inputs, nblocks=NBLK):
    x = np.asarray(inputs["L0_attn_x"], dtype=np.float32)
    base = {
        "Wk": np.ascontiguousarray(inputs["L0_attn_Wk"], dtype=np.float32),
        "Wv": np.ascontiguousarray(inputs["L0_attn_Wv"], dtype=np.float32),
        "Wr": np.ascontiguousarray(inputs["L0_attn_Wr"], dtype=np.float32),
        "Wo": np.ascontiguousarray(inputs["L0_attn_Wo"], dtype=np.float32),
        "td": np.ascontiguousarray(inputs["L0_attn_time_decay"], dtype=np.float32),
        "tf": np.ascontiguousarray(inputs["L0_attn_time_first"], dtype=np.float32),
        "tmk": np.ascontiguousarray(np.reshape(inputs["L0_attn_time_mix_k"], [D]), dtype=np.float32),
        "tmv": np.ascontiguousarray(np.reshape(inputs["L0_attn_time_mix_v"], [D]), dtype=np.float32),
        "tmr": np.ascontiguousarray(np.reshape(inputs["L0_attn_time_mix_r"], [D]), dtype=np.float32),
        "wupq": np.ascontiguousarray(inputs["L0_ffn_w_up_q"], dtype=np.int32),
        "wups": np.ascontiguousarray(inputs["L0_ffn_w_up_scale"], dtype=np.float32),
        "bup": np.ascontiguousarray(inputs["L0_ffn_b_up"], dtype=np.float32),
        "upla": np.ascontiguousarray(inputs["L0_ffn_w_up_lora_a"], dtype=np.float32),
        "uplb": np.ascontiguousarray(inputs["L0_ffn_w_up_lora_b"], dtype=np.float32),
        "wdnq": np.ascontiguousarray(inputs["L0_ffn_w_down_q"], dtype=np.int32),
        "wdns": np.ascontiguousarray(inputs["L0_ffn_w_down_scale"], dtype=np.float32),
        "bdn": np.ascontiguousarray(inputs["L0_ffn_b_down"], dtype=np.float32),
        "dnla": np.ascontiguousarray(inputs["L0_ffn_w_down_lora_a"], dtype=np.float32),
        "dnlb": np.ascontiguousarray(inputs["L0_ffn_w_down_lora_b"], dtype=np.float32),
        "c_id": np.eye(TT, dtype=np.float32),
        "c_ones": np.ones((1, 512), dtype=np.float32),
    }
    in_maps = []
    for c in range(NCORES):
        bi, qi = divmod(c, 4)
        t0 = qi * TC
        m = dict(base)
        m["x"] = np.ascontiguousarray(x[bi, t0:t0 + TC, :], dtype=np.float32)
        if qi == 0:
            m["xprev"] = np.zeros((1, D), dtype=np.float32)
            m["xwarm"] = np.zeros((TT, D), dtype=np.float32)
            m["wvalid"] = np.zeros((TT, 1), dtype=np.float32)
        else:
            m["xprev"] = np.ascontiguousarray(x[bi, t0 - 1:t0, :], dtype=np.float32)
            m["xwarm"] = np.ascontiguousarray(x[bi, t0 - TT:t0, :], dtype=np.float32)
            m["wvalid"] = np.ones((TT, 1), dtype=np.float32)
        in_maps.append(m)
    return in_maps


def kernel(**inputs):
    nc = build_program()
    in_maps = make_in_maps(inputs)
    res = run_bass_kernel_spmd(nc, in_maps, core_ids=list(range(NCORES)))
    out = np.empty((2, 16384, D), dtype=np.float32)
    for c in range(NCORES):
        bi, qi = divmod(c, 4)
        t0 = qi * TC
        out[bi, t0:t0 + TC, :] = res.results[c]["out"]
    return out


# version 25
# speedup vs baseline: 1.0529x; 1.0395x over previous
import numpy as np
from contextlib import ExitStack

import concourse.bass as bass
import concourse.mybir as mybir
from concourse.bass_utils import run_bass_kernel_spmd

F32 = mybir.dt.float32
BF16 = mybir.dt.bfloat16
I32 = mybir.dt.int32
AF = mybir.ActivationFunctionType
ALU = mybir.AluOpType
AX = mybir.AxisListType

D = 2048
H = 8192
RK = 16
TT = 128
TBT = 4
TB = TT * TBT
NBLK = 8
TC = NBLK * TB
ND = D // TT
NH = H // TT
NEC = D // 512
NHC = H // 512
EPS = 1e-5
WRING = 8
TKR = 4
NCORES = 8

ENG_SEM = {"t": "st", "v": "sv", "s": "ss"}


class Prog:

    ENGS = ("t", "v", "s", "g", "y")

    def __init__(self):
        self.q = {e: [] for e in self.ENGS}
        semnames = (["st", "sv", "ss", "sg"] + ["sy%d" % i for i in range(8)]
                    + ["sgd%d" % i for i in range(6)])
        self.semnames = semnames
        self.cnt = {s: 0 for s in semnames}
        self.floor = {e: {} for e in self.ENGS}
        self.lw = {}
        self.lr = {}
        self.rr = {"y": 0, "g": 0}

    def emit(self, eng, fn, r=(), w=(), dma=False):
        if dma:
            if eng == "y":
                self.rr["y"] += 1
                sem = "sy%d" % (self.rr["y"] % 8)
            elif eng == "g":
                self.rr["g"] += 1
                sem = "sgd%d" % (self.rr["g"] % 6)
            else:
                raise ValueError(eng)
            inc = 16
        elif eng == "g":
            sem, inc = "sg", 1
        else:
            sem, inc = ENG_SEM[eng], 1
        waits = {}

        def need(s, c):
            if c > waits.get(s, 0):
                waits[s] = c

        if dma and self.cnt[sem] > 0:
            need(sem, self.cnt[sem])
        for k in r:
            x = self.lw.get(k)
            if x:
                need(*x)
        for k in w:
            x = self.lw.get(k)
            if x:
                need(*x)
            for s2, c2 in self.lr.get(k, {}).items():
                need(s2, c2)
        fl = self.floor[eng]
        waits = {s2: c2 for s2, c2 in waits.items() if c2 > fl.get(s2, 0)}
        for s2, c2 in waits.items():
            fl[s2] = c2
        self.cnt[sem] += inc
        cafter = self.cnt[sem]
        self.q[eng].append((waits, fn, sem, inc))
        for k in w:
            self.lw[k] = (sem, cafter)
            self.lr[k] = {}
        for k in r:
            self.lr.setdefault(k, {})[sem] = cafter
        return sem, cafter

    def wait_all(self, eng, sems):
        self.q[eng].append((dict(sems), None, None, 0))


def build_program(nblocks=NBLK, debug=False):
    nc = bass.Bass("TRN2", target_bir_lowering=False)
    p = Prog()

    xin = nc.dram_tensor("x", [TC, D], F32, kind="ExternalInput").ap()
    xprev = nc.dram_tensor("xprev", [1, D], F32, kind="ExternalInput").ap()
    wk_in = nc.dram_tensor("Wk", [D, D], F32, kind="ExternalInput").ap()
    wv_in = nc.dram_tensor("Wv", [D, D], F32, kind="ExternalInput").ap()
    wr_in = nc.dram_tensor("Wr", [D, D], F32, kind="ExternalInput").ap()
    wo_in = nc.dram_tensor("Wo", [D, D], F32, kind="ExternalInput").ap()
    td_in = nc.dram_tensor("td", [D], F32, kind="ExternalInput").ap()
    tf_in = nc.dram_tensor("tf", [D], F32, kind="ExternalInput").ap()
    tmk_in = nc.dram_tensor("tmk", [D], F32, kind="ExternalInput").ap()
    tmv_in = nc.dram_tensor("tmv", [D], F32, kind="ExternalInput").ap()
    tmr_in = nc.dram_tensor("tmr", [D], F32, kind="ExternalInput").ap()
    wupq_in = nc.dram_tensor("wupq", [H, D], I32, kind="ExternalInput").ap()
    wups_in = nc.dram_tensor("wups", [H, D // 64], F32, kind="ExternalInput").ap()
    bup_in = nc.dram_tensor("bup", [H], F32, kind="ExternalInput").ap()
    upla_in = nc.dram_tensor("upla", [D, RK], F32, kind="ExternalInput").ap()
    uplb_in = nc.dram_tensor("uplb", [RK, H], F32, kind="ExternalInput").ap()
    wdnq_in = nc.dram_tensor("wdnq", [D, H], I32, kind="ExternalInput").ap()
    wdns_in = nc.dram_tensor("wdns", [D, H // 64], F32, kind="ExternalInput").ap()
    bdn_in = nc.dram_tensor("bdn", [D], F32, kind="ExternalInput").ap()
    dnla_in = nc.dram_tensor("dnla", [H, RK], F32, kind="ExternalInput").ap()
    dnlb_in = nc.dram_tensor("dnlb", [RK, D], F32, kind="ExternalInput").ap()
    cid_in = nc.dram_tensor("c_id", [TT, TT], F32, kind="ExternalInput").ap()
    cones_in = nc.dram_tensor("c_ones", [1, 512], F32, kind="ExternalInput").ap()
    xwarm_in = nc.dram_tensor("xwarm", [TT, D], F32, kind="ExternalInput").ap()
    wvalid_in = nc.dram_tensor("wvalid", [TT, 1], F32, kind="ExternalInput").ap()
    out = nc.dram_tensor("out", [TC, D], F32, kind="ExternalOutput").ap()

    wkt = nc.dram_tensor("wkt", [ND, D, TT], BF16, kind="Internal").ap()
    wvt = nc.dram_tensor("wvt", [ND, D, TT], BF16, kind="Internal").ap()
    wrt = nc.dram_tensor("wrt", [ND, D, TT], BF16, kind="Internal").ap()
    wot = nc.dram_tensor("wot", [D, D], BF16, kind="Internal").ap()
    wut = nc.dram_tensor("wut", [NH, D, TT], BF16, kind="Internal").ap()
    wdt = nc.dram_tensor("wdt", [H, D], BF16, kind="Internal").ap()
    bupxd = nc.dram_tensor("bupxd", [RK + 1, H], BF16, kind="Internal").ap()
    bdnxd = nc.dram_tensor("bdnxd", [RK + 1, D], BF16, kind="Internal").ap()
    onesd = nc.dram_tensor("onesd", [1, 512], BF16, kind="Internal").ap()

    alloc = []
    cur = [0]

    def sb(name, shape, dtype):
        nbytes = int(np.prod(shape[1:])) * (2 if dtype in (BF16,) else 4)
        cur[0] += (nbytes + 31) // 32 * 32
        t = nc.alloc_sbuf_tensor(name, shape, dtype)
        alloc.append((name, nbytes))
        return t.ap(), 0

    X, x_off = sb("Xar", [TT, TBT * D], F32)
    OA, o_off = sb("Oar", [TT, 16 * 512], F32)
    VR, vr_off = sb("VRar", [TT, 32 * 512], BF16)
    XN, _ = sb("XNar", [TT, ND * 513], BF16)
    MX, _ = sb("MXar", [TT, ND * 512], BF16)
    XT, _ = sb("xn_tok", [TT, D], F32)
    SCAN_BUFS = ["sc_p", "sc_q", "sc_pv", "sc_sa", "sc_sb", "sc_wkv", "sc_num", "sc_pad"]
    SCANT, _ = sb("scant", [TT, 8 * 512], F32)
    SC = {nm: SCANT[:, i * 512:(i + 1) * 512] for i, nm in enumerate(SCAN_BUFS)}
    SCBF = SCANT.bitcast(BF16)
    DIFF, _ = sb("diff", [TT, 512], BF16)
    TK = [sb("tok%d" % i, [TT, 512], F32)[0] for i in range(TKR)]
    WH = [sb("wh%d" % i, [TT, 4 * 512], BF16)[0] for i in range(8)]
    BUPS = [sb("bupx%d" % i, [RK + 1, 512], BF16)[0] for i in range(2)]
    BDNX, _ = sb("bdnx", [RK + 1, D], BF16)
    BUPC, _ = sb("bupc", [TT, NH], F32)
    AUPB, _ = sb("aupb", [TT, ND * RK], BF16)
    ADNB, _ = sb("adnb", [TT, NH * RK], BF16)
    L1E, _ = sb("l1e", [RK + 1, 512], BF16)
    L2E, _ = sb("l2e", [RK + 1, 512], BF16)
    IDF, _ = sb("identf", [TT, TT], F32)
    LAM, _ = sb("lamcm", [TT, ND], F32)
    UCM, _ = sb("ucm", [TT, ND], F32)
    TMK, _ = sb("tmkcm", [TT, ND], F32)
    TMV, _ = sb("tmvcm", [TT, ND], F32)
    TMR, _ = sb("tmrcm", [TT, ND], F32)
    BCOL, _ = sb("bcol", [TT, ND], BF16)
    CA, _ = sb("carrya", [TT, ND], F32)
    CB, _ = sb("carryb", [TT, ND], F32)
    NEG1, _ = sb("neg1", [TT, 1], F32)
    NEGH, _ = sb("negh", [TT, 1], F32)
    VALID, _ = sb("valid", [TT, 1], F32)
    SSUM, _ = sb("ssum0", [TT, 8], F32)
    SSQ, _ = sb("ssq0", [TT, 8], F32)
    SMEAN, _ = sb("smean0", [TT, 8], F32)
    SMSQ, _ = sb("smsq0", [TT, 8], F32)
    SVAR, _ = sb("svar0", [TT, 8], F32)
    SSTD, _ = sb("sstd0", [TT, 8], F32)
    SRSTD, _ = sb("srstd0", [TT, 8], F32)
    PSC = [sb("pscale%d" % i, [TT, 32], F32)[0] for i in range(2)]
    ONEB, _ = sb("onebf", [1, 512], BF16)
    PIN_I = [X[:, 0:D].bitcast(I32), X[:, D:2 * D].bitcast(I32)]
    PIN_F = [X[:, 0:D], X[:, D:2 * D]]
    PKEY_IN = [["X0", "X1"], ["X2", "X3"]]
    SEXP = [X[:, 2 * D:3 * D], X[:, 3 * D:4 * D]]
    PKEY_SEXP = [["X4", "X5"], ["X6", "X7"]]
    DQF = [OA[:, 0:D], OA[:, D:2 * D]]
    PKEY_DQF = [["O0", "O1", "O2", "O3"], ["O4", "O5", "O6", "O7"]]
    TBUF = [OA[:, 2 * D:2 * D + 1024].bitcast(BF16), OA[:, 2 * D + 1024:3 * D].bitcast(BF16)]
    PKEY_TB = [["O8", "O9"], ["O10", "O11"]]

    def xkeys(t):
        return ["X%d" % (2 * t), "X%d" % (2 * t + 1)]

    def okeys(slot):
        return "O%d" % slot

    def xnt(s):
        return XN[:, s * 513 + 1:s * 513 + 513]

    def xnt_ext(s):
        return XN[:, s * 513:s * 513 + 512]

    def x2tile(j):
        if j < 32:
            return VR[:, j * 512:(j + 1) * 512], ["VR%d" % j]
        if j < 48:
            m = j - 32
            return MX[:, m * 512:(m + 1) * 512], ["MX%d" % m]
        m = j - 48
        smap = ["sc_p", "sc_p", "sc_q", "sc_q", "sc_pv", "sc_pv", "sc_sa",
                "sc_sa", "sc_sb", "sc_sb", "sc_wkv", "sc_wkv", "sc_num",
                "sc_num", "sc_pad", "sc_pad"]
        return SCBF[:, m * 512:(m + 1) * 512], [smap[m]]

    psum_ctx = ExitStack()
    PB2 = [psum_ctx.enter_context(nc.psum_tensor("pbw%d" % i, [TT, 1024], F32)).ap()
           for i in range(4)]
    PB = [PB2[i // 2][:, (i % 2) * 512:(i % 2 + 1) * 512] for i in range(8)]
    LP = PB[7][0:RK, :]

    tbank = [0]

    def nexttr():
        b = tbank[0] % 8
        tbank[0] += 1
        return PB[b], "PB%d" % b

    def ln_stats3(src_ap, src_keys, junk_ap, junk_keys, c=0):
        ks = "%d" % c
        p.emit("s", lambda e, a=src_ap, j=junk_ap, cc=c: e.activation(
            out=j, in_=a, func=AF.Square, accum_out=SSQ[:, cc:cc + 1]),
            r=src_keys, w=list(junk_keys) + ["ssq" + ks])
        p.emit("v", lambda e, cc=c: e.tensor_scalar(
            out=SVAR[:, cc:cc + 1], in0=SSQ[:, cc:cc + 1], scalar1=1.0 / D, scalar2=EPS,
            op0=ALU.mult, op1=ALU.add), r=["ssq" + ks], w=["svar" + ks])
        p.emit("v", lambda e, cc=c: e.scalar_tensor_tensor(
            SMSQ[:, cc:cc + 1], SMEAN[:, cc:cc + 1], 1.0, SMEAN[:, cc:cc + 1],
            ALU.mult, ALU.mult), r=["smean" + ks], w=["smsq" + ks])
        p.emit("v", lambda e, cc=c: e.tensor_sub(
            SVAR[:, cc:cc + 1], SVAR[:, cc:cc + 1], SMSQ[:, cc:cc + 1]),
            r=["smsq" + ks], w=["svar" + ks])
        p.emit("s", lambda e, cc=c: e.activation(
            out=SSTD[:, cc:cc + 1], in_=SVAR[:, cc:cc + 1], func=AF.Ln),
            r=["svar" + ks], w=["sstd" + ks])
        p.emit("s", lambda e, cc=c: e.activation(
            out=SRSTD[:, cc:cc + 1], in_=SSTD[:, cc:cc + 1],
            func=AF.Exp, scale=NEGH[:, 0:1]),
            r=["sstd" + ks, "negh"], w=["srstd" + ks])

    def ln_stats(src_ap, src_keys, junk_ap, junk_keys, c=0):
        ks = "%d" % c
        p.emit("s", lambda e, a=src_ap, j=junk_ap, cc=c: e.activation(
            out=j, in_=a, func=AF.Identity, accum_out=SSUM[:, cc:cc + 1]),
            r=src_keys, w=list(junk_keys) + ["ssum" + ks])
        p.emit("s", lambda e, a=src_ap, j=junk_ap, cc=c: e.activation(
            out=j, in_=a, func=AF.Square, accum_out=SSQ[:, cc:cc + 1]),
            r=src_keys, w=list(junk_keys) + ["ssq" + ks])
        p.emit("v", lambda e, cc=c: e.tensor_scalar_mul(
            SMEAN[:, cc:cc + 1], SSUM[:, cc:cc + 1], 1.0 / D),
            r=["ssum" + ks], w=["smean" + ks])
        p.emit("v", lambda e, cc=c: e.scalar_tensor_tensor(
            SMSQ[:, cc:cc + 1], SSUM[:, cc:cc + 1], 1.0 / (D * D), SSUM[:, cc:cc + 1],
            ALU.mult, ALU.mult), r=["ssum" + ks], w=["smsq" + ks])
        p.emit("v", lambda e, cc=c: e.tensor_scalar(
            out=SVAR[:, cc:cc + 1], in0=SSQ[:, cc:cc + 1], scalar1=1.0 / D, scalar2=EPS,
            op0=ALU.mult, op1=ALU.add), r=["ssq" + ks], w=["svar" + ks])
        p.emit("v", lambda e, cc=c: e.tensor_sub(
            SVAR[:, cc:cc + 1], SVAR[:, cc:cc + 1], SMSQ[:, cc:cc + 1]),
            r=["smsq" + ks], w=["svar" + ks])
        p.emit("s", lambda e, cc=c: e.activation(
            out=SSTD[:, cc:cc + 1], in_=SVAR[:, cc:cc + 1], func=AF.Ln),
            r=["svar" + ks], w=["sstd" + ks])
        p.emit("s", lambda e, cc=c: e.activation(
            out=SRSTD[:, cc:cc + 1], in_=SSTD[:, cc:cc + 1],
            func=AF.Exp, scale=NEGH[:, 0:1]),
            r=["sstd" + ks, "negh"], w=["srstd" + ks])

    def transpose_group(src_ap, src_keys, n, dst_ap, dst_keys, dst_eng="v"):
        trap, trkey = nexttr()
        def tfn(e, a=src_ap, nn=n, bk=trap):
            last = None
            for j in range(nn):
                last = e.transpose(bk[:, j * TT:(j + 1) * TT],
                                   a[:, j * TT:(j + 1) * TT], IDF)
            return last
        p.emit("t", tfn, r=list(src_keys) + ["identf"], w=[trkey])
        p.emit(dst_eng, lambda e, bk=trap, d=dst_ap, nn=n: e.tensor_copy(
            d, bk[:, 0:nn * TT].rearrange("p (j c) -> p j c", j=nn)),
            r=[trkey], w=dst_keys)

    p.emit("y", lambda e: e.dma_start(out=IDF, in_=cid_in), w=["identf"], dma=True)
    p.emit("v", lambda e: e.memset(NEG1[:, 0:1], -1.0), w=["neg1"])
    p.emit("v", lambda e: e.memset(NEGH[:, 0:1], -0.5), w=["negh"])
    p.emit("v", lambda e: e.memset(CA, 0.0), w=["carrya"])
    p.emit("v", lambda e: e.memset(CB, 0.0), w=["carryb"])

    p.emit("y", lambda e: e.dma_start(out=TK[0][:1, 0:512], in_=cones_in),
           w=["TK0"], dma=True)
    p.emit("s", lambda e: e.copy(out=ONEB, in_=TK[0][:1, 0:512]),
           r=["TK0"], w=["onebf"])
    p.emit("g", lambda e: e.dma_start(out=onesd, in_=ONEB), r=["onebf"],
           w=["onesd"], dma=True)
    p.emit("y", lambda e: e.dma_start(out=L1E[RK:RK + 1, :], in_=onesd),
           r=["onesd"], w=["l1ones"], dma=True)
    p.emit("y", lambda e: e.dma_start(out=L2E[RK:RK + 1, :], in_=onesd),
           r=["onesd"], w=["l2ones"], dma=True)

    def cm_load(src, dst_ap, dst_key, post=None):
        tmp = TK[1][0:ND, 0:TT]
        p.emit("y", lambda e, s=src: e.dma_start(
            out=tmp, in_=s.rearrange("(s q) -> s q", q=TT)), w=["TK1"], dma=True)
        trap, trkey = nexttr()
        p.emit("t", lambda e, bk=trap: e.transpose(
            bk[:, 0:ND], tmp, IDF[0:ND, 0:ND]),
            r=["TK1", "identf"], w=[trkey])
        p.emit("v", lambda e, bk=trap, d=dst_ap: e.tensor_copy(d, bk[:, 0:ND]),
               r=[trkey], w=[dst_key])
        if post:
            post(dst_ap, dst_key)

    cm_load(td_in, LAM, "lamcm")
    p.emit("s", lambda e: e.activation(out=LAM, in_=LAM, func=AF.Exp),
           r=["lamcm"], w=["lamcm"])
    p.emit("s", lambda e: e.activation(out=LAM, in_=LAM, func=AF.Exp,
                                       scale=NEG1[:, 0:1]),
           r=["lamcm", "neg1"], w=["lamcm"])
    cm_load(tf_in, UCM, "ucm")
    p.emit("s", lambda e: e.activation(out=UCM, in_=UCM, func=AF.Exp),
           r=["ucm"], w=["ucm"])
    cm_load(tmk_in, TMK, "tmkcm")
    cm_load(tmv_in, TMV, "tmvcm")
    cm_load(tmr_in, TMR, "tmrcm")

    def lora_a(src, n, dst_ap, dst_key):
        step = 512 // RK
        for c0 in range(0, n, step):
            nn = min(step, n - c0)
            tmp = TK[2][:, 0:nn * RK]
            p.emit("y", lambda e, s=src, c=c0, nn=nn, t=tmp: e.dma_start(
                out=t.rearrange("p (s r) -> p s r", s=nn),
                in_=s.rearrange("(s p) r -> p s r", p=TT)[:, c:c + nn, :]),
                w=["TK2"], dma=True)
            p.emit("s", lambda e, d=dst_ap, c=c0, nn=nn, t=tmp: e.copy(
                out=d[:, c * RK:(c + nn) * RK], in_=t),
                r=["TK2"], w=[dst_key])

    lora_a(upla_in, ND, AUPB, "aupb")
    lora_a(dnla_in, NH, ADNB, "adnb")

    def lora_b(srcB, srcb, n, dram):
        for c in range(n // 512):
            tmpf = TK[2][0:RK, 0:512]
            tmpb = TK[3][0:RK, 0:256].bitcast(BF16)
            tmpb1 = TK[3][0:1, 256:512].bitcast(BF16)
            p.emit("y", lambda e, s=srcB, cc=c: e.dma_start(
                out=tmpf, in_=s[:, cc * 512:(cc + 1) * 512]), w=["TK2"], dma=True)
            p.emit("s", lambda e: e.copy(out=tmpb, in_=tmpf),
                   r=["TK2"], w=["TK3"])
            p.emit("y", lambda e, s=srcb, cc=c: e.dma_start(
                out=tmpf[0:1, :], in_=s[cc * 512:(cc + 1) * 512].rearrange("(a q) -> a q", a=1)),
                r=["TK3"], w=["TK2"], dma=True)
            p.emit("s", lambda e: e.copy(out=tmpb1, in_=tmpf[0:1, :]),
                   r=["TK2"], w=["TK3b"])
            p.emit("g", lambda e, dd=dram, cc=c: e.dma_start(
                out=dd[0:RK, cc * 512:(cc + 1) * 512], in_=tmpb),
                r=["TK3"], w=["lorabd"], dma=True)
            p.emit("g", lambda e, dd=dram, cc=c: e.dma_start(
                out=dd[RK:RK + 1, cc * 512:(cc + 1) * 512], in_=tmpb1),
                r=["TK3b"], w=["lorabd"], dma=True)

    lora_b(uplb_in, bup_in, H, bupxd)
    lora_b(dnlb_in, bdn_in, D, bdnxd)
    p.emit("y", lambda e: e.dma_start(out=BDNX, in_=bdnxd), r=["lorabd"],
           w=["bdnx"], dma=True)

    p.emit("v", lambda e: e.memset(X[:, 0:D], 0.0), w=xkeys(0))
    p.emit("y", lambda e: e.dma_start(out=X[0:1, 0:D], in_=xprev),
           r=xkeys(0), w=xkeys(0), dma=True)
    xp = X[0:1, 0:D]
    p.emit("v", lambda e: e.reduce_sum(SSUM[0:1, 0:1], xp, axis=AX.X),
           r=xkeys(0), w=["ssum0"])
    p.emit("s", lambda e: e.activation(out=XT[0:1, :], in_=xp, func=AF.Square,
                                       accum_out=SSQ[0:1, 0:1]),
           r=xkeys(0), w=["xn_tok", "ssq0"])
    p.emit("v", lambda e: e.tensor_scalar_mul(SMEAN[0:1, 0:1], SSUM[0:1, 0:1], 1.0 / D),
           r=["ssum0"], w=["smean0"])
    p.emit("v", lambda e: e.scalar_tensor_tensor(
        SMSQ[0:1, 0:1], SSUM[0:1, 0:1], 1.0 / (D * D), SSUM[0:1, 0:1],
        ALU.mult, ALU.mult), r=["ssum0"], w=["smsq0"])
    p.emit("v", lambda e: e.tensor_scalar(
        out=SVAR[0:1, 0:1], in0=SSQ[0:1, 0:1], scalar1=1.0 / D, scalar2=EPS,
        op0=ALU.mult, op1=ALU.add), r=["ssq0"], w=["svar0"])
    p.emit("v", lambda e: e.tensor_sub(SVAR[0:1, 0:1], SVAR[0:1, 0:1], SMSQ[0:1, 0:1]),
           r=["smsq0"], w=["svar0"])
    p.emit("s", lambda e: e.activation(out=SSTD[0:1, 0:1], in_=SVAR[0:1, 0:1],
                                       func=AF.Ln), r=["svar0"], w=["sstd0"])
    p.emit("s", lambda e: e.activation(out=SRSTD[0:1, 0:1], in_=SSTD[0:1, 0:1],
                                       func=AF.Exp, scale=NEGH[0:1, 0:1]),
           r=["sstd0", "negh"], w=["srstd0"])
    p.emit("v", lambda e: e.tensor_scalar(
        out=xp, in0=xp, scalar1=SMEAN[0:1, 0:1], scalar2=SRSTD[0:1, 0:1],
        op0=ALU.subtract, op1=ALU.mult),
        r=xkeys(0) + ["smean0", "srstd0"], w=xkeys(0))
    for g in range(4):
        trap, trkey = nexttr()

        def tfn(e, gg=g, bk=trap):
            last = None
            for j in range(4):
                s = gg * 4 + j
                last = e.transpose(bk[:, j * TT:(j + 1) * TT],
                                   X[:, s * TT:(s + 1) * TT], IDF)
            return last
        p.emit("t", tfn, r=xkeys(0) + ["identf"], w=[trkey])
        p.emit("v", lambda e, gg=g, bk=trap: e.tensor_copy(
            BCOL[:, gg * 4:gg * 4 + 4],
            bk[:, 0:4 * TT].rearrange("p (j c) -> p j c", j=4)[:, :, 0]),
            r=[trkey], w=["bcol"])


    prep_units = []
    prep_it = [0]

    def prep_one(qsrc, ssrc, rr, cc, dst_rearr, wkey):
        i = prep_it[0]
        prep_it[0] += 1
        slot = i % 2
        ink = PKEY_IN[slot]
        if ssrc is None:
            dq = PIN_F[slot]
            dqk = ink
            p.emit("y", lambda e, s=qsrc, r_=rr, c_=cc, b=dq: e.dma_start(
                out=b, in_=s[r_ * TT:(r_ + 1) * TT, c_ * D:(c_ + 1) * D]),
                w=ink, dma=True)
        else:
            inb = PIN_I[slot]
            sx = SEXP[slot]
            sxk = PKEY_SEXP[slot]
            dq = DQF[slot]
            dqk = PKEY_DQF[slot]
            psc = PSC[slot]
            psck = "pscale%d" % slot
            p.emit("y", lambda e, s=qsrc, r_=rr, c_=cc, b=inb: e.dma_start(
                out=b, in_=s[r_ * TT:(r_ + 1) * TT, c_ * D:(c_ + 1) * D]),
                w=ink, dma=True)
            p.emit("y", lambda e, s=ssrc, r_=rr, c_=cc, b=psc: e.dma_start(
                out=b, in_=s[r_ * TT:(r_ + 1) * TT, c_ * 32:(c_ + 1) * 32]),
                w=[psck], dma=True)
            p.emit("s", lambda e, b=psc, d=sx: e.copy(
                out=d.rearrange("p (a b) -> p a b", a=32),
                in_=b.unsqueeze(2).broadcast_to([TT, 32, 64])),
                r=[psck], w=sxk)
            p.emit("v", lambda e, b=inb, s2=sx, d=dq: e.scalar_tensor_tensor(
                d, b, -7.5, s2, ALU.add, ALU.mult),
                r=ink + sxk, w=dqk)
        tslot = i % 2
        tb = TBUF[tslot]
        tbk = PKEY_TB[tslot]
        for g in range(2):
            wb = PB2[(tbank[0] + g) % 4]
            wkeys = ["PB%d" % (((tbank[0] + g) % 4) * 2),
                     "PB%d" % (((tbank[0] + g) % 4) * 2 + 1)]

            def tfn(e, gg=g, bk=wb, d=dq):
                last = None
                for j in range(8):
                    s = gg * 8 + j
                    last = e.transpose(bk[:, j * TT:(j + 1) * TT],
                                       d[:, s * TT:(s + 1) * TT], IDF)
                return last
            p.emit("t", tfn, r=dqk + ["identf"], w=wkeys)
            if g % 2 == 0:
                p.emit("v", lambda e, gg=g, bk=wb, t=tb: e.tensor_copy(
                    t.rearrange("p (s c) -> p s c", s=ND)[:, gg * 8:gg * 8 + 8, :],
                    bk[:, 0:8 * TT].rearrange("p (j c) -> p j c", j=8)),
                    r=wkeys, w=tbk)
            else:
                p.emit("s", lambda e, gg=g, bk=wb, t=tb: e.copy(
                    out=t.rearrange("p (s c) -> p s c", s=ND)[:, gg * 8:gg * 8 + 8, :],
                    in_=bk[:, 0:8 * TT].rearrange("p (j c) -> p j c", j=8)),
                    r=wkeys, w=tbk)
        tbank[0] += 2
        p.emit("g", lambda e, r_=rr, c_=cc, t=tb: e.dma_start(
            out=dst_rearr(r_, c_), in_=t.rearrange("p (s c) -> p s c", s=ND)),
            r=tbk, w=[wkey], dma=True)

    def prep_quant(qsrc, ssrc, dram, n_row_tiles, col_tiles_per_row, dst_rearr, wkey):
        for rr in range(n_row_tiles):
            for cc in range(col_tiles_per_row):
                prep_units.append((qsrc, ssrc, rr, cc, dst_rearr, wkey))

    for w_in_, w_t_, wkey_ in [(wk_in, wkt, "wbf_k"), (wv_in, wvt, "wbf_v"),
                               (wr_in, wrt, "wbf_r")]:
        def att_dst(rr, cc, wt2=w_t_):
            return wt2[rr].rearrange("(s q) ee -> q s ee", q=TT)

        prep_quant(w_in_, None, w_t_, ND, 1, att_dst, wkey_)
    wro_ = wot.rearrange("(s q) ee -> q s ee", q=TT)

    def att_dst_o(rr, cc):
        return wro_[:, :, rr * TT:(rr + 1) * TT]

    prep_quant(wo_in, None, wot, ND, 1, att_dst_o, "wbf_o")

    def wut_dst(rr, cc):
        return wut[rr].rearrange("(s q) hh -> q s hh", q=TT)

    prep_quant(wupq_in, wups_in, wut, NH, 1, wut_dst, "wutd")
    p.emit("y", lambda e: e.dma_start(
        out=TK[1][0:NH, 0:TT], in_=bup_in.rearrange("(s q) -> s q", q=TT)),
        w=["TK1"], dma=True)
    trapb, trkeyb = nexttr()
    p.emit("t", lambda e, bk=trapb: e.transpose(
        bk[:, 0:NH], TK[1][0:NH, 0:TT], IDF[0:NH, 0:NH]),
        r=["TK1", "identf"], w=[trkeyb])
    p.emit("v", lambda e, bk=trapb: e.tensor_copy(BUPC, bk[:, 0:NH]),
           r=[trkeyb], w=["bupc"])
    wdt_r = wdt.rearrange("(s q) ee -> q s ee", q=TT)

    def wdt_dst(rr, cc):
        return wdt_r[:, cc * ND:(cc + 1) * ND, rr * TT:(rr + 1) * TT]

    prep_quant(wdnq_in, wdns_in, wdt, ND, NEC, wdt_dst, "wdtd")

    attn_u = [u for u in prep_units if u[1] is None]
    quant_u = [u for u in prep_units if u[1] is not None]
    ai, qi2 = 0, 0
    while ai < len(attn_u) or qi2 < len(quant_u):
        if ai < len(attn_u):
            prep_one(*attn_u[ai]); ai += 1
        for _ in range(2):
            if qi2 < len(quant_u):
                prep_one(*quant_u[qi2]); qi2 += 1
    prep_units.clear()




    wring = [0]

    def whalf():
        s = wring[0] % 8
        wring[0] += 1
        return WH[s], "WH%d" % s

    tkr = [0]

    def tkslot():
        s = tkr[0] % TKR
        tkr[0] += 1
        return s

    def proj_tokmajor(lhs_fn, lhs_keys_fn, wsrc3d_fn, n_s, banks,
                      extra_units=(), wkey_src=()):
        nx = len(extra_units)
        nq = n_s // 4
        bufs = {}
        qeng = [0]

        def emit_load(hh):
            buf, bkey = whalf()
            bufs[hh * 4] = (buf, bkey)
            p.emit("y" if hh % 2 == 0 else "g",
                   lambda e, a=buf, s_=hh * 4: e.dma_start(
                out=a.rearrange("p (s c) -> p s c", s=4), in_=wsrc3d_fn(s_)),
                r=list(wkey_src), w=[bkey], dma=True)

        for hh in range(min(3, nq)):
            emit_load(hh)
        for s in range(n_s):
            buf, bkey = bufs[(s // 4) * 4]
            wap = buf[:, (s % 4) * 512:(s % 4 + 1) * 512]
            first, last = (s == 0), (s == n_s - 1 and nx == 0)
            rl = [bkey] + lhs_keys_fn(s)
            wl = []
            if first or last:
                wl = ["PB%d" % b for b in banks]

            def mfn(e, ss=s, a=wap, ff=first, ll=last):
                li = None
                for t in range(TBT):
                    li = e.matmul(PB[banks[t]][:], lhs_fn(ss, t), a,
                                  start=ff, stop=ll)
                return li
            p.emit("t", mfn, r=rl, w=wl)
            if s % 4 == 3 and s // 4 + 3 < nq:
                emit_load(s // 4 + 3)
        for i, (xfn, xreads) in enumerate(extra_units):
            ll = (i == nx - 1)
            wl = ["PB%d" % b for b in banks] if ll else []

            def mfn(e, f=xfn, ll2=ll):
                li = None
                for t in range(TBT):
                    li = f(e, t, False, ll2)
                return li
            p.emit("t", mfn, r=list(xreads), w=wl)

    p.emit("y", lambda e: e.dma_start(out=VALID, in_=wvalid_in), w=["valid"], dma=True)
    p.emit("y", lambda e: e.dma_start(out=X[:, 0:D], in_=xwarm_in),
           r=xkeys(0), w=xkeys(0), dma=True)
    ln_stats(X[:, 0:D], xkeys(0), XT, ["xn_tok"], c=0)
    p.emit("v", lambda e: e.tensor_scalar(
        out=XT, in0=X[:, 0:D], scalar1=SMEAN[:, 0:1], scalar2=SRSTD[:, 0:1],
        op0=ALU.subtract, op1=ALU.mult),
        r=xkeys(0) + ["smean0", "srstd0"], w=["xn_tok"])
    for g in range(4):
        dst = XN.rearrange("p (s c) -> p s c", s=ND, c=513)[:, g * 4:g * 4 + 4, 1:1 + TT]
        transpose_group(XT[:, g * 512:(g + 1) * 512], ["xn_tok"], 4,
                        dst, ["XN%d" % (g * 4 + j) for j in range(4)])

    def warm_proj(wsrc, tmap, tkey, dst_off_fn, wbkey):
        for s in range(ND):
            xs = XN[:, s * 513 + 1:s * 513 + 1 + TT]
            p.emit("v", lambda e, a=xs: e.tensor_sub(
                DIFF[:, 1:TT], a[:, 1:TT], a[:, 0:TT - 1]),
                r=["XN%d" % s], w=["diff"])
            p.emit("v", lambda e, a=xs, ss=s, tm=tmap: e.scalar_tensor_tensor(
                MX[:, ss * 512 + 1:ss * 512 + TT], DIFF[:, 1:TT],
                tm[:, ss:ss + 1], a[:, 0:TT - 1], ALU.mult, ALU.add),
                r=["diff", tkey, "XN%d" % s], w=["MX%d" % s])
            p.emit("v", lambda e, a=xs, ss=s, tm=tmap: e.tensor_scalar_mul(
                MX[:, ss * 512:ss * 512 + 1], a[:, 0:1], tm[:, ss:ss + 1]),
                r=[tkey, "XN%d" % s], w=["MX%d" % s])
        for oc in range(ND):
            buf, bkey2 = whalf()
            p.emit("y", lambda e, a=buf, o2=oc, wb=wsrc: e.dma_start(
                out=a[:, 0:ND * TT].rearrange("p (s c) -> p s c", s=ND),
                in_=wb[o2].rearrange("(s q) ee -> q s ee", q=TT)),
                r=[wbkey], w=[bkey2], dma=True)
            bank = oc % 8

            def mfn(e, a=buf, bk=bank):
                li = None
                for s in range(ND):
                    li = e.matmul(PB[bk][:, 0:TT], a[:, s * TT:(s + 1) * TT],
                                  MX[:, s * 512:s * 512 + TT],
                                  start=(s == 0), stop=(s == ND - 1))
                return li
            p.emit("t", mfn,
                   r=[bkey2] + ["MX%d" % s for s in range(ND)],
                   w=["PB%d" % bank])
            dap, dkeys = dst_off_fn(oc)
            p.emit("v", lambda e, bk=bank, d=dap: e.tensor_copy(d, PB[bk][:, 0:TT]),
                   r=["PB%d" % bank], w=dkeys)

    warm_proj(wkt, TMK, "tmkcm",
              lambda oc: (VR[:, (16 + oc) * 512:(16 + oc) * 512 + TT],
                          ["VR%d" % (16 + oc)]), "wbf_k")
    warm_proj(wvt, TMV, "tmvcm",
              lambda oc: (VR[:, oc * 512:oc * 512 + TT], ["VR%d" % oc]), "wbf_v")
    for s in range(ND):
        kap = VR[:, (16 + s) * 512:(16 + s) * 512 + TT]
        vap = VR[:, s * 512:s * 512 + TT]
        p.emit("s", lambda e, a=kap: e.activation(out=SC["sc_p"][:, 0:TT], in_=a,
                                                  func=AF.Exp),
               r=["VR%d" % (16 + s)], w=["sc_p"])
        p.emit("v", lambda e: e.tensor_scalar_mul(
            SC["sc_p"][:, 0:TT], SC["sc_p"][:, 0:TT], VALID[:, 0:1]),
            r=["sc_p", "valid"], w=["sc_p"])
        p.emit("v", lambda e, a=vap: e.tensor_mul(
            SC["sc_pv"][:, 0:TT], SC["sc_p"][:, 0:TT], a),
            r=["sc_p", "VR%d" % s], w=["sc_pv"])
        p.emit("v", lambda e, ss=s: e.tensor_tensor_scan(
            SC["sc_sa"][:, 0:TT], LAM[:, ss:ss + 1].broadcast_to([TT, TT]),
            SC["sc_pv"][:, 0:TT], 0.0, ALU.mult, ALU.add),
            r=["sc_pv", "lamcm"], w=["sc_sa"])
        p.emit("v", lambda e, ss=s: e.tensor_tensor_scan(
            SC["sc_sb"][:, 0:TT], LAM[:, ss:ss + 1].broadcast_to([TT, TT]),
            SC["sc_p"][:, 0:TT], 0.0, ALU.mult, ALU.add),
            r=["sc_p", "lamcm"], w=["sc_sb"])
        p.emit("v", lambda e, ss=s: e.tensor_copy(
            CA[:, ss:ss + 1], SC["sc_sa"][:, TT - 1:TT]),
            r=["sc_sa"], w=["carrya"])
        p.emit("v", lambda e, ss=s: e.tensor_copy(
            CB[:, ss:ss + 1], SC["sc_sb"][:, TT - 1:TT]),
            r=["sc_sb"], w=["carryb"])

    def xload(b, t):
        p.emit("y", lambda e, bb=b, tt_=t: e.dma_start(
            out=X[:, tt_ * D:(tt_ + 1) * D],
            in_=xin[(bb * TBT + tt_) * TT:(bb * TBT + tt_ + 1) * TT, :]),
            w=xkeys(t), dma=True)

    def ln1(t):
        xap = X[:, t * D:(t + 1) * D]
        ln_stats(xap, xkeys(t), XT, ["xn_tok"], c=t)
        p.emit("v", lambda e, a=xap, cc=t: e.tensor_scalar(
            out=XT, in0=a, scalar1=SMEAN[:, cc:cc + 1], scalar2=SRSTD[:, cc:cc + 1],
            op0=ALU.subtract, op1=ALU.mult),
            r=xkeys(t) + ["smean%d" % t, "srstd%d" % t], w=["xn_tok"])
        for g in range(4):
            dst = XN.rearrange("p (s c) -> p s c", s=ND, c=513)[:, g * 4:g * 4 + 4,
                                                               1 + t * TT:1 + (t + 1) * TT]
            transpose_group(XT[:, g * 512:(g + 1) * 512], ["xn_tok"], 4,
                            dst, ["XN%d" % (g * 4 + j) for j in range(4)])

    for t in range(TBT):
        xload(0, t)

    for b in range(nblocks):
        for t in range(TBT):
            ln1(t)

        def emit_mix_one(s, tmap, tkey, reg, kfn):
                dst = reg[:, s * 512:(s + 1) * 512]
                p.emit("v", lambda e, ss=s: e.tensor_sub(
                    DIFF, xnt(ss), xnt_ext(ss)),
                    r=["XN%d" % s], w=["diff"])
                p.emit("v", lambda e, ss=s, tm=tmap, d=dst: e.scalar_tensor_tensor(
                    d, DIFF, tm[:, ss:ss + 1], xnt_ext(ss), ALU.mult, ALU.add),
                    r=["diff", tkey, "XN%d" % s], w=[kfn(s)])

        def emit_mix(tmap, tkey, reg, kfn):
            for s in range(ND):
                emit_mix_one(s, tmap, tkey, reg, kfn)

        def proj_chanmajor(wsrcb, n_oc, rhs_fn, rhs_keys_sub, dst_fn, wkey_src,
                           evac=None, post_oc=None):
            for oc in range(n_oc):
                buf, bkey2 = whalf()
                p.emit("y" if oc % 2 == 0 else "g",
                       lambda e, a=buf, o2=oc: e.dma_start(
                    out=a[:, 0:ND * TT].rearrange("p (s c) -> p s c", s=ND),
                    in_=wsrcb[o2].rearrange("(s q) cc -> q s cc", q=TT)),
                    r=list(wkey_src), w=[bkey2], dma=True)
                bank = oc % 8
                for s0 in range(0, ND, 4):
                    last_sub = (s0 == ND - 4)

                    def mfn(e, a=buf, bk=bank, s_=s0):
                        li = None
                        for s in range(s_, s_ + 4):
                            li = e.matmul(PB[bk][:], a[:, s * TT:(s + 1) * TT],
                                          rhs_fn(s), start=(s == 0),
                                          stop=(s == ND - 1))
                        return li
                    wl = ["PB%d" % bank] if (s0 == 0 or last_sub) else []
                    p.emit("t", mfn, r=[bkey2] + rhs_keys_sub(s0), w=wl)
                evac(oc, bank)
                if post_oc is not None:
                    post_oc(oc)

        def kvr_proj(wsrcb, rhs_reg, rhs_kfn, dst_fn, wbkey, post_oc=None):

            def evac(oc, bank):
                dap, dkeys = dst_fn(oc)
                if oc % 2 == 0:
                    p.emit("v", lambda e, bk=bank, d=dap: e.tensor_copy(d, PB[bk][:]),
                           r=["PB%d" % bank], w=dkeys)
                else:
                    p.emit("s", lambda e, bk=bank, d=dap: e.copy(out=d, in_=PB[bk][:]),
                           r=["PB%d" % bank], w=dkeys)

            proj_chanmajor(wsrcb, ND,
                           lambda s: rhs_reg[:, s * 512:(s + 1) * 512],
                           lambda s0: [rhs_kfn(s) for s in range(s0, s0 + 4)],
                           dst_fn, [wbkey], evac=evac, post_oc=post_oc)

        def scan_part1(s):
            kap = OA[:, s * 512:(s + 1) * 512]
            vap = VR[:, s * 512:(s + 1) * 512]
            kk = [okeys(s)]
            p.emit("s", lambda e, a=kap: e.activation(out=SC["sc_p"], in_=a,
                                                      func=AF.Exp),
                   r=kk, w=["sc_p"])
            p.emit("v", lambda e, a=vap: e.tensor_mul(SC["sc_pv"], SC["sc_p"], a),
                   r=["sc_p", "VR%d" % s], w=["sc_pv"])
            p.emit("v", lambda e, ss=s: e.tensor_tensor_scan(
                SC["sc_sa"], LAM[:, ss:ss + 1].broadcast_to([TT, 512]),
                SC["sc_pv"], CA[:, ss:ss + 1], ALU.mult, ALU.add),
                r=["sc_pv", "lamcm", "carrya"], w=["sc_sa"])
            p.emit("v", lambda e, ss=s: e.tensor_tensor_scan(
                SC["sc_sb"], LAM[:, ss:ss + 1].broadcast_to([TT, 512]),
                SC["sc_p"], CB[:, ss:ss + 1], ALU.mult, ALU.add),
                r=["sc_p", "lamcm", "carryb"], w=["sc_sb"])
            p.emit("v", lambda e, a=vap, ss=s: e.scalar_tensor_tensor(
                SC["sc_wkv"], SC["sc_p"], UCM[:, ss:ss + 1], a,
                ALU.mult, ALU.mult),
                r=["sc_p", "ucm", "VR%d" % s], w=["sc_wkv"])
            p.emit("v", lambda e, ss=s: e.tensor_add(
                SC["sc_num"][:, 0:1], CA[:, ss:ss + 1], SC["sc_wkv"][:, 0:1]),
                r=["carrya", "sc_wkv"], w=["sc_num"])
            p.emit("v", lambda e: e.tensor_add(
                SC["sc_num"][:, 1:512], SC["sc_sa"][:, 0:511],
                SC["sc_wkv"][:, 1:512]), r=["sc_sa", "sc_wkv"], w=["sc_num"])
            p.emit("v", lambda e, ss=s: e.scalar_tensor_tensor(
                SC["sc_q"][:, 0:1], SC["sc_p"][:, 0:1], UCM[:, ss:ss + 1],
                CB[:, ss:ss + 1], ALU.mult, ALU.add),
                r=["carryb", "sc_p", "ucm"], w=["sc_q"])
            p.emit("v", lambda e, ss=s: e.scalar_tensor_tensor(
                SC["sc_q"][:, 1:512], SC["sc_p"][:, 1:512], UCM[:, ss:ss + 1],
                SC["sc_sb"][:, 0:511], ALU.mult, ALU.add),
                r=["sc_sb", "sc_p", "ucm"], w=["sc_q"])
            p.emit("v", lambda e, ss=s: e.tensor_copy(CA[:, ss:ss + 1],
                                                      SC["sc_sa"][:, 511:512]),
                   r=["sc_sa"], w=["carrya"])
            p.emit("v", lambda e, ss=s: e.tensor_copy(CB[:, ss:ss + 1],
                                                      SC["sc_sb"][:, 511:512]),
                   r=["sc_sb"], w=["carryb"])
            p.emit("s", lambda e: e.activation(out=SC["sc_pv"], in_=SC["sc_q"],
                                               func=AF.Ln),
                   r=["sc_q"], w=["sc_pv"])
            p.emit("s", lambda e: e.activation(out=SC["sc_pv"], in_=SC["sc_pv"],
                                               func=AF.Exp, scale=NEG1[:, 0:1]),
                   r=["sc_pv", "neg1"], w=["sc_pv"])
            p.emit("v", lambda e, ss=s: e.tensor_mul(
                OA[:, ss * 512:(ss + 1) * 512], SC["sc_num"], SC["sc_pv"]),
                r=["sc_num", "sc_pv"], w=[okeys(s)])

        def rproj_hook(oc):
            if oc % 4 != 3:
                return
            ec2 = oc // 4
            p.emit("s", lambda e, e2=ec2: e.activation(
                out=VR[:, (16 + e2 * 4) * 512:(16 + e2 * 4 + 4) * 512],
                in_=VR[:, (16 + e2 * 4) * 512:(16 + e2 * 4 + 4) * 512],
                func=AF.Sigmoid),
                r=["VR%d" % (16 + ec2 * 4 + j) for j in range(4)],
                w=["VR%d" % (16 + ec2 * 4 + j) for j in range(4)])
            for j in range(4):
                s = ec2 * 4 + j
                p.emit("v", lambda e, ss=s: e.tensor_mul(
                    VR[:, ss * 512:(ss + 1) * 512],
                    OA[:, ss * 512:(ss + 1) * 512],
                    VR[:, (16 + ss) * 512:(16 + ss + 1) * 512]),
                    r=[okeys(s), "VR%d" % (16 + s)], w=["VR%d" % s])

        p.emit("v", lambda e: e.tensor_copy(
            XN.rearrange("p (s c) -> p s c", s=ND, c=513)[:, :, 0:1], BCOL.unsqueeze(2)),
            r=["bcol"], w=["XN%d" % s for s in range(ND)])
        VR16 = VR[:, 16 * 512:32 * 512]
        kmx = lambda s: "MX%d" % s
        kvr16 = lambda s: "VR%d" % (16 + s)
        emit_mix(TMK, "tmkcm", MX, kmx)
        kvr_proj(wkt, MX, kmx,
                 lambda oc: (OA[:, oc * 512:(oc + 1) * 512], [okeys(oc)]), "wbf_k",
                 post_oc=lambda oc: emit_mix_one(oc, TMV, "tmvcm", VR16, kvr16))
        def vproj_post(oc):
            emit_mix_one(oc, TMR, "tmrcm", MX, kmx)
            scan_part1(oc)

        kvr_proj(wvt, VR16, kvr16,
                 lambda oc: (VR[:, oc * 512:(oc + 1) * 512], ["VR%d" % oc]), "wbf_v",
                 post_oc=vproj_post)
        kvr_proj(wrt, MX, kmx,
                 lambda oc: (VR[:, (16 + oc) * 512:(16 + oc + 1) * 512],
                             ["VR%d" % (16 + oc)]), "wbf_r",
                 post_oc=rproj_hook)

        p.emit("v", lambda e: e.tensor_copy(
            BCOL.unsqueeze(2),
            XN.rearrange("p (s c) -> p s c", s=ND, c=513)[:, :, 512:513]),
            r=["XN%d" % s for s in range(ND)], w=["bcol"])

        wot_r = wot.rearrange("(s q) ee -> q s ee", q=TT)
        for ec in range(NEC):
            banks = [0, 1, 2, 3] if ec % 2 == 0 else [4, 5, 6, 7]

            def wsrc3d(s0, e2=ec):
                return wot_r[:, s0:s0 + 4, e2 * 512:(e2 + 1) * 512]

            proj_tokmajor(
                lambda s, t: VR[:, s * 512 + t * TT:s * 512 + (t + 1) * TT],
                lambda s: ["VR%d" % s], wsrc3d, ND, banks,
                wkey_src=["wbf_o"])
            for t in range(TBT):
                p.emit("s", lambda e, bk=banks[t], tt_=t, e2=ec: e.copy(
                    out=OA[:, (tt_ * 4 + e2) * 512:(tt_ * 4 + e2 + 1) * 512],
                    in_=PB[bk][:]),
                    r=["PB%d" % banks[t]], w=[okeys(t * 4 + ec)])

        for t in range(TBT):
            oap = OA[:, t * 4 * 512:(t + 1) * 4 * 512]
            okl = [okeys(t * 4 + j) for j in range(4)]
            ln_stats(oap, okl, XT, ["xn_tok"], c=t + 4)
            p.emit("v", lambda e, a=oap, cc=t + 4: e.tensor_scalar(
                out=a, in0=a, scalar1=SMEAN[:, cc:cc + 1], scalar2=SRSTD[:, cc:cc + 1],
                op0=ALU.subtract, op1=ALU.mult),
                r=okl + ["smean%d" % (t + 4), "srstd%d" % (t + 4)], w=okl)
            p.emit("v", lambda e, a=oap, tt_=t: e.tensor_add(
                X[:, tt_ * D:(tt_ + 1) * D], X[:, tt_ * D:(tt_ + 1) * D], a),
                r=okl + xkeys(t), w=xkeys(t))

        for t in range(TBT):
            xap = X[:, t * D:(t + 1) * D]
            ln_stats3(xap, xkeys(t), XT, ["xn_tok"], c=t)
            p.emit("v", lambda e, a=xap, cc=t: e.tensor_scalar(
                out=XT, in0=a, scalar1=SMEAN[:, cc:cc + 1], scalar2=SRSTD[:, cc:cc + 1],
                op0=ALU.subtract, op1=ALU.mult),
                r=xkeys(t) + ["smean%d" % t, "srstd%d" % t], w=["xn_tok"])
            for g in range(4):
                dst = XN.rearrange("p (s c) -> p s c", s=ND, c=513)[:, g * 4:g * 4 + 4,
                                                                   1 + t * TT:1 + (t + 1) * TT]
                transpose_group(XT[:, g * 512:(g + 1) * 512], ["xn_tok"], 4,
                                dst, ["XN%d" % (g * 4 + j) for j in range(4)])

        def l1fn(e):
            li = None
            for s in range(ND):
                li = e.matmul(LP[:],
                              AUPB[:, s * RK:(s + 1) * RK],
                              xnt(s), start=(s == 0), stop=(s == ND - 1))
            return li
        p.emit("t", l1fn, r=["aupb"] + ["XN%d" % s for s in range(ND)], w=["PB7"])
        p.emit("v", lambda e: e.tensor_copy(L1E[0:RK, :], LP[:]),
               r=["PB7"], w=["l1e"])

        for hc4 in range(NHC):
            bslot = hc4 % 2
            bkey = "BUP%d" % bslot
            p.emit("y", lambda e, h=hc4, sl=bslot: e.dma_start(
                out=BUPS[sl], in_=bupxd[:, h * 512:(h + 1) * 512]),
                r=["lorabd"], w=[bkey], dma=True)
            for j4 in range(4):
                oc = hc4 * 4 + j4
                buf, bkey2 = whalf()
                p.emit("y" if oc % 2 == 0 else "g",
                       lambda e, a=buf, o2=oc: e.dma_start(
                    out=a[:, 0:ND * TT].rearrange("p (s c) -> p s c", s=ND),
                    in_=wut[o2].rearrange("(s q) hh -> q s hh", q=TT)),
                    r=["wutd"], w=[bkey2], dma=True)
                bank = oc % 8
                for s0 in range(0, ND, 4):
                    last_sub = (s0 == ND - 4)

                    def mfn(e, a=buf, bk=bank, s_=s0, ls=last_sub, o2=oc, sl=bslot):
                        li = None
                        for s in range(s_, s_ + 4):
                            li = e.matmul(PB[bk][:], a[:, s * TT:(s + 1) * TT],
                                          xnt(s), start=(s == 0), stop=False)
                        if ls:
                            li = e.matmul(
                                PB[bk][:], BUPS[sl][0:RK, (o2 % 4) * TT:(o2 % 4 + 1) * TT],
                                L1E[0:RK, :], start=False, stop=True)
                        return li
                    rl = [bkey2] + ["XN%d" % s for s in range(s0, s0 + 4)]
                    if last_sub:
                        rl += [bkey, "l1e"]
                    wl = ["PB%d" % bank] if (s0 == 0 or last_sub) else []
                    p.emit("t", mfn, r=rl, w=wl)
                dap, dkeys = x2tile(oc)
                p.emit("s", lambda e, bk=bank, d=dap, o2=oc: e.activation(
                    out=d, in_=PB[bk][:], func=AF.Relu,
                    bias=BUPC[:, o2:o2 + 1]),
                    r=["PB%d" % bank, "bupc"], w=dkeys)

        x2keys_all = []
        for j in range(NH):
            x2keys_all += x2tile(j)[1]
        x2keys_all = list(set(x2keys_all))

        def l2fn(e):
            li = None
            for s in range(NH):
                li = e.matmul(LP[:], ADNB[:, s * RK:(s + 1) * RK],
                              x2tile(s)[0], start=(s == 0), stop=(s == NH - 1))
            return li
        p.emit("t", l2fn, r=["adnb"] + x2keys_all, w=["PB7"])
        p.emit("v", lambda e: e.tensor_copy(L2E[0:RK, :], LP[:]),
               r=["PB7"], w=["l2e"])

        wdt_r3 = wdt.rearrange("(s q) ee -> q s ee", q=TT)
        for ec in range(NEC):
            banks = [0, 1, 2, 3] if ec % 2 == 0 else [4, 5, 6, 7]

            def wsrc3d(s0, e2=ec):
                return wdt_r3[:, s0:s0 + 4, e2 * 512:(e2 + 1) * 512]

            def lorafn(e, t, start, stop, bb=tuple(banks), e2=ec):
                return e.matmul(PB[bb[t]][:], L2E[:, t * TT:(t + 1) * TT],
                                BDNX[:, e2 * 512:(e2 + 1) * 512],
                                start=start, stop=stop)

            proj_tokmajor(
                lambda s, t: x2tile(s)[0][:, t * TT:(t + 1) * TT],
                lambda s: x2tile(s)[1], wsrc3d, NH, banks,
                extra_units=[(lorafn, ["l2e", "l2ones", "bdnx"])],
                wkey_src=["wdtd"])
            for t in range(TBT):
                p.emit("v", lambda e, bk=banks[t], tt_=t, e2=ec: e.tensor_copy(
                    OA[:, (tt_ * 4 + e2) * 512:(tt_ * 4 + e2 + 1) * 512],
                    PB[bk][:]),
                    r=["PB%d" % banks[t]], w=[okeys(t * 4 + ec)])

        for t in range(TBT):
            yap = OA[:, t * 4 * 512:(t + 1) * 4 * 512]
            okl = [okeys(t * 4 + j) for j in range(4)]
            ln_stats(yap, okl, XT, ["xn_tok"], c=t + 4)
            p.emit("v", lambda e, a=yap, cc=t + 4: e.tensor_scalar(
                out=a, in0=a, scalar1=SMEAN[:, cc:cc + 1], scalar2=SRSTD[:, cc:cc + 1],
                op0=ALU.subtract, op1=ALU.mult),
                r=okl + ["smean%d" % (t + 4), "srstd%d" % (t + 4)], w=okl)
            p.emit("v", lambda e, a=yap, tt_=t: e.tensor_add(
                a, X[:, tt_ * D:(tt_ + 1) * D], a),
                r=okl + xkeys(t), w=okl)
            p.emit("g", lambda e, bb=b, tt_=t, a=yap: e.dma_start(
                out=out[(bb * TBT + tt_) * TT:(bb * TBT + tt_ + 1) * TT, :],
                in_=a),
                r=okl, w=["outw"], dma=True)
            if b + 1 < nblocks:
                xload(b + 1, t)

    p.wait_all("g", {s: p.cnt[s] for s in ["sgd%d" % i for i in range(6)] if p.cnt[s] > 0})

    with ExitStack() as ctx:
        sems = {}
        for s in p.semnames:
            sems[s] = ctx.enter_context(nc.semaphore(s))
        block = ctx.enter_context(nc.Block())
        eng_holder = {}

        @block.sync
        def _(sync):
            eng_holder["y"] = sync
            for waits, fn, sem, inc in p.q["y"]:
                for s2, c2 in waits.items():
                    sync.wait_ge(sems[s2], c2)
                if fn is not None:
                    inst = fn(sync)
                    if inc:
                        inst.then_inc(sems[sem], inc)

        @block.tensor
        def _(tensor):
            for waits, fn, sem, inc in p.q["t"]:
                for s2, c2 in waits.items():
                    tensor.wait_ge(sems[s2], c2)
                if fn is not None:
                    inst = fn(tensor)
                    if inc:
                        inst.then_inc(sems[sem], inc)

        @block.vector
        def _(vector):
            for waits, fn, sem, inc in p.q["v"]:
                for s2, c2 in waits.items():
                    vector.wait_ge(sems[s2], c2)
                if fn is not None:
                    inst = fn(vector)
                    if inc:
                        inst.then_inc(sems[sem], inc)

        @block.scalar
        def _(scalar):
            for waits, fn, sem, inc in p.q["s"]:
                for s2, c2 in waits.items():
                    scalar.wait_ge(sems[s2], c2)
                if fn is not None:
                    inst = fn(scalar)
                    if inc:
                        inst.then_inc(sems[sem], inc)

        @block.gpsimd
        def _(gpsimd):
            for waits, fn, sem, inc in p.q["g"]:
                for s2, c2 in waits.items():
                    gpsimd.wait_ge(sems[s2], c2)
                if fn is not None:
                    inst = fn(gpsimd)
                    if inc:
                        inst.then_inc(sems[sem], inc)

    psum_ctx.close()
    return nc


def make_in_maps(inputs, nblocks=NBLK):
    x = np.asarray(inputs["L0_attn_x"], dtype=np.float32)
    base = {
        "Wk": np.ascontiguousarray(inputs["L0_attn_Wk"], dtype=np.float32),
        "Wv": np.ascontiguousarray(inputs["L0_attn_Wv"], dtype=np.float32),
        "Wr": np.ascontiguousarray(inputs["L0_attn_Wr"], dtype=np.float32),
        "Wo": np.ascontiguousarray(inputs["L0_attn_Wo"], dtype=np.float32),
        "td": np.ascontiguousarray(inputs["L0_attn_time_decay"], dtype=np.float32),
        "tf": np.ascontiguousarray(inputs["L0_attn_time_first"], dtype=np.float32),
        "tmk": np.ascontiguousarray(np.reshape(inputs["L0_attn_time_mix_k"], [D]), dtype=np.float32),
        "tmv": np.ascontiguousarray(np.reshape(inputs["L0_attn_time_mix_v"], [D]), dtype=np.float32),
        "tmr": np.ascontiguousarray(np.reshape(inputs["L0_attn_time_mix_r"], [D]), dtype=np.float32),
        "wupq": np.ascontiguousarray(inputs["L0_ffn_w_up_q"], dtype=np.int32),
        "wups": np.ascontiguousarray(inputs["L0_ffn_w_up_scale"], dtype=np.float32),
        "bup": np.ascontiguousarray(inputs["L0_ffn_b_up"], dtype=np.float32),
        "upla": np.ascontiguousarray(inputs["L0_ffn_w_up_lora_a"], dtype=np.float32),
        "uplb": np.ascontiguousarray(inputs["L0_ffn_w_up_lora_b"], dtype=np.float32),
        "wdnq": np.ascontiguousarray(inputs["L0_ffn_w_down_q"], dtype=np.int32),
        "wdns": np.ascontiguousarray(inputs["L0_ffn_w_down_scale"], dtype=np.float32),
        "bdn": np.ascontiguousarray(inputs["L0_ffn_b_down"], dtype=np.float32),
        "dnla": np.ascontiguousarray(inputs["L0_ffn_w_down_lora_a"], dtype=np.float32),
        "dnlb": np.ascontiguousarray(inputs["L0_ffn_w_down_lora_b"], dtype=np.float32),
        "c_id": np.eye(TT, dtype=np.float32),
        "c_ones": np.ones((1, 512), dtype=np.float32),
    }
    in_maps = []
    for c in range(NCORES):
        bi, qi = divmod(c, 4)
        t0 = qi * TC
        m = dict(base)
        m["x"] = np.ascontiguousarray(x[bi, t0:t0 + TC, :], dtype=np.float32)
        if qi == 0:
            m["xprev"] = np.zeros((1, D), dtype=np.float32)
            m["xwarm"] = np.zeros((TT, D), dtype=np.float32)
            m["wvalid"] = np.zeros((TT, 1), dtype=np.float32)
        else:
            m["xprev"] = np.ascontiguousarray(x[bi, t0 - 1:t0, :], dtype=np.float32)
            m["xwarm"] = np.ascontiguousarray(x[bi, t0 - TT:t0, :], dtype=np.float32)
            m["wvalid"] = np.ones((TT, 1), dtype=np.float32)
        in_maps.append(m)
    return in_maps


def kernel(**inputs):
    nc = build_program()
    in_maps = make_in_maps(inputs)
    res = run_bass_kernel_spmd(nc, in_maps, core_ids=list(range(NCORES)))
    out = np.empty((2, 16384, D), dtype=np.float32)
    for c in range(NCORES):
        bi, qi = divmod(c, 4)
        t0 = qi * TC
        out[bi, t0:t0 + TC, :] = res.results[c]["out"]
    return out


# version 26 (final)
# speedup vs baseline: 1.0614x; 1.0081x over previous
import numpy as np
from contextlib import ExitStack

import concourse.bass as bass
import concourse.mybir as mybir
from concourse.bass_utils import run_bass_kernel_spmd

F32 = mybir.dt.float32
BF16 = mybir.dt.bfloat16
I32 = mybir.dt.int32
AF = mybir.ActivationFunctionType
ALU = mybir.AluOpType
AX = mybir.AxisListType

D = 2048
H = 8192
RK = 16
TT = 128
TBT = 4
TB = TT * TBT
NBLK = 8
TC = NBLK * TB
ND = D // TT
NH = H // TT
NEC = D // 512
NHC = H // 512
EPS = 1e-5
WRING = 8
TKR = 4
NCORES = 8

ENG_SEM = {"t": "st", "v": "sv", "s": "ss"}


class Prog:

    ENGS = ("t", "v", "s", "g", "y")

    def __init__(self):
        self.q = {e: [] for e in self.ENGS}
        semnames = (["st", "sv", "ss", "sg"] + ["sy%d" % i for i in range(8)]
                    + ["sgd%d" % i for i in range(6)])
        self.semnames = semnames
        self.cnt = {s: 0 for s in semnames}
        self.floor = {e: {} for e in self.ENGS}
        self.lw = {}
        self.lr = {}
        self.rr = {"y": 0, "g": 0}

    def emit(self, eng, fn, r=(), w=(), dma=False):
        if dma:
            if eng == "y":
                self.rr["y"] += 1
                sem = "sy%d" % (self.rr["y"] % 8)
            elif eng == "g":
                self.rr["g"] += 1
                sem = "sgd%d" % (self.rr["g"] % 6)
            else:
                raise ValueError(eng)
            inc = 16
        elif eng == "g":
            sem, inc = "sg", 1
        else:
            sem, inc = ENG_SEM[eng], 1
        waits = {}

        def need(s, c):
            if c > waits.get(s, 0):
                waits[s] = c

        if dma and self.cnt[sem] > 0:
            need(sem, self.cnt[sem])
        for k in r:
            x = self.lw.get(k)
            if x:
                need(*x)
        for k in w:
            x = self.lw.get(k)
            if x:
                need(*x)
            for s2, c2 in self.lr.get(k, {}).items():
                need(s2, c2)
        fl = self.floor[eng]
        waits = {s2: c2 for s2, c2 in waits.items() if c2 > fl.get(s2, 0)}
        for s2, c2 in waits.items():
            fl[s2] = c2
        self.cnt[sem] += inc
        cafter = self.cnt[sem]
        self.q[eng].append((waits, fn, sem, inc))
        for k in w:
            self.lw[k] = (sem, cafter)
            self.lr[k] = {}
        for k in r:
            self.lr.setdefault(k, {})[sem] = cafter
        return sem, cafter

    def wait_all(self, eng, sems):
        self.q[eng].append((dict(sems), None, None, 0))


def build_program(nblocks=NBLK, debug=False):
    nc = bass.Bass("TRN2", target_bir_lowering=False)
    p = Prog()

    xin = nc.dram_tensor("x", [TC, D], F32, kind="ExternalInput").ap()
    xprev = nc.dram_tensor("xprev", [1, D], F32, kind="ExternalInput").ap()
    wk_in = nc.dram_tensor("Wk", [D, D], F32, kind="ExternalInput").ap()
    wv_in = nc.dram_tensor("Wv", [D, D], F32, kind="ExternalInput").ap()
    wr_in = nc.dram_tensor("Wr", [D, D], F32, kind="ExternalInput").ap()
    wo_in = nc.dram_tensor("Wo", [D, D], F32, kind="ExternalInput").ap()
    td_in = nc.dram_tensor("td", [D], F32, kind="ExternalInput").ap()
    tf_in = nc.dram_tensor("tf", [D], F32, kind="ExternalInput").ap()
    tmk_in = nc.dram_tensor("tmk", [D], F32, kind="ExternalInput").ap()
    tmv_in = nc.dram_tensor("tmv", [D], F32, kind="ExternalInput").ap()
    tmr_in = nc.dram_tensor("tmr", [D], F32, kind="ExternalInput").ap()
    wupq_in = nc.dram_tensor("wupq", [H, D], I32, kind="ExternalInput").ap()
    wups_in = nc.dram_tensor("wups", [H, D // 64], F32, kind="ExternalInput").ap()
    bup_in = nc.dram_tensor("bup", [H], F32, kind="ExternalInput").ap()
    upla_in = nc.dram_tensor("upla", [D, RK], F32, kind="ExternalInput").ap()
    uplb_in = nc.dram_tensor("uplb", [RK, H], F32, kind="ExternalInput").ap()
    wdnq_in = nc.dram_tensor("wdnq", [D, H], I32, kind="ExternalInput").ap()
    wdns_in = nc.dram_tensor("wdns", [D, H // 64], F32, kind="ExternalInput").ap()
    bdn_in = nc.dram_tensor("bdn", [D], F32, kind="ExternalInput").ap()
    dnla_in = nc.dram_tensor("dnla", [H, RK], F32, kind="ExternalInput").ap()
    dnlb_in = nc.dram_tensor("dnlb", [RK, D], F32, kind="ExternalInput").ap()
    cid_in = nc.dram_tensor("c_id", [TT, TT], F32, kind="ExternalInput").ap()
    cones_in = nc.dram_tensor("c_ones", [1, 512], F32, kind="ExternalInput").ap()
    xwarm_in = nc.dram_tensor("xwarm", [TT, D], F32, kind="ExternalInput").ap()
    wvalid_in = nc.dram_tensor("wvalid", [TT, 1], F32, kind="ExternalInput").ap()
    out = nc.dram_tensor("out", [TC, D], F32, kind="ExternalOutput").ap()

    wkt = nc.dram_tensor("wkt", [ND, D, TT], BF16, kind="Internal").ap()
    wvt = nc.dram_tensor("wvt", [ND, D, TT], BF16, kind="Internal").ap()
    wrt = nc.dram_tensor("wrt", [ND, D, TT], BF16, kind="Internal").ap()
    wot = nc.dram_tensor("wot", [D, D], BF16, kind="Internal").ap()
    wut = nc.dram_tensor("wut", [NH, D, TT], BF16, kind="Internal").ap()
    wdt = nc.dram_tensor("wdt", [H, D], BF16, kind="Internal").ap()
    bupxd = nc.dram_tensor("bupxd", [RK + 1, H], BF16, kind="Internal").ap()
    bdnxd = nc.dram_tensor("bdnxd", [RK + 1, D], BF16, kind="Internal").ap()
    onesd = nc.dram_tensor("onesd", [1, 512], BF16, kind="Internal").ap()

    alloc = []
    cur = [0]

    def sb(name, shape, dtype):
        nbytes = int(np.prod(shape[1:])) * (2 if dtype in (BF16,) else 4)
        cur[0] += (nbytes + 31) // 32 * 32
        t = nc.alloc_sbuf_tensor(name, shape, dtype)
        alloc.append((name, nbytes))
        return t.ap(), 0

    X, x_off = sb("Xar", [TT, TBT * D], F32)
    OA, o_off = sb("Oar", [TT, 16 * 512], F32)
    VR, vr_off = sb("VRar", [TT, 32 * 512], BF16)
    XN, _ = sb("XNar", [TT, ND * 513], BF16)
    MX, _ = sb("MXar", [TT, ND * 512], BF16)
    XT, _ = sb("xn_tok", [TT, D], F32)
    SCAN_BUFS = ["sc_p", "sc_q", "sc_pv", "sc_sa", "sc_sb", "sc_wkv", "sc_num", "sc_pad"]
    SCANT, _ = sb("scant", [TT, 8 * 512], F32)
    SC = {nm: SCANT[:, i * 512:(i + 1) * 512] for i, nm in enumerate(SCAN_BUFS)}
    SCBF = SCANT.bitcast(BF16)
    DIFF, _ = sb("diff", [TT, 512], BF16)
    TK = [sb("tok%d" % i, [TT, 512], F32)[0] for i in range(TKR)]
    WH = [sb("wh%d" % i, [TT, 4 * 512], BF16)[0] for i in range(8)]
    BUPS = [sb("bupx%d" % i, [RK + 1, 512], BF16)[0] for i in range(2)]
    BDNX, _ = sb("bdnx", [RK + 1, D], BF16)
    BUPC, _ = sb("bupc", [TT, NH], F32)
    AUPB, _ = sb("aupb", [TT, ND * RK], BF16)
    ADNB, _ = sb("adnb", [TT, NH * RK], BF16)
    L1E, _ = sb("l1e", [RK + 1, 512], BF16)
    L2E, _ = sb("l2e", [RK + 1, 512], BF16)
    IDF, _ = sb("identf", [TT, TT], F32)
    LAM, _ = sb("lamcm", [TT, ND], F32)
    UCM, _ = sb("ucm", [TT, ND], F32)
    TMK, _ = sb("tmkcm", [TT, ND], F32)
    TMV, _ = sb("tmvcm", [TT, ND], F32)
    TMR, _ = sb("tmrcm", [TT, ND], F32)
    BCOL, _ = sb("bcol", [TT, ND], BF16)
    CA, _ = sb("carrya", [TT, ND], F32)
    CB, _ = sb("carryb", [TT, ND], F32)
    NEG1, _ = sb("neg1", [TT, 1], F32)
    NEGH, _ = sb("negh", [TT, 1], F32)
    VALID, _ = sb("valid", [TT, 1], F32)
    SSUM, _ = sb("ssum0", [TT, 8], F32)
    SSQ, _ = sb("ssq0", [TT, 8], F32)
    SMEAN, _ = sb("smean0", [TT, 8], F32)
    SMSQ, _ = sb("smsq0", [TT, 8], F32)
    SVAR, _ = sb("svar0", [TT, 8], F32)
    SSTD, _ = sb("sstd0", [TT, 8], F32)
    SRSTD, _ = sb("srstd0", [TT, 8], F32)
    PSC = [sb("pscale%d" % i, [TT, 32], F32)[0] for i in range(2)]
    ONEB, _ = sb("onebf", [1, 512], BF16)
    PIN_I = [X[:, 0:D].bitcast(I32), X[:, D:2 * D].bitcast(I32)]
    PIN_F = [X[:, 0:D], X[:, D:2 * D]]
    PKEY_IN = [["X0", "X1"], ["X2", "X3"]]
    SEXP = [X[:, 2 * D:3 * D], X[:, 3 * D:4 * D]]
    PKEY_SEXP = [["X4", "X5"], ["X6", "X7"]]
    DQF = [OA[:, 0:D], OA[:, D:2 * D]]
    PKEY_DQF = [["O0", "O1", "O2", "O3"], ["O4", "O5", "O6", "O7"]]
    TBUF = [OA[:, 2 * D:2 * D + 1024].bitcast(BF16), OA[:, 2 * D + 1024:3 * D].bitcast(BF16)]
    PKEY_TB = [["O8", "O9"], ["O10", "O11"]]

    def xkeys(t):
        return ["X%d" % (2 * t), "X%d" % (2 * t + 1)]

    def okeys(slot):
        return "O%d" % slot

    def xnt(s):
        return XN[:, s * 513 + 1:s * 513 + 513]

    def xnt_ext(s):
        return XN[:, s * 513:s * 513 + 512]

    def x2tile(j):
        if j < 32:
            return VR[:, j * 512:(j + 1) * 512], ["VR%d" % j]
        if j < 48:
            m = j - 32
            return MX[:, m * 512:(m + 1) * 512], ["MX%d" % m]
        m = j - 48
        smap = ["sc_p", "sc_p", "sc_q", "sc_q", "sc_pv", "sc_pv", "sc_sa",
                "sc_sa", "sc_sb", "sc_sb", "sc_wkv", "sc_wkv", "sc_num",
                "sc_num", "sc_pad", "sc_pad"]
        return SCBF[:, m * 512:(m + 1) * 512], [smap[m]]

    psum_ctx = ExitStack()
    PB2 = [psum_ctx.enter_context(nc.psum_tensor("pbw%d" % i, [TT, 1024], F32)).ap()
           for i in range(4)]
    PB = [PB2[i // 2][:, (i % 2) * 512:(i % 2 + 1) * 512] for i in range(8)]
    LP = PB[7][0:RK, :]

    tbank = [0]

    def nexttr():
        b = tbank[0] % 8
        tbank[0] += 1
        return PB[b], "PB%d" % b

    def ln_stats3(src_ap, src_keys, junk_ap, junk_keys, c=0):
        ks = "%d" % c
        p.emit("s", lambda e, a=src_ap, j=junk_ap, cc=c: e.activation(
            out=j, in_=a, func=AF.Square, accum_out=SSQ[:, cc:cc + 1]),
            r=src_keys, w=list(junk_keys) + ["ssq" + ks])
        p.emit("v", lambda e, cc=c: e.tensor_scalar(
            out=SVAR[:, cc:cc + 1], in0=SSQ[:, cc:cc + 1], scalar1=1.0 / D, scalar2=EPS,
            op0=ALU.mult, op1=ALU.add), r=["ssq" + ks], w=["svar" + ks])
        p.emit("v", lambda e, cc=c: e.scalar_tensor_tensor(
            SMSQ[:, cc:cc + 1], SMEAN[:, cc:cc + 1], 1.0, SMEAN[:, cc:cc + 1],
            ALU.mult, ALU.mult), r=["smean" + ks], w=["smsq" + ks])
        p.emit("v", lambda e, cc=c: e.tensor_sub(
            SVAR[:, cc:cc + 1], SVAR[:, cc:cc + 1], SMSQ[:, cc:cc + 1]),
            r=["smsq" + ks], w=["svar" + ks])
        p.emit("s", lambda e, cc=c: e.activation(
            out=SSTD[:, cc:cc + 1], in_=SVAR[:, cc:cc + 1], func=AF.Ln),
            r=["svar" + ks], w=["sstd" + ks])
        p.emit("s", lambda e, cc=c: e.activation(
            out=SRSTD[:, cc:cc + 1], in_=SSTD[:, cc:cc + 1],
            func=AF.Exp, scale=NEGH[:, 0:1]),
            r=["sstd" + ks, "negh"], w=["srstd" + ks])

    def ln_stats(src_ap, src_keys, junk_ap, junk_keys, c=0):
        ks = "%d" % c
        p.emit("s", lambda e, a=src_ap, j=junk_ap, cc=c: e.activation(
            out=j, in_=a, func=AF.Identity, accum_out=SSUM[:, cc:cc + 1]),
            r=src_keys, w=list(junk_keys) + ["ssum" + ks])
        p.emit("s", lambda e, a=src_ap, j=junk_ap, cc=c: e.activation(
            out=j, in_=a, func=AF.Square, accum_out=SSQ[:, cc:cc + 1]),
            r=src_keys, w=list(junk_keys) + ["ssq" + ks])
        p.emit("v", lambda e, cc=c: e.tensor_scalar_mul(
            SMEAN[:, cc:cc + 1], SSUM[:, cc:cc + 1], 1.0 / D),
            r=["ssum" + ks], w=["smean" + ks])
        p.emit("v", lambda e, cc=c: e.scalar_tensor_tensor(
            SMSQ[:, cc:cc + 1], SSUM[:, cc:cc + 1], 1.0 / (D * D), SSUM[:, cc:cc + 1],
            ALU.mult, ALU.mult), r=["ssum" + ks], w=["smsq" + ks])
        p.emit("v", lambda e, cc=c: e.tensor_scalar(
            out=SVAR[:, cc:cc + 1], in0=SSQ[:, cc:cc + 1], scalar1=1.0 / D, scalar2=EPS,
            op0=ALU.mult, op1=ALU.add), r=["ssq" + ks], w=["svar" + ks])
        p.emit("v", lambda e, cc=c: e.tensor_sub(
            SVAR[:, cc:cc + 1], SVAR[:, cc:cc + 1], SMSQ[:, cc:cc + 1]),
            r=["smsq" + ks], w=["svar" + ks])
        p.emit("s", lambda e, cc=c: e.activation(
            out=SSTD[:, cc:cc + 1], in_=SVAR[:, cc:cc + 1], func=AF.Ln),
            r=["svar" + ks], w=["sstd" + ks])
        p.emit("s", lambda e, cc=c: e.activation(
            out=SRSTD[:, cc:cc + 1], in_=SSTD[:, cc:cc + 1],
            func=AF.Exp, scale=NEGH[:, 0:1]),
            r=["sstd" + ks, "negh"], w=["srstd" + ks])

    tg_alt = [0]

    def transpose_group(src_ap, src_keys, n, dst_ap, dst_keys):
        trap, trkey = nexttr()
        def tfn(e, a=src_ap, nn=n, bk=trap):
            last = None
            for j in range(nn):
                last = e.transpose(bk[:, j * TT:(j + 1) * TT],
                                   a[:, j * TT:(j + 1) * TT], IDF)
            return last
        p.emit("t", tfn, r=list(src_keys) + ["identf"], w=[trkey])
        tg_alt[0] += 1
        if tg_alt[0] % 2 == 0:
            p.emit("v", lambda e, bk=trap, d=dst_ap, nn=n: e.tensor_copy(
                d, bk[:, 0:nn * TT].rearrange("p (j c) -> p j c", j=nn)),
                r=[trkey], w=dst_keys)
        else:
            p.emit("s", lambda e, bk=trap, d=dst_ap, nn=n: e.copy(
                out=d, in_=bk[:, 0:nn * TT].rearrange("p (j c) -> p j c", j=nn)),
                r=[trkey], w=dst_keys)

    p.emit("y", lambda e: e.dma_start(out=IDF, in_=cid_in), w=["identf"], dma=True)
    p.emit("v", lambda e: e.memset(NEG1[:, 0:1], -1.0), w=["neg1"])
    p.emit("v", lambda e: e.memset(NEGH[:, 0:1], -0.5), w=["negh"])
    p.emit("v", lambda e: e.memset(CA, 0.0), w=["carrya"])
    p.emit("v", lambda e: e.memset(CB, 0.0), w=["carryb"])

    p.emit("y", lambda e: e.dma_start(out=TK[0][:1, 0:512], in_=cones_in),
           w=["TK0"], dma=True)
    p.emit("s", lambda e: e.copy(out=ONEB, in_=TK[0][:1, 0:512]),
           r=["TK0"], w=["onebf"])
    p.emit("g", lambda e: e.dma_start(out=onesd, in_=ONEB), r=["onebf"],
           w=["onesd"], dma=True)
    p.emit("y", lambda e: e.dma_start(out=L1E[RK:RK + 1, :], in_=onesd),
           r=["onesd"], w=["l1ones"], dma=True)
    p.emit("y", lambda e: e.dma_start(out=L2E[RK:RK + 1, :], in_=onesd),
           r=["onesd"], w=["l2ones"], dma=True)

    def cm_load(src, dst_ap, dst_key, post=None):
        tmp = TK[1][0:ND, 0:TT]
        p.emit("y", lambda e, s=src: e.dma_start(
            out=tmp, in_=s.rearrange("(s q) -> s q", q=TT)), w=["TK1"], dma=True)
        trap, trkey = nexttr()
        p.emit("t", lambda e, bk=trap: e.transpose(
            bk[:, 0:ND], tmp, IDF[0:ND, 0:ND]),
            r=["TK1", "identf"], w=[trkey])
        p.emit("v", lambda e, bk=trap, d=dst_ap: e.tensor_copy(d, bk[:, 0:ND]),
               r=[trkey], w=[dst_key])
        if post:
            post(dst_ap, dst_key)

    cm_load(td_in, LAM, "lamcm")
    p.emit("s", lambda e: e.activation(out=LAM, in_=LAM, func=AF.Exp),
           r=["lamcm"], w=["lamcm"])
    p.emit("s", lambda e: e.activation(out=LAM, in_=LAM, func=AF.Exp,
                                       scale=NEG1[:, 0:1]),
           r=["lamcm", "neg1"], w=["lamcm"])
    cm_load(tf_in, UCM, "ucm")
    p.emit("s", lambda e: e.activation(out=UCM, in_=UCM, func=AF.Exp),
           r=["ucm"], w=["ucm"])
    cm_load(tmk_in, TMK, "tmkcm")
    cm_load(tmv_in, TMV, "tmvcm")
    cm_load(tmr_in, TMR, "tmrcm")

    def lora_a(src, n, dst_ap, dst_key):
        step = 512 // RK
        for c0 in range(0, n, step):
            nn = min(step, n - c0)
            tmp = TK[2][:, 0:nn * RK]
            p.emit("y", lambda e, s=src, c=c0, nn=nn, t=tmp: e.dma_start(
                out=t.rearrange("p (s r) -> p s r", s=nn),
                in_=s.rearrange("(s p) r -> p s r", p=TT)[:, c:c + nn, :]),
                w=["TK2"], dma=True)
            p.emit("s", lambda e, d=dst_ap, c=c0, nn=nn, t=tmp: e.copy(
                out=d[:, c * RK:(c + nn) * RK], in_=t),
                r=["TK2"], w=[dst_key])

    lora_a(upla_in, ND, AUPB, "aupb")
    lora_a(dnla_in, NH, ADNB, "adnb")

    def lora_b(srcB, srcb, n, dram):
        for c in range(n // 512):
            tmpf = TK[2][0:RK, 0:512]
            tmpb = TK[3][0:RK, 0:256].bitcast(BF16)
            tmpb1 = TK[3][0:1, 256:512].bitcast(BF16)
            p.emit("y", lambda e, s=srcB, cc=c: e.dma_start(
                out=tmpf, in_=s[:, cc * 512:(cc + 1) * 512]), w=["TK2"], dma=True)
            p.emit("s", lambda e: e.copy(out=tmpb, in_=tmpf),
                   r=["TK2"], w=["TK3"])
            p.emit("y", lambda e, s=srcb, cc=c: e.dma_start(
                out=tmpf[0:1, :], in_=s[cc * 512:(cc + 1) * 512].rearrange("(a q) -> a q", a=1)),
                r=["TK3"], w=["TK2"], dma=True)
            p.emit("s", lambda e: e.copy(out=tmpb1, in_=tmpf[0:1, :]),
                   r=["TK2"], w=["TK3b"])
            p.emit("g", lambda e, dd=dram, cc=c: e.dma_start(
                out=dd[0:RK, cc * 512:(cc + 1) * 512], in_=tmpb),
                r=["TK3"], w=["lorabd"], dma=True)
            p.emit("g", lambda e, dd=dram, cc=c: e.dma_start(
                out=dd[RK:RK + 1, cc * 512:(cc + 1) * 512], in_=tmpb1),
                r=["TK3b"], w=["lorabd"], dma=True)

    lora_b(uplb_in, bup_in, H, bupxd)
    lora_b(dnlb_in, bdn_in, D, bdnxd)
    p.emit("y", lambda e: e.dma_start(out=BDNX, in_=bdnxd), r=["lorabd"],
           w=["bdnx"], dma=True)

    p.emit("v", lambda e: e.memset(X[:, 0:D], 0.0), w=xkeys(0))
    p.emit("y", lambda e: e.dma_start(out=X[0:1, 0:D], in_=xprev),
           r=xkeys(0), w=xkeys(0), dma=True)
    xp = X[0:1, 0:D]
    p.emit("v", lambda e: e.reduce_sum(SSUM[0:1, 0:1], xp, axis=AX.X),
           r=xkeys(0), w=["ssum0"])
    p.emit("s", lambda e: e.activation(out=XT[0:1, :], in_=xp, func=AF.Square,
                                       accum_out=SSQ[0:1, 0:1]),
           r=xkeys(0), w=["xn_tok", "ssq0"])
    p.emit("v", lambda e: e.tensor_scalar_mul(SMEAN[0:1, 0:1], SSUM[0:1, 0:1], 1.0 / D),
           r=["ssum0"], w=["smean0"])
    p.emit("v", lambda e: e.scalar_tensor_tensor(
        SMSQ[0:1, 0:1], SSUM[0:1, 0:1], 1.0 / (D * D), SSUM[0:1, 0:1],
        ALU.mult, ALU.mult), r=["ssum0"], w=["smsq0"])
    p.emit("v", lambda e: e.tensor_scalar(
        out=SVAR[0:1, 0:1], in0=SSQ[0:1, 0:1], scalar1=1.0 / D, scalar2=EPS,
        op0=ALU.mult, op1=ALU.add), r=["ssq0"], w=["svar0"])
    p.emit("v", lambda e: e.tensor_sub(SVAR[0:1, 0:1], SVAR[0:1, 0:1], SMSQ[0:1, 0:1]),
           r=["smsq0"], w=["svar0"])
    p.emit("s", lambda e: e.activation(out=SSTD[0:1, 0:1], in_=SVAR[0:1, 0:1],
                                       func=AF.Ln), r=["svar0"], w=["sstd0"])
    p.emit("s", lambda e: e.activation(out=SRSTD[0:1, 0:1], in_=SSTD[0:1, 0:1],
                                       func=AF.Exp, scale=NEGH[0:1, 0:1]),
           r=["sstd0", "negh"], w=["srstd0"])
    p.emit("v", lambda e: e.tensor_scalar(
        out=xp, in0=xp, scalar1=SMEAN[0:1, 0:1], scalar2=SRSTD[0:1, 0:1],
        op0=ALU.subtract, op1=ALU.mult),
        r=xkeys(0) + ["smean0", "srstd0"], w=xkeys(0))
    for g in range(4):
        trap, trkey = nexttr()

        def tfn(e, gg=g, bk=trap):
            last = None
            for j in range(4):
                s = gg * 4 + j
                last = e.transpose(bk[:, j * TT:(j + 1) * TT],
                                   X[:, s * TT:(s + 1) * TT], IDF)
            return last
        p.emit("t", tfn, r=xkeys(0) + ["identf"], w=[trkey])
        p.emit("v", lambda e, gg=g, bk=trap: e.tensor_copy(
            BCOL[:, gg * 4:gg * 4 + 4],
            bk[:, 0:4 * TT].rearrange("p (j c) -> p j c", j=4)[:, :, 0]),
            r=[trkey], w=["bcol"])


    prep_units = []
    prep_it = [0]

    def prep_one(qsrc, ssrc, rr, cc, dst_rearr, wkey):
        i = prep_it[0]
        prep_it[0] += 1
        slot = i % 2
        ink = PKEY_IN[slot]
        if ssrc is None:
            dq = PIN_F[slot]
            dqk = ink
            p.emit("y", lambda e, s=qsrc, r_=rr, c_=cc, b=dq: e.dma_start(
                out=b, in_=s[r_ * TT:(r_ + 1) * TT, c_ * D:(c_ + 1) * D]),
                w=ink, dma=True)
        else:
            inb = PIN_I[slot]
            sx = SEXP[slot]
            sxk = PKEY_SEXP[slot]
            dq = DQF[slot]
            dqk = PKEY_DQF[slot]
            psc = PSC[slot]
            psck = "pscale%d" % slot
            p.emit("y", lambda e, s=qsrc, r_=rr, c_=cc, b=inb: e.dma_start(
                out=b, in_=s[r_ * TT:(r_ + 1) * TT, c_ * D:(c_ + 1) * D]),
                w=ink, dma=True)
            p.emit("y", lambda e, s=ssrc, r_=rr, c_=cc, b=psc: e.dma_start(
                out=b, in_=s[r_ * TT:(r_ + 1) * TT, c_ * 32:(c_ + 1) * 32]),
                w=[psck], dma=True)
            p.emit("s", lambda e, b=psc, d=sx: e.copy(
                out=d.rearrange("p (a b) -> p a b", a=32),
                in_=b.unsqueeze(2).broadcast_to([TT, 32, 64])),
                r=[psck], w=sxk)
            p.emit("v", lambda e, b=inb, s2=sx, d=dq: e.scalar_tensor_tensor(
                d, b, -7.5, s2, ALU.add, ALU.mult),
                r=ink + sxk, w=dqk)
        tslot = i % 2
        tb = TBUF[tslot]
        tbk = PKEY_TB[tslot]
        for g in range(2):
            wb = PB2[(tbank[0] + g) % 4]
            wkeys = ["PB%d" % (((tbank[0] + g) % 4) * 2),
                     "PB%d" % (((tbank[0] + g) % 4) * 2 + 1)]

            def tfn(e, gg=g, bk=wb, d=dq):
                last = None
                for j in range(8):
                    s = gg * 8 + j
                    last = e.transpose(bk[:, j * TT:(j + 1) * TT],
                                       d[:, s * TT:(s + 1) * TT], IDF)
                return last
            p.emit("t", tfn, r=dqk + ["identf"], w=wkeys)
            if g % 2 == 0:
                p.emit("v", lambda e, gg=g, bk=wb, t=tb: e.tensor_copy(
                    t.rearrange("p (s c) -> p s c", s=ND)[:, gg * 8:gg * 8 + 8, :],
                    bk[:, 0:8 * TT].rearrange("p (j c) -> p j c", j=8)),
                    r=wkeys, w=tbk)
            else:
                p.emit("s", lambda e, gg=g, bk=wb, t=tb: e.copy(
                    out=t.rearrange("p (s c) -> p s c", s=ND)[:, gg * 8:gg * 8 + 8, :],
                    in_=bk[:, 0:8 * TT].rearrange("p (j c) -> p j c", j=8)),
                    r=wkeys, w=tbk)
        tbank[0] += 2
        p.emit("g", lambda e, r_=rr, c_=cc, t=tb: e.dma_start(
            out=dst_rearr(r_, c_), in_=t.rearrange("p (s c) -> p s c", s=ND)),
            r=tbk, w=[wkey], dma=True)

    def prep_quant(qsrc, ssrc, dram, n_row_tiles, col_tiles_per_row, dst_rearr, wkey):
        for rr in range(n_row_tiles):
            for cc in range(col_tiles_per_row):
                prep_units.append((qsrc, ssrc, rr, cc, dst_rearr, wkey))

    for w_in_, w_t_, wkey_ in [(wk_in, wkt, "wbf_k"), (wv_in, wvt, "wbf_v"),
                               (wr_in, wrt, "wbf_r")]:
        def att_dst(rr, cc, wt2=w_t_):
            return wt2[rr].rearrange("(s q) ee -> q s ee", q=TT)

        prep_quant(w_in_, None, w_t_, ND, 1, att_dst, wkey_)
    wro_ = wot.rearrange("(s q) ee -> q s ee", q=TT)

    def att_dst_o(rr, cc):
        return wro_[:, :, rr * TT:(rr + 1) * TT]

    prep_quant(wo_in, None, wot, ND, 1, att_dst_o, "wbf_o")

    def wut_dst(rr, cc):
        return wut[rr].rearrange("(s q) hh -> q s hh", q=TT)

    prep_quant(wupq_in, wups_in, wut, NH, 1, wut_dst, "wutd")
    p.emit("y", lambda e: e.dma_start(
        out=TK[1][0:NH, 0:TT], in_=bup_in.rearrange("(s q) -> s q", q=TT)),
        w=["TK1"], dma=True)
    trapb, trkeyb = nexttr()
    p.emit("t", lambda e, bk=trapb: e.transpose(
        bk[:, 0:NH], TK[1][0:NH, 0:TT], IDF[0:NH, 0:NH]),
        r=["TK1", "identf"], w=[trkeyb])
    p.emit("v", lambda e, bk=trapb: e.tensor_copy(BUPC, bk[:, 0:NH]),
           r=[trkeyb], w=["bupc"])
    wdt_r = wdt.rearrange("(s q) ee -> q s ee", q=TT)

    def wdt_dst(rr, cc):
        return wdt_r[:, cc * ND:(cc + 1) * ND, rr * TT:(rr + 1) * TT]

    prep_quant(wdnq_in, wdns_in, wdt, ND, NEC, wdt_dst, "wdtd")

    attn_u = [u for u in prep_units if u[1] is None]
    quant_u = [u for u in prep_units if u[1] is not None]
    ai, qi2 = 0, 0
    while ai < len(attn_u) or qi2 < len(quant_u):
        if ai < len(attn_u):
            prep_one(*attn_u[ai]); ai += 1
        for _ in range(2):
            if qi2 < len(quant_u):
                prep_one(*quant_u[qi2]); qi2 += 1
    prep_units.clear()




    wring = [0]

    def whalf():
        s = wring[0] % 8
        wring[0] += 1
        return WH[s], "WH%d" % s

    tkr = [0]

    def tkslot():
        s = tkr[0] % TKR
        tkr[0] += 1
        return s

    def proj_tokmajor(lhs_fn, lhs_keys_fn, wsrc3d_fn, n_s, banks,
                      extra_units=(), wkey_src=()):
        nx = len(extra_units)
        nq = n_s // 4
        bufs = {}
        qeng = [0]

        def emit_load(hh):
            buf, bkey = whalf()
            bufs[hh * 4] = (buf, bkey)
            p.emit("y" if hh % 2 == 0 else "g",
                   lambda e, a=buf, s_=hh * 4: e.dma_start(
                out=a.rearrange("p (s c) -> p s c", s=4), in_=wsrc3d_fn(s_)),
                r=list(wkey_src), w=[bkey], dma=True)

        for hh in range(min(4, nq)):
            emit_load(hh)
        for s in range(n_s):
            buf, bkey = bufs[(s // 4) * 4]
            wap = buf[:, (s % 4) * 512:(s % 4 + 1) * 512]
            first, last = (s == 0), (s == n_s - 1 and nx == 0)
            rl = [bkey] + lhs_keys_fn(s)
            wl = []
            if first or last:
                wl = ["PB%d" % b for b in banks]

            def mfn(e, ss=s, a=wap, ff=first, ll=last):
                li = None
                for t in range(TBT):
                    li = e.matmul(PB[banks[t]][:], lhs_fn(ss, t), a,
                                  start=ff, stop=ll)
                return li
            p.emit("t", mfn, r=rl, w=wl)
            if s % 4 == 3 and s // 4 + 4 < nq:
                emit_load(s // 4 + 4)
        for i, (xfn, xreads) in enumerate(extra_units):
            ll = (i == nx - 1)
            wl = ["PB%d" % b for b in banks] if ll else []

            def mfn(e, f=xfn, ll2=ll):
                li = None
                for t in range(TBT):
                    li = f(e, t, False, ll2)
                return li
            p.emit("t", mfn, r=list(xreads), w=wl)

    p.emit("y", lambda e: e.dma_start(out=VALID, in_=wvalid_in), w=["valid"], dma=True)
    p.emit("y", lambda e: e.dma_start(out=X[:, 0:D], in_=xwarm_in),
           r=xkeys(0), w=xkeys(0), dma=True)
    ln_stats(X[:, 0:D], xkeys(0), XT, ["xn_tok"], c=0)
    p.emit("v", lambda e: e.tensor_scalar(
        out=XT, in0=X[:, 0:D], scalar1=SMEAN[:, 0:1], scalar2=SRSTD[:, 0:1],
        op0=ALU.subtract, op1=ALU.mult),
        r=xkeys(0) + ["smean0", "srstd0"], w=["xn_tok"])
    for g in range(4):
        dst = XN.rearrange("p (s c) -> p s c", s=ND, c=513)[:, g * 4:g * 4 + 4, 1:1 + TT]
        transpose_group(XT[:, g * 512:(g + 1) * 512], ["xn_tok"], 4,
                        dst, ["XN%d" % (g * 4 + j) for j in range(4)])

    def warm_proj(wsrc, tmap, tkey, dst_off_fn, wbkey):
        for s in range(ND):
            xs = XN[:, s * 513 + 1:s * 513 + 1 + TT]
            p.emit("v", lambda e, a=xs: e.tensor_sub(
                DIFF[:, 1:TT], a[:, 1:TT], a[:, 0:TT - 1]),
                r=["XN%d" % s], w=["diff"])
            p.emit("v", lambda e, a=xs, ss=s, tm=tmap: e.scalar_tensor_tensor(
                MX[:, ss * 512 + 1:ss * 512 + TT], DIFF[:, 1:TT],
                tm[:, ss:ss + 1], a[:, 0:TT - 1], ALU.mult, ALU.add),
                r=["diff", tkey, "XN%d" % s], w=["MX%d" % s])
            p.emit("v", lambda e, a=xs, ss=s, tm=tmap: e.tensor_scalar_mul(
                MX[:, ss * 512:ss * 512 + 1], a[:, 0:1], tm[:, ss:ss + 1]),
                r=[tkey, "XN%d" % s], w=["MX%d" % s])
        for oc in range(ND):
            buf, bkey2 = whalf()
            p.emit("y", lambda e, a=buf, o2=oc, wb=wsrc: e.dma_start(
                out=a[:, 0:ND * TT].rearrange("p (s c) -> p s c", s=ND),
                in_=wb[o2].rearrange("(s q) ee -> q s ee", q=TT)),
                r=[wbkey], w=[bkey2], dma=True)
            bank = oc % 8

            def mfn(e, a=buf, bk=bank):
                li = None
                for s in range(ND):
                    li = e.matmul(PB[bk][:, 0:TT], a[:, s * TT:(s + 1) * TT],
                                  MX[:, s * 512:s * 512 + TT],
                                  start=(s == 0), stop=(s == ND - 1))
                return li
            p.emit("t", mfn,
                   r=[bkey2] + ["MX%d" % s for s in range(ND)],
                   w=["PB%d" % bank])
            dap, dkeys = dst_off_fn(oc)
            p.emit("v", lambda e, bk=bank, d=dap: e.tensor_copy(d, PB[bk][:, 0:TT]),
                   r=["PB%d" % bank], w=dkeys)

    warm_proj(wkt, TMK, "tmkcm",
              lambda oc: (VR[:, (16 + oc) * 512:(16 + oc) * 512 + TT],
                          ["VR%d" % (16 + oc)]), "wbf_k")
    warm_proj(wvt, TMV, "tmvcm",
              lambda oc: (VR[:, oc * 512:oc * 512 + TT], ["VR%d" % oc]), "wbf_v")
    for s in range(ND):
        kap = VR[:, (16 + s) * 512:(16 + s) * 512 + TT]
        vap = VR[:, s * 512:s * 512 + TT]
        p.emit("s", lambda e, a=kap: e.activation(out=SC["sc_p"][:, 0:TT], in_=a,
                                                  func=AF.Exp),
               r=["VR%d" % (16 + s)], w=["sc_p"])
        p.emit("v", lambda e: e.tensor_scalar_mul(
            SC["sc_p"][:, 0:TT], SC["sc_p"][:, 0:TT], VALID[:, 0:1]),
            r=["sc_p", "valid"], w=["sc_p"])
        p.emit("v", lambda e, a=vap: e.tensor_mul(
            SC["sc_pv"][:, 0:TT], SC["sc_p"][:, 0:TT], a),
            r=["sc_p", "VR%d" % s], w=["sc_pv"])
        p.emit("v", lambda e, ss=s: e.tensor_tensor_scan(
            SC["sc_sa"][:, 0:TT], LAM[:, ss:ss + 1].broadcast_to([TT, TT]),
            SC["sc_pv"][:, 0:TT], 0.0, ALU.mult, ALU.add),
            r=["sc_pv", "lamcm"], w=["sc_sa"])
        p.emit("v", lambda e, ss=s: e.tensor_tensor_scan(
            SC["sc_sb"][:, 0:TT], LAM[:, ss:ss + 1].broadcast_to([TT, TT]),
            SC["sc_p"][:, 0:TT], 0.0, ALU.mult, ALU.add),
            r=["sc_p", "lamcm"], w=["sc_sb"])
        p.emit("v", lambda e, ss=s: e.tensor_copy(
            CA[:, ss:ss + 1], SC["sc_sa"][:, TT - 1:TT]),
            r=["sc_sa"], w=["carrya"])
        p.emit("v", lambda e, ss=s: e.tensor_copy(
            CB[:, ss:ss + 1], SC["sc_sb"][:, TT - 1:TT]),
            r=["sc_sb"], w=["carryb"])

    def xload(b, t):
        p.emit("y", lambda e, bb=b, tt_=t: e.dma_start(
            out=X[:, tt_ * D:(tt_ + 1) * D],
            in_=xin[(bb * TBT + tt_) * TT:(bb * TBT + tt_ + 1) * TT, :]),
            w=xkeys(t), dma=True)

    def ln1(t):
        xap = X[:, t * D:(t + 1) * D]
        ln_stats(xap, xkeys(t), XT, ["xn_tok"], c=t)
        p.emit("v", lambda e, a=xap, cc=t: e.tensor_scalar(
            out=XT, in0=a, scalar1=SMEAN[:, cc:cc + 1], scalar2=SRSTD[:, cc:cc + 1],
            op0=ALU.subtract, op1=ALU.mult),
            r=xkeys(t) + ["smean%d" % t, "srstd%d" % t], w=["xn_tok"])
        for g in range(4):
            dst = XN.rearrange("p (s c) -> p s c", s=ND, c=513)[:, g * 4:g * 4 + 4,
                                                               1 + t * TT:1 + (t + 1) * TT]
            transpose_group(XT[:, g * 512:(g + 1) * 512], ["xn_tok"], 4,
                            dst, ["XN%d" % (g * 4 + j) for j in range(4)])

    for t in range(TBT):
        xload(0, t)

    for b in range(nblocks):
        for t in range(TBT):
            ln1(t)

        def emit_mix_one(s, tmap, tkey, reg, kfn):
                dst = reg[:, s * 512:(s + 1) * 512]
                p.emit("v", lambda e, ss=s: e.tensor_sub(
                    DIFF, xnt(ss), xnt_ext(ss)),
                    r=["XN%d" % s], w=["diff"])
                p.emit("v", lambda e, ss=s, tm=tmap, d=dst: e.scalar_tensor_tensor(
                    d, DIFF, tm[:, ss:ss + 1], xnt_ext(ss), ALU.mult, ALU.add),
                    r=["diff", tkey, "XN%d" % s], w=[kfn(s)])

        def emit_mix(tmap, tkey, reg, kfn):
            for s in range(ND):
                emit_mix_one(s, tmap, tkey, reg, kfn)

        def proj_chanmajor(wsrcb, n_oc, rhs_fn, rhs_keys_sub, dst_fn, wkey_src,
                           evac=None, post_oc=None):
            for oc in range(n_oc):
                buf, bkey2 = whalf()
                p.emit("y" if oc % 2 == 0 else "g",
                       lambda e, a=buf, o2=oc: e.dma_start(
                    out=a[:, 0:ND * TT].rearrange("p (s c) -> p s c", s=ND),
                    in_=wsrcb[o2].rearrange("(s q) cc -> q s cc", q=TT)),
                    r=list(wkey_src), w=[bkey2], dma=True)
                bank = oc % 8
                for s0 in range(0, ND, 4):
                    last_sub = (s0 == ND - 4)

                    def mfn(e, a=buf, bk=bank, s_=s0):
                        li = None
                        for s in range(s_, s_ + 4):
                            li = e.matmul(PB[bk][:], a[:, s * TT:(s + 1) * TT],
                                          rhs_fn(s), start=(s == 0),
                                          stop=(s == ND - 1))
                        return li
                    wl = ["PB%d" % bank] if (s0 == 0 or last_sub) else []
                    p.emit("t", mfn, r=[bkey2] + rhs_keys_sub(s0), w=wl)
                evac(oc, bank)
                if post_oc is not None:
                    post_oc(oc)

        def kvr_proj(wsrcb, rhs_reg, rhs_kfn, dst_fn, wbkey, post_oc=None):

            def evac(oc, bank):
                dap, dkeys = dst_fn(oc)
                if oc % 2 == 0:
                    p.emit("v", lambda e, bk=bank, d=dap: e.tensor_copy(d, PB[bk][:]),
                           r=["PB%d" % bank], w=dkeys)
                else:
                    p.emit("s", lambda e, bk=bank, d=dap: e.copy(out=d, in_=PB[bk][:]),
                           r=["PB%d" % bank], w=dkeys)

            proj_chanmajor(wsrcb, ND,
                           lambda s: rhs_reg[:, s * 512:(s + 1) * 512],
                           lambda s0: [rhs_kfn(s) for s in range(s0, s0 + 4)],
                           dst_fn, [wbkey], evac=evac, post_oc=post_oc)

        def scan_part1(s):
            kap = OA[:, s * 512:(s + 1) * 512]
            vap = VR[:, s * 512:(s + 1) * 512]
            kk = [okeys(s)]
            p.emit("s", lambda e, a=kap: e.activation(out=SC["sc_p"], in_=a,
                                                      func=AF.Exp),
                   r=kk, w=["sc_p"])
            p.emit("v", lambda e, a=vap: e.tensor_mul(SC["sc_pv"], SC["sc_p"], a),
                   r=["sc_p", "VR%d" % s], w=["sc_pv"])
            p.emit("v", lambda e, ss=s: e.tensor_tensor_scan(
                SC["sc_sa"], LAM[:, ss:ss + 1].broadcast_to([TT, 512]),
                SC["sc_pv"], CA[:, ss:ss + 1], ALU.mult, ALU.add),
                r=["sc_pv", "lamcm", "carrya"], w=["sc_sa"])
            p.emit("v", lambda e, ss=s: e.tensor_tensor_scan(
                SC["sc_sb"], LAM[:, ss:ss + 1].broadcast_to([TT, 512]),
                SC["sc_p"], CB[:, ss:ss + 1], ALU.mult, ALU.add),
                r=["sc_p", "lamcm", "carryb"], w=["sc_sb"])
            p.emit("v", lambda e, a=vap, ss=s: e.scalar_tensor_tensor(
                SC["sc_wkv"], SC["sc_p"], UCM[:, ss:ss + 1], a,
                ALU.mult, ALU.mult),
                r=["sc_p", "ucm", "VR%d" % s], w=["sc_wkv"])
            p.emit("v", lambda e, ss=s: e.tensor_add(
                SC["sc_num"][:, 0:1], CA[:, ss:ss + 1], SC["sc_wkv"][:, 0:1]),
                r=["carrya", "sc_wkv"], w=["sc_num"])
            p.emit("v", lambda e: e.tensor_add(
                SC["sc_num"][:, 1:512], SC["sc_sa"][:, 0:511],
                SC["sc_wkv"][:, 1:512]), r=["sc_sa", "sc_wkv"], w=["sc_num"])
            p.emit("v", lambda e, ss=s: e.scalar_tensor_tensor(
                SC["sc_q"][:, 0:1], SC["sc_p"][:, 0:1], UCM[:, ss:ss + 1],
                CB[:, ss:ss + 1], ALU.mult, ALU.add),
                r=["carryb", "sc_p", "ucm"], w=["sc_q"])
            p.emit("v", lambda e, ss=s: e.scalar_tensor_tensor(
                SC["sc_q"][:, 1:512], SC["sc_p"][:, 1:512], UCM[:, ss:ss + 1],
                SC["sc_sb"][:, 0:511], ALU.mult, ALU.add),
                r=["sc_sb", "sc_p", "ucm"], w=["sc_q"])
            p.emit("v", lambda e, ss=s: e.tensor_copy(CA[:, ss:ss + 1],
                                                      SC["sc_sa"][:, 511:512]),
                   r=["sc_sa"], w=["carrya"])
            p.emit("v", lambda e, ss=s: e.tensor_copy(CB[:, ss:ss + 1],
                                                      SC["sc_sb"][:, 511:512]),
                   r=["sc_sb"], w=["carryb"])
            p.emit("s", lambda e: e.activation(out=SC["sc_pv"], in_=SC["sc_q"],
                                               func=AF.Ln),
                   r=["sc_q"], w=["sc_pv"])
            p.emit("s", lambda e: e.activation(out=SC["sc_pv"], in_=SC["sc_pv"],
                                               func=AF.Exp, scale=NEG1[:, 0:1]),
                   r=["sc_pv", "neg1"], w=["sc_pv"])
            p.emit("v", lambda e, ss=s: e.tensor_mul(
                OA[:, ss * 512:(ss + 1) * 512], SC["sc_num"], SC["sc_pv"]),
                r=["sc_num", "sc_pv"], w=[okeys(s)])

        def rproj_hook(oc):
            if oc % 4 != 3:
                return
            ec2 = oc // 4
            p.emit("s", lambda e, e2=ec2: e.activation(
                out=VR[:, (16 + e2 * 4) * 512:(16 + e2 * 4 + 4) * 512],
                in_=VR[:, (16 + e2 * 4) * 512:(16 + e2 * 4 + 4) * 512],
                func=AF.Sigmoid),
                r=["VR%d" % (16 + ec2 * 4 + j) for j in range(4)],
                w=["VR%d" % (16 + ec2 * 4 + j) for j in range(4)])
            for j in range(4):
                s = ec2 * 4 + j
                p.emit("v", lambda e, ss=s: e.tensor_mul(
                    VR[:, ss * 512:(ss + 1) * 512],
                    OA[:, ss * 512:(ss + 1) * 512],
                    VR[:, (16 + ss) * 512:(16 + ss + 1) * 512]),
                    r=[okeys(s), "VR%d" % (16 + s)], w=["VR%d" % s])

        p.emit("v", lambda e: e.tensor_copy(
            XN.rearrange("p (s c) -> p s c", s=ND, c=513)[:, :, 0:1], BCOL.unsqueeze(2)),
            r=["bcol"], w=["XN%d" % s for s in range(ND)])
        VR16 = VR[:, 16 * 512:32 * 512]
        kmx = lambda s: "MX%d" % s
        kvr16 = lambda s: "VR%d" % (16 + s)
        emit_mix(TMK, "tmkcm", MX, kmx)
        kvr_proj(wkt, MX, kmx,
                 lambda oc: (OA[:, oc * 512:(oc + 1) * 512], [okeys(oc)]), "wbf_k",
                 post_oc=lambda oc: emit_mix_one(oc, TMV, "tmvcm", VR16, kvr16))
        def vproj_post(oc):
            emit_mix_one(oc, TMR, "tmrcm", MX, kmx)
            scan_part1(oc)

        kvr_proj(wvt, VR16, kvr16,
                 lambda oc: (VR[:, oc * 512:(oc + 1) * 512], ["VR%d" % oc]), "wbf_v",
                 post_oc=vproj_post)
        kvr_proj(wrt, MX, kmx,
                 lambda oc: (VR[:, (16 + oc) * 512:(16 + oc + 1) * 512],
                             ["VR%d" % (16 + oc)]), "wbf_r",
                 post_oc=rproj_hook)

        p.emit("v", lambda e: e.tensor_copy(
            BCOL.unsqueeze(2),
            XN.rearrange("p (s c) -> p s c", s=ND, c=513)[:, :, 512:513]),
            r=["XN%d" % s for s in range(ND)], w=["bcol"])

        wot_r = wot.rearrange("(s q) ee -> q s ee", q=TT)
        for ec in range(NEC):
            banks = [0, 1, 2, 3] if ec % 2 == 0 else [4, 5, 6, 7]

            def wsrc3d(s0, e2=ec):
                return wot_r[:, s0:s0 + 4, e2 * 512:(e2 + 1) * 512]

            proj_tokmajor(
                lambda s, t: VR[:, s * 512 + t * TT:s * 512 + (t + 1) * TT],
                lambda s: ["VR%d" % s], wsrc3d, ND, banks,
                wkey_src=["wbf_o"])
            for t in range(TBT):
                p.emit("s", lambda e, bk=banks[t], tt_=t, e2=ec: e.copy(
                    out=OA[:, (tt_ * 4 + e2) * 512:(tt_ * 4 + e2 + 1) * 512],
                    in_=PB[bk][:]),
                    r=["PB%d" % banks[t]], w=[okeys(t * 4 + ec)])

        for t in range(TBT):
            oap = OA[:, t * 4 * 512:(t + 1) * 4 * 512]
            okl = [okeys(t * 4 + j) for j in range(4)]
            ln_stats(oap, okl, XT, ["xn_tok"], c=t + 4)
            p.emit("v", lambda e, a=oap, cc=t + 4: e.tensor_scalar(
                out=a, in0=a, scalar1=SMEAN[:, cc:cc + 1], scalar2=SRSTD[:, cc:cc + 1],
                op0=ALU.subtract, op1=ALU.mult),
                r=okl + ["smean%d" % (t + 4), "srstd%d" % (t + 4)], w=okl)
            p.emit("v", lambda e, a=oap, tt_=t: e.tensor_add(
                X[:, tt_ * D:(tt_ + 1) * D], X[:, tt_ * D:(tt_ + 1) * D], a),
                r=okl + xkeys(t), w=xkeys(t))

        for t in range(TBT):
            xap = X[:, t * D:(t + 1) * D]
            ln_stats3(xap, xkeys(t), XT, ["xn_tok"], c=t)
            p.emit("v", lambda e, a=xap, cc=t: e.tensor_scalar(
                out=XT, in0=a, scalar1=SMEAN[:, cc:cc + 1], scalar2=SRSTD[:, cc:cc + 1],
                op0=ALU.subtract, op1=ALU.mult),
                r=xkeys(t) + ["smean%d" % t, "srstd%d" % t], w=["xn_tok"])
            for g in range(4):
                dst = XN.rearrange("p (s c) -> p s c", s=ND, c=513)[:, g * 4:g * 4 + 4,
                                                                   1 + t * TT:1 + (t + 1) * TT]
                transpose_group(XT[:, g * 512:(g + 1) * 512], ["xn_tok"], 4,
                                dst, ["XN%d" % (g * 4 + j) for j in range(4)])

        def l1fn(e):
            li = None
            for s in range(ND):
                li = e.matmul(LP[:],
                              AUPB[:, s * RK:(s + 1) * RK],
                              xnt(s), start=(s == 0), stop=(s == ND - 1))
            return li
        p.emit("t", l1fn, r=["aupb"] + ["XN%d" % s for s in range(ND)], w=["PB7"])
        p.emit("v", lambda e: e.tensor_copy(L1E[0:RK, :], LP[:]),
               r=["PB7"], w=["l1e"])

        for hc4 in range(NHC):
            bslot = hc4 % 2
            bkey = "BUP%d" % bslot
            p.emit("y", lambda e, h=hc4, sl=bslot: e.dma_start(
                out=BUPS[sl], in_=bupxd[:, h * 512:(h + 1) * 512]),
                r=["lorabd"], w=[bkey], dma=True)
            for j4 in range(4):
                oc = hc4 * 4 + j4
                buf, bkey2 = whalf()
                p.emit("y" if oc % 2 == 0 else "g",
                       lambda e, a=buf, o2=oc: e.dma_start(
                    out=a[:, 0:ND * TT].rearrange("p (s c) -> p s c", s=ND),
                    in_=wut[o2].rearrange("(s q) hh -> q s hh", q=TT)),
                    r=["wutd"], w=[bkey2], dma=True)
                bank = oc % 8
                for s0 in range(0, ND, 4):
                    last_sub = (s0 == ND - 4)

                    def mfn(e, a=buf, bk=bank, s_=s0, ls=last_sub, o2=oc, sl=bslot):
                        li = None
                        for s in range(s_, s_ + 4):
                            li = e.matmul(PB[bk][:], a[:, s * TT:(s + 1) * TT],
                                          xnt(s), start=(s == 0), stop=False)
                        if ls:
                            li = e.matmul(
                                PB[bk][:], BUPS[sl][0:RK, (o2 % 4) * TT:(o2 % 4 + 1) * TT],
                                L1E[0:RK, :], start=False, stop=True)
                        return li
                    rl = [bkey2] + ["XN%d" % s for s in range(s0, s0 + 4)]
                    if last_sub:
                        rl += [bkey, "l1e"]
                    wl = ["PB%d" % bank] if (s0 == 0 or last_sub) else []
                    p.emit("t", mfn, r=rl, w=wl)
                dap, dkeys = x2tile(oc)
                p.emit("s", lambda e, bk=bank, d=dap, o2=oc: e.activation(
                    out=d, in_=PB[bk][:], func=AF.Relu,
                    bias=BUPC[:, o2:o2 + 1]),
                    r=["PB%d" % bank, "bupc"], w=dkeys)

        x2keys_all = []
        for j in range(NH):
            x2keys_all += x2tile(j)[1]
        x2keys_all = list(set(x2keys_all))

        def l2fn(e):
            li = None
            for s in range(NH):
                li = e.matmul(LP[:], ADNB[:, s * RK:(s + 1) * RK],
                              x2tile(s)[0], start=(s == 0), stop=(s == NH - 1))
            return li
        p.emit("t", l2fn, r=["adnb"] + x2keys_all, w=["PB7"])
        p.emit("v", lambda e: e.tensor_copy(L2E[0:RK, :], LP[:]),
               r=["PB7"], w=["l2e"])

        wdt_r3 = wdt.rearrange("(s q) ee -> q s ee", q=TT)
        for ec in range(NEC):
            banks = [0, 1, 2, 3] if ec % 2 == 0 else [4, 5, 6, 7]

            def wsrc3d(s0, e2=ec):
                return wdt_r3[:, s0:s0 + 4, e2 * 512:(e2 + 1) * 512]

            def lorafn(e, t, start, stop, bb=tuple(banks), e2=ec):
                return e.matmul(PB[bb[t]][:], L2E[:, t * TT:(t + 1) * TT],
                                BDNX[:, e2 * 512:(e2 + 1) * 512],
                                start=start, stop=stop)

            proj_tokmajor(
                lambda s, t: x2tile(s)[0][:, t * TT:(t + 1) * TT],
                lambda s: x2tile(s)[1], wsrc3d, NH, banks,
                extra_units=[(lorafn, ["l2e", "l2ones", "bdnx"])],
                wkey_src=["wdtd"])
            for t in range(TBT):
                p.emit("v", lambda e, bk=banks[t], tt_=t, e2=ec: e.tensor_copy(
                    OA[:, (tt_ * 4 + e2) * 512:(tt_ * 4 + e2 + 1) * 512],
                    PB[bk][:]),
                    r=["PB%d" % banks[t]], w=[okeys(t * 4 + ec)])

        for t in range(TBT):
            yap = OA[:, t * 4 * 512:(t + 1) * 4 * 512]
            okl = [okeys(t * 4 + j) for j in range(4)]
            ln_stats(yap, okl, XT, ["xn_tok"], c=t + 4)
            p.emit("v", lambda e, a=yap, cc=t + 4: e.tensor_scalar(
                out=a, in0=a, scalar1=SMEAN[:, cc:cc + 1], scalar2=SRSTD[:, cc:cc + 1],
                op0=ALU.subtract, op1=ALU.mult),
                r=okl + ["smean%d" % (t + 4), "srstd%d" % (t + 4)], w=okl)
            p.emit("v", lambda e, a=yap, tt_=t: e.tensor_add(
                a, X[:, tt_ * D:(tt_ + 1) * D], a),
                r=okl + xkeys(t), w=okl)
            p.emit("g", lambda e, bb=b, tt_=t, a=yap: e.dma_start(
                out=out[(bb * TBT + tt_) * TT:(bb * TBT + tt_ + 1) * TT, :],
                in_=a),
                r=okl, w=["outw"], dma=True)
            if b + 1 < nblocks:
                xload(b + 1, t)

    p.wait_all("g", {s: p.cnt[s] for s in ["sgd%d" % i for i in range(6)] if p.cnt[s] > 0})

    with ExitStack() as ctx:
        sems = {}
        for s in p.semnames:
            sems[s] = ctx.enter_context(nc.semaphore(s))
        block = ctx.enter_context(nc.Block())
        eng_holder = {}

        @block.sync
        def _(sync):
            eng_holder["y"] = sync
            for waits, fn, sem, inc in p.q["y"]:
                for s2, c2 in waits.items():
                    sync.wait_ge(sems[s2], c2)
                if fn is not None:
                    inst = fn(sync)
                    if inc:
                        inst.then_inc(sems[sem], inc)

        @block.tensor
        def _(tensor):
            for waits, fn, sem, inc in p.q["t"]:
                for s2, c2 in waits.items():
                    tensor.wait_ge(sems[s2], c2)
                if fn is not None:
                    inst = fn(tensor)
                    if inc:
                        inst.then_inc(sems[sem], inc)

        @block.vector
        def _(vector):
            for waits, fn, sem, inc in p.q["v"]:
                for s2, c2 in waits.items():
                    vector.wait_ge(sems[s2], c2)
                if fn is not None:
                    inst = fn(vector)
                    if inc:
                        inst.then_inc(sems[sem], inc)

        @block.scalar
        def _(scalar):
            for waits, fn, sem, inc in p.q["s"]:
                for s2, c2 in waits.items():
                    scalar.wait_ge(sems[s2], c2)
                if fn is not None:
                    inst = fn(scalar)
                    if inc:
                        inst.then_inc(sems[sem], inc)

        @block.gpsimd
        def _(gpsimd):
            for waits, fn, sem, inc in p.q["g"]:
                for s2, c2 in waits.items():
                    gpsimd.wait_ge(sems[s2], c2)
                if fn is not None:
                    inst = fn(gpsimd)
                    if inc:
                        inst.then_inc(sems[sem], inc)

    psum_ctx.close()
    return nc


def make_in_maps(inputs, nblocks=NBLK):
    x = np.asarray(inputs["L0_attn_x"], dtype=np.float32)
    base = {
        "Wk": np.ascontiguousarray(inputs["L0_attn_Wk"], dtype=np.float32),
        "Wv": np.ascontiguousarray(inputs["L0_attn_Wv"], dtype=np.float32),
        "Wr": np.ascontiguousarray(inputs["L0_attn_Wr"], dtype=np.float32),
        "Wo": np.ascontiguousarray(inputs["L0_attn_Wo"], dtype=np.float32),
        "td": np.ascontiguousarray(inputs["L0_attn_time_decay"], dtype=np.float32),
        "tf": np.ascontiguousarray(inputs["L0_attn_time_first"], dtype=np.float32),
        "tmk": np.ascontiguousarray(np.reshape(inputs["L0_attn_time_mix_k"], [D]), dtype=np.float32),
        "tmv": np.ascontiguousarray(np.reshape(inputs["L0_attn_time_mix_v"], [D]), dtype=np.float32),
        "tmr": np.ascontiguousarray(np.reshape(inputs["L0_attn_time_mix_r"], [D]), dtype=np.float32),
        "wupq": np.ascontiguousarray(inputs["L0_ffn_w_up_q"], dtype=np.int32),
        "wups": np.ascontiguousarray(inputs["L0_ffn_w_up_scale"], dtype=np.float32),
        "bup": np.ascontiguousarray(inputs["L0_ffn_b_up"], dtype=np.float32),
        "upla": np.ascontiguousarray(inputs["L0_ffn_w_up_lora_a"], dtype=np.float32),
        "uplb": np.ascontiguousarray(inputs["L0_ffn_w_up_lora_b"], dtype=np.float32),
        "wdnq": np.ascontiguousarray(inputs["L0_ffn_w_down_q"], dtype=np.int32),
        "wdns": np.ascontiguousarray(inputs["L0_ffn_w_down_scale"], dtype=np.float32),
        "bdn": np.ascontiguousarray(inputs["L0_ffn_b_down"], dtype=np.float32),
        "dnla": np.ascontiguousarray(inputs["L0_ffn_w_down_lora_a"], dtype=np.float32),
        "dnlb": np.ascontiguousarray(inputs["L0_ffn_w_down_lora_b"], dtype=np.float32),
        "c_id": np.eye(TT, dtype=np.float32),
        "c_ones": np.ones((1, 512), dtype=np.float32),
    }
    in_maps = []
    for c in range(NCORES):
        bi, qi = divmod(c, 4)
        t0 = qi * TC
        m = dict(base)
        m["x"] = np.ascontiguousarray(x[bi, t0:t0 + TC, :], dtype=np.float32)
        if qi == 0:
            m["xprev"] = np.zeros((1, D), dtype=np.float32)
            m["xwarm"] = np.zeros((TT, D), dtype=np.float32)
            m["wvalid"] = np.zeros((TT, 1), dtype=np.float32)
        else:
            m["xprev"] = np.ascontiguousarray(x[bi, t0 - 1:t0, :], dtype=np.float32)
            m["xwarm"] = np.ascontiguousarray(x[bi, t0 - TT:t0, :], dtype=np.float32)
            m["wvalid"] = np.ones((TT, 1), dtype=np.float32)
        in_maps.append(m)
    return in_maps


def kernel(**inputs):
    nc = build_program()
    in_maps = make_in_maps(inputs)
    res = run_bass_kernel_spmd(nc, in_maps, core_ids=list(range(NCORES)))
    out = np.empty((2, 16384, D), dtype=np.float32)
    for c in range(NCORES):
        bi, qi = divmod(c, 4)
        t0 = qi * TC
        out[bi, t0:t0 + TC, :] = res.results[c]["out"]
    return out
